# version 05
# baseline (speedup 1.0000x reference)
import numpy as np
import ml_dtypes
import concourse.bass as bass
import concourse.mybir as mybir
from concourse.bass_utils import run_bass_kernel_spmd

F32 = mybir.dt.float32
BF16 = mybir.dt.bfloat16
I32 = mybir.dt.int32
AF = mybir.ActivationFunctionType
ALU = mybir.AluOpType
AX = mybir.AxisListType
BF = ml_dtypes.bfloat16

D, S, B, T = 1024, 4096, 4, 2048
NCH, CH, DT = 4, 512, 8
LN_EPS = 1e-5
L0_SHIFT = 14.0
L3_SHIFT = 4.0
POOL_COLS = 42240

_NONCE = [int(np.random.default_rng().integers(1, 2 ** 24)) * 64]
L0_HEAD_PERM = [0, 4, 1, 5, 2, 6, 3, 7, 8, 12, 9, 13, 10, 14, 11, 15]


def own_positions(p):
    pos = []
    for k in range(NCH):
        base = 1024 * k + 512 * p
        pos.extend(range(base, base + CH))
    return np.array(pos, dtype=np.int64)



_DVE_OPS = {"tensor_tensor", "tensor_scalar", "tensor_copy", "tensor_add",
            "tensor_sub", "tensor_mul", "tensor_max", "tensor_relu",
            "scalar_tensor_tensor", "tensor_reduce", "reduce_max", "reduce_sum",
            "reciprocal", "memset", "memzero", "select", "tensor_tensor_scan",
            "stream_shuffle", "transpose", "copy_predicated", "bn_stats",
            "bn_aggr", "tensor_scalar_add", "tensor_scalar_mul",
            "tensor_scalar_max", "tensor_scalar_min", "tensor_scalar_sub"}
_ACT_OPS = {"activation", "copy", "add", "mul", "sqrt", "square", "sign",
            "memzero"}


class ChainedEngine:

    def __init__(self, cc, eng, semname, ops):
        self._cc, self._eng, self._sem, self._ops = cc, eng, semname, ops

    def __getattr__(self, name):
        eng = self._eng
        f = getattr(eng, name)
        if name not in self._ops:
            return f
        cc, semname = self._cc, self._sem

        def wrapped(*a, **k):
            cc.wait(eng, cc.tok(semname))
            inst = f(*a, **k)
            cc.inc(inst, semname)
            return inst
        return wrapped


class _ExitStack:
    def __init__(self):
        self._cms = []

    def push(self, cm):
        self._cms.append(cm)

    def close(self):
        for cm in reversed(self._cms):
            cm.__exit__(None, None, None)
        self._cms = []


class CC:
    def __init__(self, nc):
        self.nc = nc
        self._sems = {}
        self._cms = []
        self.flag_word = 0
        self.spin_id = 0

    def sem(self, name):
        if name not in self._sems:
            cm = self.nc.semaphore(name)
            h = cm.__enter__()
            self._cms.append(cm)
            self._sems[name] = [h, 0]
        return self._sems[name]

    def inc(self, inst, name, n=1):
        s = self.sem(name)
        inst.then_inc(s[0], n)
        s[1] += n
        return (name, s[1])

    def tok(self, name):
        return (name, self.sem(name)[1])

    def wait(self, eng, token):
        if token is None:
            return
        name, cnt = token
        if name == "multi":
            for t_ in cnt:
                self.wait(eng, t_)
            return
        if cnt > 0:
            eng.wait_ge(self._sems[name][0], cnt)


def _dma_on(cc, eng, sem, out, in_, slow=False):
    cc.wait(eng, cc.tok(sem))
    return cc.inc(eng.dma_start(out=out, in_=in_,
                                allow_slow_non_contiguous=slow), sem, 16)


def dma_w(cc, out, in_):
    cc._wrr = (getattr(cc, "_wrr", 0) + 1) % 2
    return _dma_on(cc, cc.nc.sync, f"w{cc._wrr}", out, in_)


def dma_g(cc, out, in_, slow=False):
    cc._grr = (getattr(cc, "_grr", 0) + 1) % 4
    return _dma_on(cc, cc.nc.gpsimd, f"g{cc._grr}", out, in_, slow=slow)


def dma_s(cc, out, in_):
    return _dma_on(cc, cc.nc.scalar, "s0", out, in_)


def gbar(cc, eng):
    for j in range(4):
        cc.wait(eng, cc.tok(f"g{j}"))
        cc.wait(eng, cc.tok(f"gx{j}"))


def block_fence(cc):
    nc = cc.nc
    toks = ([cc.tok(nm) for nm in ("v", "a", "p")]
            + [cc.tok(f"g{j}") for j in range(4)]
            + [cc.tok(f"gx{j}") for j in range(4)]
            + [cc.tok("w0"), cc.tok("w1"), cc.tok("s0")])
    for eng in (nc.tensor, nc.vector, nc.scalar, nc.gpsimd, nc.sync):
        for tk in toks:
            cc.wait(eng, tk)


def percore(cc, fn):
    g = cc.nc.gpsimd
    for c in range(8):
        with g.If_eq(cc.me_reg, c):
            fn(c)


def ex_dma(cc, plan, slow=False):
    g = cc.nc.gpsimd
    s = cc.sem("gx")
    n = [None]

    sems = [cc.sem(f"gx{j}") for j in range(4)]
    n0 = [None]

    base = [cc.tok(f"gx{j}")[1] for j in range(4)]

    def body(c):
        pairs = plan(c)
        if n0[0] is None:
            n0[0] = len(pairs)
        assert len(pairs) == n0[0]
        uses = [0, 0, 0, 0]
        for j, (o, i) in enumerate(pairs):
            sj = j % 4
            if uses[sj] > 0:
                g.wait_ge(sems[sj][0], base[sj] + 16 * uses[sj])
            g.dma_start(out=o, in_=i,
                        allow_slow_non_contiguous=slow
                        ).then_inc(sems[sj][0], 16)
            uses[sj] += 1
    for j in range(min(4, 999)):
        cc.wait(g, cc.tok(f"gx{j}"))
    percore(cc, body)
    for j, (h, _) in enumerate(sems):
        pass
    cnt = [0, 0, 0, 0]
    for j in range(n0[0]):
        cnt[j % 4] += 16
    toks = []
    for j in range(4):
        if cnt[j]:
            sems[j][1] += cnt[j]
            toks.append((f"gx{j}", sems[j][1]))
    return ("multi", tuple(toks))


import os as _os
SIM_EXCHANGE = (_os.environ.get("BASS_SIM_FP64") == "1"
                or _os.environ.get("KERNEL_EXCHANGE") == "rdma")


def publish(cc, name, pairs, wait_tokens=()):
    g = cc.nc.gpsimd
    for t_ in wait_tokens:
        cc.wait(g, t_)
    srcs = [p if not isinstance(p, tuple) else p[0] for p in pairs]
    dt_ = srcs[0].dtype
    offs, o = [], 0
    for sp in srcs:
        offs.append(o)
        o += int(sp.shape[-1])
    if SIM_EXCHANGE:
        buf = cc.nc.dram_tensor(f"exb_{name}", [2, 128, o], dt_,
                                kind="Internal").ap()
        from concourse.libnrt import node_relative_to_routing_address
        from concourse.library_config import remote_dma as _RDMA_LIB
        from concourse.library_config import standard as _STD_LIB
        g.load_library(_RDMA_LIB)
        rs = cc.sem("rsem")
        ls = cc.sem("lsem")
        rdy = cc.sem("rdy")
        wtoks = ex_dma(cc, lambda c: [
            (buf[c % 2, :, offs[i]:offs[i] + int(srcs[i].shape[-1])], srcs[i])
            for i in range(len(srcs))])
        cc.wait(g, wtoks)

        def body0(c):
            peer = c ^ 1
            rid, pnc = node_relative_to_routing_address(peer, 8)
            g.remote_sem_update(rdy[0], ls[0], pid=pnc, routing_id=rid,
                                dma_engine_mask=0xFFFF)
        percore(cc, body0)
        g.trigger_dma()
        ls[1] += 16
        rdy[1] += 16
        cc.wait(g, ("rdy", rdy[1]))

        chunks = []
        for i, sp in enumerate(srcs):
            ncols = int(sp.shape[-1])
            for cstart in range(0, ncols, 2048):
                chunks.append((i, cstart, min(2048, ncols - cstart)))
        bounce = cc.ych
        for (i, cstart, n_) in chunks:

            def push(c, i=i, cstart=cstart, n_=n_):
                peer = c ^ 1
                rid, pnc = node_relative_to_routing_address(peer, 8)
                g.remote_dma(bounce[:, 0:n_], srcs[i][:, cstart:cstart + n_],
                             rs[0], ls[0], pid=pnc, routing_id=rid,
                             dma_engine_mask=0xFFFF)
            percore(cc, push)
            g.trigger_dma()
            rs[1] += 16
            ls[1] += 16
            cc.wait(g, ("rsem", rs[1]))
            ct = ex_dma(cc, lambda c, i=i, cstart=cstart, n_=n_: [
                (buf[(c % 2) ^ 1, :, offs[i] + cstart:offs[i] + cstart + n_],
                 bounce[:, 0:n_])])
            cc.wait(g, ct)

            def ack(c):
                peer = c ^ 1
                rid, pnc = node_relative_to_routing_address(peer, 8)
                g.remote_sem_update(rdy[0], ls[0], pid=pnc, routing_id=rid,
                                    dma_engine_mask=0xFFFF)
            percore(cc, ack)
            g.trigger_dma()
            ls[1] += 16
            rdy[1] += 16
            cc.wait(g, ("rdy", rdy[1]))
        g.load_library(_STD_LIB)
        return buf, offs, cc.tok("gx")
    buf = cc.nc.dram_tensor(f"exb_{name}", [2, 128, o], dt_,
                            kind="Internal", addr_space="Shared").ap()
    wtoks = ex_dma(cc, lambda c: [
        (buf[c % 2, :, offs[i]:offs[i] + int(srcs[i].shape[-1])], srcs[i])
        for i in range(len(srcs))])
    word = new_word(cc)
    post_flag(cc, word, [wtoks])
    spin_flag(cc, word)
    return buf, offs, cc.tok("gx")


def exchange(cc, name, pairs, wait_tokens=()):
    buf, offs, rtok = publish(cc, name, [p[1] for p in pairs], wait_tokens)
    g = cc.nc.gpsimd
    cc.wait(g, rtok)
    rdt = ex_dma(cc, lambda c: [
        (pairs[i][0],
         buf[(c % 2) ^ 1, :, offs[i]:offs[i] + int(pairs[i][1].shape[-1])])
        for i in range(len(pairs))])
    return rdt


def post_flag(cc, word, wait_tokens):
    g = cc.nc.gpsimd
    for t in wait_tokens:
        cc.wait(g, t)
    return ex_dma(cc, lambda c: [(cc.sh_flg[c, word:word + 1][None, :],
                                  cc.noncetile[0:1, 0:1])])


def spin_flag(cc, word):
    g = cc.nc.gpsimd
    sid = cc.spin_id
    cc.spin_id += 1
    assert sid < 32
    sp = cc.sem(f"sp{sid}")[0]
    ft = cc.flagtile[0:1, sid:sid + 1]

    def body(c):
        peer = c ^ 1
        with g.register(f"c{sid}_{c}") as creg, g.register(f"f{sid}_{c}") as fv:
            g.reg_mov(creg, 1)
            with g.While(creg):
                g.dma_start(out=ft, in_=cc.sh_flg[peer, word:word + 1][None, :]
                            ).then_inc(sp, 16)
                g.wait_ge(sp, 16)
                g.load(fv, ft)
                g.reg_sub(creg, fv, cc.nonce_reg)
    percore(cc, body)


def new_word(cc):
    w = cc.flag_word
    cc.flag_word += 1
    assert w < 63
    return w


def pv(cc, off, ncols):
    return cc.pool[:, off:off + ncols]


def chain(cc, eng, inst, name):
    tok = cc.inc(inst, name)
    cc.wait(eng, tok)
    return tok


def emit_ln_cols(cc, src_t, dst_t, mode, c0, src_c0=None, src_is_bf16=False,
                 wait_tokens=()):
    nc = cc.nc
    t, v, a = nc.tensor, cc.V, cc.A
    s0 = c0 if src_c0 is None else src_c0
    rows_guard = cc.tok("v")
    for tk in wait_tokens:
        cc.wait(v, tk)
        cc.wait(a, tk)
    mm_toks = {}
    for ti in range(DT):
        src = src_t(ti)[:, s0:s0 + 512]
        if src_is_bf16:
            xb = src
        else:
            xb = cc.xb2[:, (ti % 2) * 512:(ti % 2) * 512 + 512]
            if ti >= 2:
                cc.wait(v, mm_toks[ti - 2])
            v.tensor_copy(xb, src)
        sq = cc.sq2[:, (ti % 2) * 512:(ti % 2) * 512 + 512]
        if ti >= 2:
            cc.wait(a, mm_toks[ti - 2])
        a.activation(sq, src, AF.Square)
        ta = cc.tok("a")
        if not src_is_bf16:
            cc.wait(t, cc.tok("v"))
        cc.wait(t, ta)
        if ti == 0:
            cc.wait(t, rows_guard)
        t.matmul(cc.row0[0:1, :], cc.onesc[:, 0:1], xb,
                 start=(ti == 0), stop=(ti == DT - 1))
        i_ = t.matmul(cc.row1[0:1, :], cc.onesc[:, 0:1], sq,
                      start=(ti == 0), stop=(ti == DT - 1))
        mm_toks[ti] = cc.inc(i_, "p")
    tok_stats = mm_toks[DT - 1]
    mean = cc.rowf[0:1, 0:512]
    var = cc.rowf[0:1, 512:1024]
    rstd = cc.rowf[0:1, 1024:1536]
    cc.wait(v, tok_stats)
    v.tensor_scalar(mean, cc.row0[0:1, :], 1.0 / D, None, ALU.mult)
    v.tensor_tensor(var, mean, mean, ALU.mult)
    v.scalar_tensor_tensor(var, cc.row1[0:1, :], 1.0 / D, var,
                           ALU.mult, ALU.subtract)
    tv = cc.tok("v")
    cc.wait(a, tv)
    a.activation(rstd, var, AF.Sqrt, bias=cc.constt[0:1, 0:1])
    ta = cc.tok("a")
    cc.wait(v, ta)
    v.reciprocal(rstd, rstd)
    v.tensor_copy(cc.rowR[0:1, :], rstd)
    v.scalar_tensor_tensor(mean, mean, -1.0, rstd, ALU.mult, ALU.mult)
    v.tensor_copy(cc.rowM[0:1, :], mean)
    tv = cc.tok("v")
    cc.wait(t, tv)
    t.matmul(cc.big0[:, :], cc.ones1[0:1, :], cc.rowR[0:1, :], start=True, stop=True)
    i_ = t.matmul(cc.big1[:, :], cc.ones1[0:1, :], cc.rowM[0:1, :], start=True, stop=True)
    tok_bc = cc.inc(i_, "p")
    cc.wait(v, tok_bc)
    for tp_ in range(0, DT, 2):
        srcs = [src_t(tp_ + j)[:, s0:s0 + 512] for j in (0, 1)]
        dsts = [dst_t(tp_ + j)[:, c0:c0 + 512] for j in (0, 1)]
        tmps = [cc.lntmp[:, 0:512], cc.lntmp2[:, 0:512]]
        v.tensor_tensor(tmps[0], srcs[0], cc.big0[:, :], ALU.mult)
        v.tensor_tensor(tmps[1], srcs[1], cc.big0[:, :], ALU.mult)
        if mode == "set":
            v.tensor_tensor(dsts[0], tmps[0], cc.big1[:, :], ALU.add)
            v.tensor_tensor(dsts[1], tmps[1], cc.big1[:, :], ALU.add)
        else:
            v.tensor_tensor(tmps[0], tmps[0], cc.big1[:, :], ALU.add)
            v.tensor_tensor(tmps[1], tmps[1], cc.big1[:, :], ALU.add)
            v.tensor_tensor(dsts[0], dsts[0], tmps[0], ALU.add)
            v.tensor_tensor(dsts[1], dsts[1], tmps[1], ALU.add)
    return cc.tok("v")


def xres_t(cc, i):
    return cc.xres[:, i * T:(i + 1) * T]


def xn_t(cc, i):
    return cc.xn[:, i * T:(i + 1) * T]


def emit_ln_in(cc, wait_tokens=()):
    tk = None
    for c0 in range(0, T, 512):
        tk = emit_ln_cols(cc, lambda i: xres_t(cc, i), lambda i: xn_t(cc, i),
                          "set", c0, wait_tokens=wait_tokens)
    return tk


def emit_l0_attn(cc, W):
    import os
    PHASE = int(os.environ.get("L0A_PHASE", "9"))
    nc = cc.nc
    t, v, a, g = nc.tensor, cc.V, cc.A, nc.gpsimd

    tok_xn = emit_ln_in(cc)
    if PHASE <= 0:
        return tok_xn

    wq = pv(cc, 0, 8192)
    Ctab = pv(cc, 8192, T)
    Stab = pv(cc, 10240, T)
    P128 = pv(cc, 12288, 128)
    wk = pv(cc, 12416, 2048)
    wv = pv(cc, 14464, 2048)
    Ksb = pv(cc, 16512, 2 * T)
    Vsb = pv(cc, 20608, 16 * 260)
    Qf = pv(cc, 24768, 8 * T)

    def dma_wt(out, in_):
        return cc.inc(cc.nc.sync.dma_start(out=out, in_=in_), "wt", 16)
    dma_wt(Ctab, W["rope_c"][:, :])
    dma_wt(Stab, W["rope_s"][:, :])
    tok_tab3 = dma_wt(P128, W["p128"][:, :])
    tok_tab = [tok_tab3, tok_tab3, tok_tab3]
    tokw = []
    for dt_ in range(DT):
        tokw.append(dma_g(cc, wq[:, dt_ * 1024:(dt_ + 1) * 1024],
                          W["L0_attn_w_q"][dt_ * 128:(dt_ + 1) * 128, :]))
        tokw.append(dma_g(cc, wk[:, dt_ * 256:(dt_ + 1) * 256],
                          W["L0_attn_w_k"][dt_ * 128:(dt_ + 1) * 128, :]))
        tokw.append(dma_g(cc, wv[:, dt_ * 256:(dt_ + 1) * 256],
                          W["L0_attn_w_v"][dt_ * 128:(dt_ + 1) * 128, :]))
    tok_wlast = tokw[-1]

    v.memset(Vsb, 1.0)
    tok_vset = cc.tok("v")

    def rope_tile(dst, c0):
        cc.wait(a, cc.tok("p"))
        a.activation(cc.qswp[:, 0:512], cc.big2[:, :], AF.Copy)
        ta = cc.tok("a")
        cc.wait(t, ta)
        cc.wait(t, tok_tab[2])
        tp = cc.inc(t.matmul(cc.big3[:, :], P128, cc.qswp[:, 0:512],
                             start=True, stop=True), "p")
        cc.wait(v, tp)
        cc.wait(v, tok_tab[0])
        cc.wait(v, tok_tab[1])
        v.tensor_tensor(cc.lntmp[:, 0:512], cc.big2[:, :], Ctab[:, c0:c0 + 512],
                        ALU.mult)
        v.tensor_tensor(cc.lntmp2[:, 0:512], cc.big3[:, :],
                        Stab[:, c0:c0 + 512], ALU.mult)
        v.tensor_tensor(dst, cc.lntmp[:, 0:512], cc.lntmp2[:, 0:512], ALU.add)
        return cc.tok("v")

    tok_k = None
    for kt in range(2):
        for c0 in range(0, T, 512):
            for dt_ in range(DT):
                cc.wait(t, tok_wlast)
                cc.wait(t, tok_xn)
                if dt_ == 0:
                    cc.wait(t, tok_k)
                i_ = t.matmul(cc.big2[:, :],
                              wk[:, dt_ * 256 + kt * 128:dt_ * 256 + (kt + 1) * 128],
                              xn_t(cc, dt_)[:, c0:c0 + 512],
                              start=(dt_ == 0), stop=(dt_ == DT - 1))
            cc.inc(i_, "p")
            tok_k = rope_tile(Ksb[:, kt * T + c0:kt * T + c0 + 512], c0)

    tok_vv = None
    for tt in range(16):
        for dt_ in range(DT):
            cc.wait(t, tok_xn)
            cc.wait(t, tok_wlast)
            if dt_ == 0:
                cc.wait(t, tok_k)
                cc.wait(t, tok_vv)
            i_ = t.matmul(cc.big2[:, 0:256],
                          xn_t(cc, dt_)[:, tt * 128:(tt + 1) * 128],
                          wv[:, dt_ * 256:(dt_ + 1) * 256],
                          start=(dt_ == 0), stop=(dt_ == DT - 1))
        tp = cc.inc(i_, "p")
        cc.wait(a, tp)
        cc.wait(a, tok_vset)
        for kh in range(4):
            a.activation(Vsb[:, tt * 260 + kh * 65:tt * 260 + kh * 65 + 64],
                         cc.big2[:, kh * 64:(kh + 1) * 64], AF.Copy)
        tok_vv = cc.tok("a")

    tok_q = None
    for qt in range(DT):
        for c0 in range(0, T, 512):
            for dt_ in range(DT):
                cc.wait(t, tok_wlast)
                cc.wait(t, tok_xn)
                if dt_ == 0:
                    cc.wait(t, tok_q if tok_q is not None else tok_vv)
                i_ = t.matmul(cc.big2[:, :],
                              wq[:, dt_ * 1024 + qt * 128:dt_ * 1024 + (qt + 1) * 128],
                              xn_t(cc, dt_)[:, c0:c0 + 512],
                              start=(dt_ == 0), stop=(dt_ == DT - 1))
            cc.inc(i_, "p")
            tok_q = rope_tile(Qf[:, qt * T + c0:qt * T + c0 + 512], c0)

    if PHASE <= 1:
        return tok_q
    cc.wait(g, tok_q)
    Kpe = pv(cc, 0, 2 * T)
    Vpe = pv(cc, 4096, 16 * 260)
    woring = pv(cc, 8256, 2048)
    Pt0 = pv(cc, 10304, 512)
    Pt1 = pv(cc, 10816, 512)
    AVb = pv(cc, 11328, 8 * 512)
    ychunk = cc.ych
    tok_rd = exchange(cc, "l0kv", [
        (Kpe[:, 0:T], Ksb[:, 0:T]),
        (Kpe[:, T:2 * T], Ksb[:, T:2 * T]),
        (Vpe, Vsb),
    ], wait_tokens=(tok_k, tok_vv))

    if PHASE <= 2:
        return tok_rd
    ktile_src = [("own", k) for k in range(16)] + [("peer", k) for k in range(16)]
    prev_ln = None
    slab_used = [None, None]
    evac_toks = [None, None]
    for qc in range(NCH):
        c0 = qc * CH
        prev_evac = None
        for qs in range(16):
            qh = L0_HEAD_PERM[qs]
            kh = qh // 4
            qtile, qb = qs // 2, 64 * (qs % 2)
            ktt, kb = kh // 2, 64 * (kh % 2)
            assert qb == kb
            exp_toks, av_toks = {}, {}
            for ki, (srcid, kt) in enumerate(ktile_src):
                K_ = Ksb if srcid == "own" else Kpe
                V_ = Vsb if srcid == "own" else Vpe
                ps_s = cc.big0 if ki % 2 == 0 else cc.big1
                pt = Pt0 if ki % 2 == 0 else Pt1
                cc.wait(t, tok_q)
                if srcid == "peer":
                    cc.wait(t, tok_rd)
                if ki >= 2:
                    cc.wait(t, exp_toks[ki - 2])
                if ki == 0 and prev_evac is not None:
                    cc.wait(t, prev_evac)
                if ki == 0 and prev_ln is not None:
                    cc.wait(t, prev_ln)
                i_ = t.matmul(ps_s[:, :],
                              K_[kb:kb + 64, ktt * T + kt * 128:ktt * T + (kt + 1) * 128],
                              Qf[qb:qb + 64, qtile * T + c0:qtile * T + c0 + 512],
                              start=True, stop=True)
                ts_ = cc.inc(i_, "p")
                cc.wait(a, ts_)
                if ki >= 2:
                    cc.wait(a, av_toks[ki - 2])
                a.activation(pt, ps_s[:, :], AF.Exp, scale=0.125,
                             bias=cc.constt[:, 1:2])
                exp_toks[ki] = cc.tok("a")
                cc.wait(t, exp_toks[ki])
                i_ = t.matmul(cc.big2[0:65, :],
                              V_[:, kt * 260 + kh * 65:kt * 260 + kh * 65 + 65],
                              pt, start=(ki == 0), stop=(ki == 31))
                av_toks[ki] = cc.inc(i_, "p")
            cc.wait(v, av_toks[31])
            rec = cc.rowf[0:1, 0:512]
            v.reciprocal(rec, cc.big2[64:65, :])
            v.tensor_copy(cc.rowR[0:1, :], rec)
            tv = cc.tok("v")
            cc.wait(t, tv)
            tb = cc.inc(t.matmul(cc.big3[:, :], cc.ones1[0:1, :],
                                 cc.rowR[0:1, :], start=True, stop=True), "p")
            cc.wait(a, av_toks[31])
            cc.wait(a, prev_evac)
            a.activation(cc.qswp[0:64, 0:512], cc.big2[0:64, :], AF.Copy)
            ta2 = cc.tok("a")
            cc.wait(v, tb)
            cc.wait(v, ta2)
            v.tensor_tensor(AVb[qb:qb + 64, qtile * 512:(qtile + 1) * 512],
                            cc.qswp[0:64, 0:512], cc.big3[0:64, :], ALU.mult)
            prev_evac = cc.tok("v")
        tok_av = cc.tok("v")
        tok_y = None
        for ot in range(DT):
            slab = woring[:, (ot % 2) * 1024:(ot % 2) * 1024 + 1024]
            cc.wait(g, slab_used[ot % 2])
            rsem = "r0" if ot % 2 == 0 else "r1"
            tok_l = None
            for dt_ in range(DT):
                tok_l = cc.inc(g.dma_start(
                    out=slab[:, dt_ * 128:(dt_ + 1) * 128],
                    in_=W["L0_attn_w_o"][dt_ * 128:(dt_ + 1) * 128,
                                         ot * 128:(ot + 1) * 128]), rsem, 16)
            ps_o = cc.big4 if ot % 2 == 0 else cc.big5
            for dvt in range(DT):
                cc.wait(t, tok_av)
                cc.wait(t, tok_l)
                if dvt == 0:
                    cc.wait(t, evac_toks[ot % 2])
                if prev_ln is not None:
                    cc.wait(t, prev_ln)
                i_ = t.matmul(ps_o[:, :], slab[:, dvt * 128:(dvt + 1) * 128],
                              AVb[:, dvt * 512:(dvt + 1) * 512],
                              start=(dvt == 0), stop=(dvt == DT - 1))
            tp = cc.inc(i_, "p")
            slab_used[ot % 2] = tp
            cc.wait(a, tp)
            cc.wait(a, prev_ln)
            a.activation(ychunk[:, ot * 512:(ot + 1) * 512], ps_o[:, :], AF.Copy)
            tok_y = cc.tok("a")
            evac_toks[ot % 2] = tok_y
        prev_ln = emit_ln_cols(cc, lambda i: ychunk[:, i * 512:(i + 1) * 512],
                               lambda i: xres_t(cc, i), "add", c0, src_c0=0,
                               src_is_bf16=True, wait_tokens=(tok_y,))
    return prev_ln




def emit_l1_ffn(cc, W):
    nc = cc.nc
    t, v, a, g = nc.tensor, cc.V, cc.A, nc.gpsimd
    tok_xn = emit_ln_in(cc)
    HT = 24
    wgT = pv(cc, 0, 24576)
    wuring = pv(cc, 24576, 2048)
    wdring = pv(cc, 26624, 2048)
    h = pv(cc, 28672, HT * 512)
    tokw = None
    for dt_ in range(DT):
        tokw = dma_g(cc, wgT[:, dt_ * 3072:(dt_ + 1) * 3072],
                     W["l1_wgT"][dt_ * 128:(dt_ + 1) * 128, :])
    tok_wg = tokw

    prev_ln = None
    slabu = [None, None]
    slabd = [None, None]
    psev = [None, None]
    dvev = [None, None]
    yevac = [None, None]
    for qc in range(NCH):
        c0 = qc * CH
        for m in range(HT):
            par = m % 2
            psg = cc.big0 if par == 0 else cc.big1
            psu = cc.big2 if par == 0 else cc.big3
            tA = cc.lntmp[:, 0:512] if par == 0 else cc.xb2[:, 0:512]
            tB = cc.lntmp2[:, 0:512] if par == 0 else cc.sq2[:, 0:512]
            slab = wuring[:, par * 1024:par * 1024 + 1024]
            cc.wait(g, slabu[par])
            tok_u = dma_g(cc, slab.rearrange("p (k m) -> p k m", k=DT),
                          W["l1_wuT"][:, m * 128:(m + 1) * 128]
                          .rearrange("(k p) m -> p k m", p=128))
            i_ = None
            for dt_ in range(DT):
                cc.wait(t, tok_wg)
                cc.wait(t, tok_xn)
                if dt_ == 0:
                    cc.wait(t, psev[par])
                    if prev_ln is not None:
                        cc.wait(t, prev_ln)
                t.matmul(psg[:, :],
                         wgT[:, dt_ * 3072 + m * 128:dt_ * 3072 + (m + 1) * 128],
                         xn_t(cc, dt_)[:, c0:c0 + 512],
                         start=(dt_ == 0), stop=(dt_ == DT - 1))
                cc.wait(t, tok_u)
                i_ = t.matmul(psu[:, :], slab[:, dt_ * 128:(dt_ + 1) * 128],
                              xn_t(cc, dt_)[:, c0:c0 + 512],
                              start=(dt_ == 0), stop=(dt_ == DT - 1))
            tokp = cc.inc(i_, "p")
            slabu[par] = tokp
            cc.wait(a, tokp)
            cc.wait(a, dvev[par])
            a.activation(tA, psg[:, :], AF.Silu, scale=1.0 / 127.0)
            a.activation(tB, psu[:, :], AF.Copy, scale=1.0 / 127.0)
            psev[par] = cc.tok("a")
            cc.wait(v, psev[par])
            v.tensor_tensor(h[:, m * 512:(m + 1) * 512], tA, tB, ALU.mult)
            dvev[par] = cc.tok("v")
        tok_h = cc.tok("v")
        tok_y = None
        for ot in range(DT):
            ps_o = cc.big4 if ot % 2 == 0 else cc.big5
            slab = wdring[:, (ot % 2) * 1024:(ot % 2) * 1024 + 1024]
            i_ = None
            for sub in range(3):
                cc.wait(g, slabd[ot % 2])
                tok_d = dma_g(cc, slab.rearrange("p (k m) -> p k m", k=8),
                              W["l1_wdT"][sub * 1024:(sub + 1) * 1024,
                                          ot * 128:(ot + 1) * 128]
                              .rearrange("(k p) m -> p k m", p=128))
                for kk in range(8):
                    k = sub * 8 + kk
                    cc.wait(t, tok_d)
                    cc.wait(t, tok_h)
                    if k == 0:
                        cc.wait(t, yevac[ot % 2])
                    i_ = t.matmul(ps_o[:, :], slab[:, kk * 128:(kk + 1) * 128],
                                  h[:, k * 512:(k + 1) * 512],
                                  start=(k == 0), stop=(k == HT - 1))
                slabd[ot % 2] = cc.inc(i_, "p")
            tokp = slabd[ot % 2]
            cc.wait(a, tokp)
            if prev_ln is not None:
                cc.wait(a, prev_ln)
            a.activation(cc.ych[:, ot * 512:(ot + 1) * 512], ps_o[:, :],
                         AF.Copy, scale=1.0 / 127.0)
            tok_y = cc.tok("a")
            yevac[ot % 2] = tok_y
        prev_ln = emit_ln_cols(cc, lambda i: cc.ych[:, i * 512:(i + 1) * 512],
                               lambda i: xres_t(cc, i), "add", c0, src_c0=0,
                               src_is_bf16=True, wait_tokens=(tok_y,))
    return prev_ln



def emit_l3_ffn(cc, W):
    nc = cc.nc
    t, v, a, g = nc.tensor, cc.V, cc.A, nc.gpsimd
    tok_xn = emit_ln_in(cc)
    WcT = pv(cc, 0, 8192)
    stag = pv(cc, 8192, 2048)
    biascol = pv(cc, 10240, 16)
    accA = cc.lntmp[:, 0:512]
    accB = cc.lntmp2[:, 0:512]
    tok_b = dma_g(cc, cc.constt[:, 8:16], W["l3_bias"][:, :])
    for part in range(2):
        pre = "l3_t1r" if part == 0 else "l3_t1i"
        pre2 = "l3_t2r" if part == 0 else "l3_t2i"
        for rt in range(4):
            acc = accA
            for c4 in range(4):
                s1 = stag[:, 0:512]
                s2 = stag[:, 512:1024]
                tokl = None
                cc.wait(g, cc.tok("v"))
                dma_g(cc, s1, W[f"{pre}_{c4}"][rt * 128:(rt + 1) * 128, :])
                tokl = dma_g(cc, s2, W[f"{pre2}_{c4}"][rt * 128:(rt + 1) * 128, :])
                cc.wait(v, tokl)
                if c4 == 0:
                    v.tensor_tensor(acc, s1, s2, ALU.mult)
                else:
                    v.tensor_tensor(accB, s1, s2, ALU.mult)
                    v.tensor_tensor(acc, acc, accB, ALU.add)
            if part == 0:
                v.tensor_copy(WcT[:, rt * 1024:rt * 1024 + 512], acc)
                v.tensor_copy(WcT[:, (rt + 4) * 1024 + 512:(rt + 5) * 1024], acc)
            else:
                v.tensor_copy(WcT[:, rt * 1024 + 512:(rt + 1) * 1024], acc)
                v.tensor_scalar(WcT[:, (rt + 4) * 1024:(rt + 4) * 1024 + 512],
                                acc, -1.0, None, ALU.mult)
    tok_w = cc.tok("v")

    prev_ln = None
    yevac = [None, None]
    for qc in range(NCH):
        c0 = qc * CH
        tok_y = None
        for ot in range(DT):
            ps_o = cc.big4 if ot % 2 == 0 else cc.big5
            i_ = None
            for dt_ in range(DT):
                cc.wait(t, tok_w)
                cc.wait(t, tok_xn)
                if dt_ == 0:
                    cc.wait(t, yevac[ot % 2])
                    if prev_ln is not None:
                        cc.wait(t, prev_ln)
                i_ = t.matmul(ps_o[:, :],
                              WcT[:, dt_ * 1024 + ot * 128:dt_ * 1024 + (ot + 1) * 128],
                              xn_t(cc, dt_)[:, c0:c0 + 512],
                              start=(dt_ == 0), stop=(dt_ == DT - 1))
            tokp = cc.inc(i_, "p")
            cc.wait(v, tokp)
            cc.wait(v, tok_b)
            if prev_ln is not None:
                cc.wait(v, prev_ln)
            v.tensor_scalar(cc.ych[:, ot * 512:(ot + 1) * 512], ps_o[:, :],
                            cc.constt[:, 8 + ot:9 + ot], None, ALU.add)
            tok_y = cc.tok("v")
            yevac[ot % 2] = tok_y
        prev_ln = emit_ln_cols(cc, lambda i: cc.ych[:, i * 512:(i + 1) * 512],
                               lambda i: xres_t(cc, i), "add", c0, src_c0=0,
                               src_is_bf16=True, wait_tokens=(tok_y,))
    return prev_ln


def emit_l0_ffn(cc, W):
    nc = cc.nc
    t, v, a, g = nc.tensor, cc.V, cc.A, nc.gpsimd
    tok_xn = emit_ln_in(cc)
    HT = 32
    ha = pv(cc, 0, HT * 512)
    hb = pv(cc, 16384, HT * 512)
    stream = pv(cc, 32768, 8192)
    tok_s = dma_g(cc, cc.scl[:, :], W["l0_scales"][:, :])

    prev_ln = None
    psev = [None, None]
    dvev = [None, None]
    yevac = [None, None]
    for qc in range(NCH):
        c0 = qc * CH
        ring_tok = [cc.tok("p"), cc.tok("p")]
        for m in range(HT):
            par = m % 2
            psg = cc.big0 if par == 0 else cc.big1
            psu = cc.big2 if par == 0 else cc.big3
            tA = cc.lntmp[:, 0:512] if par == 0 else cc.xb2[:, 0:512]
            tB = cc.lntmp2[:, 0:512] if par == 0 else cc.sq2[:, 0:512]
            s1m = cc.scl[:, m:m + 1]
            s2m = cc.scl[:, 32 + m:33 + m]
            w1s = stream[:, par * 2048:par * 2048 + 1024]
            w2s = stream[:, par * 2048 + 1024:par * 2048 + 2048]
            cc.wait(g, ring_tok[par])
            dma_g(cc, w1s.rearrange("p (k m) -> p k m", k=DT),
                  W["l0_w1T"][:, m * 128:(m + 1) * 128]
                  .rearrange("(k p) m -> p k m", p=128))
            tok_l = dma_g(cc, w2s.rearrange("p (k m) -> p k m", k=DT),
                          W["l0_w2T"][:, m * 128:(m + 1) * 128]
                          .rearrange("(k p) m -> p k m", p=128))
            i_ = None
            for dt_ in range(DT):
                cc.wait(t, tok_l)
                cc.wait(t, tok_xn)
                if dt_ == 0:
                    cc.wait(t, psev[par])
                    if prev_ln is not None:
                        cc.wait(t, prev_ln)
                t.matmul(psg[:, :], w1s[:, dt_ * 128:(dt_ + 1) * 128],
                         xn_t(cc, dt_)[:, c0:c0 + 512],
                         start=(dt_ == 0), stop=(dt_ == DT - 1))
                i_ = t.matmul(psu[:, :], w2s[:, dt_ * 128:(dt_ + 1) * 128],
                              xn_t(cc, dt_)[:, c0:c0 + 512],
                              start=(dt_ == 0), stop=(dt_ == DT - 1))
            tokp = cc.inc(i_, "p")
            ring_tok[par] = tokp
            cc.wait(a, tokp)
            cc.wait(a, tok_s)
            cc.wait(a, dvev[par])
            a.activation(tA, psg[:, :], AF.Silu, scale=s1m)
            a.activation(tB, psu[:, :], AF.Copy, scale=s2m)
            psev[par] = cc.tok("a")
            cc.wait(v, psev[par])
            v.tensor_tensor(ha[:, m * 512:(m + 1) * 512], tA, tB, ALU.mult)
            dvev[par] = cc.tok("v")
        for stage, (wname, scoff, src, dst) in enumerate(
                [("l0_h1T", 64, ha, hb), ("l0_h2T", 96, hb, ha)]):
            tok_src = cc.tok("v") if stage == 0 else cc.tok("a")
            ring_tok = [cc.tok("p"), cc.tok("p")]
            for m in range(HT):
                par = m % 2
                ps = cc.big0 if par == 0 else cc.big1
                slab = stream[:, par * 4096:par * 4096 + 4096]
                cc.wait(g, ring_tok[par])
                tok_l = dma_g(cc, slab.rearrange("p (k m) -> p k m", k=HT),
                              W[wname][:, m * 128:(m + 1) * 128]
                              .rearrange("(k p) m -> p k m", p=128))
                i_ = None
                for k in range(HT):
                    cc.wait(t, tok_l)
                    cc.wait(t, tok_src)
                    if k == 0:
                        cc.wait(t, psev[par])
                    i_ = t.matmul(ps[:, :], slab[:, k * 128:(k + 1) * 128],
                                  src[:, k * 512:(k + 1) * 512],
                                  start=(k == 0), stop=(k == HT - 1))
                tokp = cc.inc(i_, "p")
                ring_tok[par] = tokp
                cc.wait(a, tokp)
                cc.wait(a, cc.tok("v"))
                a.activation(dst[:, m * 512:(m + 1) * 512], ps[:, :], AF.Gelu,
                             scale=cc.scl[:, scoff + m:scoff + m + 1])
                psev[par] = cc.tok("a")
        tok_hc = cc.tok("a")
        ring_tok = [cc.tok("p"), cc.tok("p")]
        tok_y = None
        for ot in range(DT):
            par = ot % 2
            ps_o = cc.big4 if par == 0 else cc.big5
            slab = stream[:, par * 4096:par * 4096 + 4096]
            cc.wait(g, ring_tok[par])
            tok_l = dma_g(cc, slab.rearrange("p (k m) -> p k m", k=HT),
                          W["l0_w3T"][:, ot * 128:(ot + 1) * 128]
                          .rearrange("(k p) m -> p k m", p=128))
            i_ = None
            for k in range(HT):
                cc.wait(t, tok_l)
                cc.wait(t, tok_hc)
                if k == 0:
                    cc.wait(t, yevac[par])
                i_ = t.matmul(ps_o[:, :], slab[:, k * 128:(k + 1) * 128],
                              ha[:, k * 512:(k + 1) * 512],
                              start=(k == 0), stop=(k == HT - 1))
            tokp = cc.inc(i_, "p")
            ring_tok[par] = tokp
            cc.wait(a, tokp)
            if prev_ln is not None:
                cc.wait(a, prev_ln)
            a.activation(cc.ych[:, ot * 512:(ot + 1) * 512], ps_o[:, :],
                         AF.Copy, scale=cc.scl[:, 128 + ot:129 + ot])
            tok_y = cc.tok("a")
            yevac[par] = tok_y
        prev_ln = emit_ln_cols(cc, lambda i: cc.ych[:, i * 512:(i + 1) * 512],
                               lambda i: xres_t(cc, i), "add", c0, src_c0=0,
                               src_is_bf16=True, wait_tokens=(tok_y,))
    return prev_ln


def emit_l3_attn(cc, W):
    nc = cc.nc
    t, v, a, g = nc.tensor, cc.V, cc.A, nc.gpsimd
    p_par = None
    tok_xn = emit_ln_in(cc)
    Ksb = pv(cc, 0, 8 * T)
    Vsb = pv(cc, 16384, 16 * 1040)
    wq2 = pv(cc, 40960, 64)
    wk2 = pv(cc, 41024, 64)
    wv2 = pv(cc, 41088, 64)
    m3c = pv(cc, 33024, 1920)
    tok_w = [dma_g(cc, wq2, W["l3_wq2"][:, :]),
             dma_g(cc, wk2, W["l3_wk2"][:, :]),
             dma_g(cc, wv2, W["l3_wv2"][:, :]),
             dma_w(cc, m3c, W["l3_m3"][:, :])]
    v.memset(Vsb, 1.0)
    tok_vset = cc.tok("v")

    tok_k = None
    for h in range(16):
        qb = 64 * (h % 2)
        ht = h // 2
        for c0 in range(0, T, 512):
            ps = cc.big0 if (c0 // 512) % 2 == 0 else cc.big1
            cc.wait(t, tok_w[1])
            cc.wait(t, tok_xn)
            cc.wait(t, cc.tok("a"))
            i_ = t.matmul(ps[0:64, :], wk2[qb:qb + 64, :],
                          xn_t(cc, ht)[qb:qb + 64, c0:c0 + 512],
                          start=True, stop=True)
            tp = cc.inc(i_, "p")
            cc.wait(a, tp)
            a.activation(Ksb[:, ht * T + c0:ht * T + c0 + 512][qb:qb + 64, :],
                         ps[0:64, :], AF.Copy)
            tok_k = cc.tok("a")
    tok_vv = None
    for h in range(16):
        qb = 64 * (h % 2)
        ht = h // 2
        for tt in range(16):
            ps = cc.big2 if tt % 2 == 0 else cc.big3
            cc.wait(t, tok_w[2])
            cc.wait(t, tok_xn)
            cc.wait(t, cc.tok("a"))
            i_ = t.matmul(ps[:, 0:64],
                          xn_t(cc, ht)[qb:qb + 64, tt * 128:(tt + 1) * 128],
                          wv2[qb:qb + 64, :], start=True, stop=True)
            tp = cc.inc(i_, "p")
            cc.wait(a, tp)
            cc.wait(a, tok_vset)
            a.activation(Vsb[:, tt * 1040 + h * 65:tt * 1040 + h * 65 + 64],
                         ps[:, 0:64], AF.Copy)
            tok_vv = cc.tok("a")

    srcs = [Ksb[:, i * T:(i + 1) * T] for i in range(8)] + \
           [Vsb[:, i * 1040:(i + 1) * 1040] for i in range(16)]
    buf, offs, rtok = publish(cc, "l3kv", srcs, wait_tokens=(tok_k, tok_vv))
    koff = offs[:8]
    voff = offs[8:]

    Qf = pv(cc, 0, 8 * T)
    cc.wait(a, rtok)
    tok_q = None
    for h in range(16):
        qb = 64 * (h % 2)
        ht = h // 2
        for c0 in range(0, T, 512):
            ps = cc.big0 if (c0 // 512) % 2 == 0 else cc.big1
            cc.wait(t, tok_w[0])
            cc.wait(t, cc.tok("a"))
            i_ = t.matmul(ps[0:64, :], wq2[qb:qb + 64, :],
                          xn_t(cc, ht)[qb:qb + 64, c0:c0 + 512],
                          start=True, stop=True)
            tp = cc.inc(i_, "p")
            cc.wait(a, tp)
            a.activation(Qf[:, ht * T + c0:ht * T + c0 + 512][qb:qb + 64, :],
                         ps[0:64, :], AF.Copy)
            tok_q = cc.tok("a")

    kring = pv(cc, 16384, 1024)
    vring = pv(cc, 17408, 520)
    AVb = pv(cc, 17928, 4096)
    woring = pv(cc, 22024, 2048)
    Pt0 = pv(cc, 24072, 512)
    Pt1 = pv(cc, 24584, 512)

    prev_ln = None
    slab_used = [None, None]
    yevac = [None, None]
    kused = [None, None]
    for qc in range(NCH):
        c0 = qc * CH
        prev_evac = None
        for qh in range(16):
            qb, qtile = 64 * (qh % 2), qh // 2
            nb = 2 * qc + 2
            exp_toks, av_toks = {}, {}
            ki = 0
            nkt = nb * 4
            for b in range(nb):
                kr = kring[:, (b % 2) * 512:(b % 2) * 512 + 512]
                vr = vring[:, (b % 2) * 260:(b % 2) * 260 + 260]
                cc.wait(g, kused[b % 2])
                def plan(c, b=b, qtile=qtile, qh=qh, kr=kr, vr=vr):
                    par = c % 2
                    w_ = b % 2
                    kk = b // 2
                    outs = [(kr, buf[w_, :, koff[qtile] + kk * 512:
                                    koff[qtile] + kk * 512 + 512])]
                    for j in range(4):
                        tt = kk * 4 + j
                        outs.append((vr[:, j * 65:(j + 1) * 65],
                                     buf[w_, :, voff[tt] + qh * 65:
                                         voff[tt] + qh * 65 + 65]))
                    return outs
                tok_kv = ex_dma(cc, plan)
                for j in range(4):
                    ps_s = cc.big0 if ki % 2 == 0 else cc.big1
                    pt = Pt0 if ki % 2 == 0 else Pt1
                    cc.wait(t, tok_q)
                    cc.wait(t, tok_kv)
                    if ki >= 2:
                        cc.wait(t, exp_toks[ki - 2])
                    if ki == 0:
                        if prev_evac is not None:
                            cc.wait(t, prev_evac)
                        if prev_ln is not None:
                            cc.wait(t, prev_ln)
                    i_ = t.matmul(ps_s[:, :], kr[qb:qb + 64, j * 128:(j + 1) * 128],
                                  Qf[qb:qb + 64, qtile * T + c0:qtile * T + c0 + 512],
                                  start=True, stop=True)
                    ts_ = cc.inc(i_, "p")
                    if b >= 2 * qc:
                        dd = (b - 2 * qc) * 512 + j * 128
                        cc.wait(v, ts_)
                        cc.wait(v, tok_w[3])
                        v.tensor_tensor(ps_s[:, :], ps_s[:, :],
                                        m3c[:, 1408 - dd:1408 - dd + 512],
                                        ALU.add)
                        cc.wait(a, cc.tok("v"))
                    cc.wait(a, ts_)
                    if ki >= 2:
                        cc.wait(a, av_toks[ki - 2])
                    a.activation(pt, ps_s[:, :], AF.Exp, scale=0.125,
                                 bias=cc.constt[:, 2:3])
                    exp_toks[ki] = cc.tok("a")
                    cc.wait(t, exp_toks[ki])
                    i_ = t.matmul(cc.big2[0:65, :], vr[:, j * 65:(j + 1) * 65],
                                  pt, start=(ki == 0), stop=(ki == nkt - 1))
                    av_toks[ki] = cc.inc(i_, "p")
                    ki += 1
                kused[b % 2] = av_toks[ki - 1]
            cc.wait(v, av_toks[nkt - 1])
            rec = cc.rowf[0:1, 0:512]
            v.reciprocal(rec, cc.big2[64:65, :])
            v.tensor_copy(cc.rowR[0:1, :], rec)
            tv = cc.tok("v")
            cc.wait(t, tv)
            tb = cc.inc(t.matmul(cc.big3[:, :], cc.ones1[0:1, :],
                                 cc.rowR[0:1, :], start=True, stop=True), "p")
            cc.wait(a, av_toks[nkt - 1])
            cc.wait(a, prev_evac)
            a.activation(cc.qswp[0:64, 0:512], cc.big2[0:64, :], AF.Copy)
            ta2 = cc.tok("a")
            cc.wait(v, tb)
            cc.wait(v, ta2)
            v.tensor_tensor(AVb[qb:qb + 64, qtile * 512:(qtile + 1) * 512],
                            cc.qswp[0:64, 0:512], cc.big3[0:64, :], ALU.mult)
            prev_evac = cc.tok("v")
        tok_av = cc.tok("v")
        tok_y = None
        for ot in range(DT):
            slab = woring[:, (ot % 2) * 1024:(ot % 2) * 1024 + 1024]
            cc.wait(g, slab_used[ot % 2])
            tok_l = None
            for dt_ in range(DT):
                tok_l = dma_g(cc, slab[:, dt_ * 128:(dt_ + 1) * 128],
                              W["l3_woT"][dt_ * 128:(dt_ + 1) * 128,
                                          ot * 128:(ot + 1) * 128])
            ps_o = cc.big4 if ot % 2 == 0 else cc.big5
            i_ = None
            for dvt in range(DT):
                cc.wait(t, tok_av)
                cc.wait(t, tok_l)
                if dvt == 0:
                    cc.wait(t, yevac[ot % 2])
                if prev_ln is not None:
                    cc.wait(t, prev_ln)
                i_ = t.matmul(ps_o[:, :], slab[:, dvt * 128:(dvt + 1) * 128],
                              AVb[:, dvt * 512:(dvt + 1) * 512],
                              start=(dvt == 0), stop=(dvt == DT - 1))
            tp = cc.inc(i_, "p")
            slab_used[ot % 2] = tp
            cc.wait(a, tp)
            cc.wait(a, prev_ln)
            a.activation(cc.ych[:, ot * 512:(ot + 1) * 512], ps_o[:, :], AF.Copy)
            tok_y = cc.tok("a")
            yevac[ot % 2] = tok_y
        prev_ln = emit_ln_cols(cc, lambda i: cc.ych[:, i * 512:(i + 1) * 512],
                               lambda i: xres_t(cc, i), "add", c0, src_c0=0,
                               src_is_bf16=True, wait_tokens=(tok_y,))
    return prev_ln


def emit_l1_attn(cc, W):
    nc = cc.nc
    t, v, a, g = nc.tensor, cc.V, cc.A, nc.gpsimd
    tok_xn = emit_ln_in(cc)
    GAM = [1.0 - 2.0 ** (-5 - (h // 2)) for h in range(16)]

    Ksb = pv(cc, 0, 8 * T)
    Vsb = pv(cc, 16384, 16 * 1024)
    ring = pv(cc, 32768, 2048)
    GQ = pv(cc, 34816, 4096)
    M01 = pv(cc, 38912, 1920)
    wring = pv(cc, 32768, 2048)
    tok_tab = [dma_w(cc, GQ, W["l1_gq"][:, :]),
               dma_w(cc, M01, W["l1_m01"][:, :]),
               dma_g(cc, cc.l1s[:, 0:16], W["l1_gk"][:, :]),
               dma_g(cc, cc.l1s[:, 16:24], W["l1_gnw"][:, :]),
               dma_g(cc, cc.l1s[:, 24:32], W["l1_gnb"][:, :]),
               dma_g(cc, cc.l1s[0:16, 32:40], W["l1_msk16"][:, :])]

    slab_tok = [None, None]
    tok_k = None
    for m in range(DT):
        par = m % 2
        slab = ring[:, par * 1024:par * 1024 + 1024]
        cc.wait(g, slab_tok[par])
        tok_l = dma_g(cc, slab.rearrange("p (k m) -> p k m", k=DT),
                      W["l1_wkcat"][:, m * 128:(m + 1) * 128]
                      .rearrange("(k p) m -> p k m", p=128))
        for c0 in range(0, T, 512):
            ps = cc.big0 if (c0 // 512) % 2 == 0 else cc.big1
            i_ = None
            for dt_ in range(DT):
                cc.wait(t, tok_l)
                cc.wait(t, tok_xn)
                if dt_ == 0:
                    cc.wait(t, cc.tok("a"))
                i_ = t.matmul(ps[:, :], slab[:, dt_ * 128:(dt_ + 1) * 128],
                              xn_t(cc, dt_)[:, c0:c0 + 512],
                              start=(dt_ == 0), stop=(dt_ == DT - 1))
            tp = cc.inc(i_, "p")
            slab_tok[par] = tp
            cc.wait(a, tp)
            a.activation(Ksb[:, m * T + c0:m * T + c0 + 512], ps[:, :], AF.Copy)
            tok_k = cc.tok("a")

    tok_vv = None
    vprev = None
    for q4 in range(4):
        cc.wait(g, slab_tok[0])
        cc.wait(g, slab_tok[1])
        cc.wait(g, vprev)
        tok_l = dma_g(cc, ring.rearrange("p (k m) -> p k m", k=DT),
                      W["l1_wvcat"][:, q4 * 256:(q4 + 1) * 256]
                      .rearrange("(k p) m -> p k m", p=128))
        for tt in range(16):
            ps = cc.big2 if tt % 2 == 0 else cc.big3
            i_ = None
            for dt_ in range(DT):
                cc.wait(t, tok_l)
                cc.wait(t, tok_xn)
                if dt_ == 0:
                    cc.wait(t, cc.tok("a"))
                i_ = t.matmul(ps[:, 0:256],
                              xn_t(cc, dt_)[:, tt * 128:(tt + 1) * 128],
                              ring[:, dt_ * 256:(dt_ + 1) * 256],
                              start=(dt_ == 0), stop=(dt_ == DT - 1))
            tp = cc.inc(i_, "p")
            vprev = tp
            cc.wait(a, tp)
            a.activation(Vsb[:, tt * 1024 + q4 * 256:tt * 1024 + (q4 + 1) * 256],
                         ps[:, 0:256], AF.Copy)
            tok_vv = cc.tok("a")
        slab_tok = [cc.tok("p"), cc.tok("p")]

    srcs = [Ksb[:, i * T:(i + 1) * T] for i in range(8)] + \
           [Vsb[:, i * 1024:(i + 1) * 1024] for i in range(16)]
    buf, offs, rtok = publish(cc, "l1kv", srcs, wait_tokens=(tok_k, tok_vv))
    koff = offs[:8]
    voff = offs[8:]

    avb = pv(cc, 0, 8 * T)
    Qc = pv(cc, 16384, 4096)
    kr = pv(cc, 20480, 256)
    vr = pv(cc, 20736, 128)
    qring = pv(cc, 22016, 2048)
    cc.wait(a, rtok)
    cc.wait(v, rtok)

    kused = [None, None]
    avev = None
    qslab_tok = [None, None]
    for qc in range(NCH):
        c0 = qc * CH
        tok_q = None
        for m in range(DT):
            par = m % 2
            slab = qring[:, par * 1024:par * 1024 + 1024]
            cc.wait(g, qslab_tok[par])
            tok_l = dma_g(cc, slab.rearrange("p (k m) -> p k m", k=DT),
                          W["l1_wqcat"][:, m * 128:(m + 1) * 128]
                          .rearrange("(k p) m -> p k m", p=128))
            ps = cc.big0 if par == 0 else cc.big1
            i_ = None
            for dt_ in range(DT):
                cc.wait(t, tok_l)
                cc.wait(t, tok_xn)
                if dt_ == 0:
                    cc.wait(t, cc.tok("v"))
                    cc.wait(t, avev)
                i_ = t.matmul(ps[:, :], slab[:, dt_ * 128:(dt_ + 1) * 128],
                              xn_t(cc, dt_)[:, c0:c0 + 512],
                              start=(dt_ == 0), stop=(dt_ == DT - 1))
            tp = cc.inc(i_, "p")
            qslab_tok[par] = tp
            cc.wait(v, tp)
            cc.wait(v, tok_tab[0])
            v.tensor_tensor(Qc[:, m * 512:(m + 1) * 512], ps[:, :],
                            GQ[:, m * 512:(m + 1) * 512], ALU.mult)
            tok_q = cc.tok("v")
        for h in range(16):
            qb, qtile = 64 * (h % 2), h // 2
            gam = GAM[h]
            kts = [kt for kt in range(8 * qc + 8)
                   if gam ** (1024 * qc - 128 * kt + 512) >= 1e-28]
            exp_toks, av_toks = {}, {}
            for ki, kt in enumerate(kts):
                par = ki % 2
                ps_s = cc.big0 if par == 0 else cc.big1
                bb = kt // 4
                w_ = bb % 2
                vtt = (bb // 2) * 4 + kt % 4
                kcol = koff[qtile] + (bb // 2) * 512 + (kt % 4) * 128
                cc.wait(g, kused[par])
                tok_kv = ex_dma(cc, lambda c, kcol=kcol, w_=w_, vtt=vtt, h=h, par=par: [
                    (kr[qb:qb + 64, par * 128:par * 128 + 128],
                     buf[w_, qb:qb + 64, kcol:kcol + 128]),
                    (vr[:, par * 64:par * 64 + 64],
                     buf[w_, :, voff[vtt] + h * 64:voff[vtt] + h * 64 + 64]),
                ])
                cc.wait(t, tok_q)
                cc.wait(t, tok_kv)
                if ki >= 2:
                    cc.wait(t, exp_toks[ki - 2])
                if ki == 0:
                    cc.wait(t, avev)
                i_ = t.matmul(ps_s[:, :],
                              kr[qb:qb + 64, par * 128:par * 128 + 128],
                              Qc[qb:qb + 64, qtile * 512:(qtile + 1) * 512],
                              start=True, stop=True)
                ts_ = cc.inc(i_, "p")
                tfac = gam ** (1024 * qc - 128 * kt)
                Ssl = cc.xb2[:, par * 512:par * 512 + 512]
                tmp = cc.sq2[:, par * 512:par * 512 + 512]
                cc.wait(v, ts_)
                if ki >= 2:
                    cc.wait(v, av_toks[ki - 2])
                dprime = 128 * kt - 1024 * qc
                cc.wait(v, tok_tab[2])
                if -128 < dprime < 1024:
                    cc.wait(v, tok_tab[1])
                    v.tensor_scalar(tmp, ps_s[:, :], cc.l1s[:, h:h + 1],
                                    float(tfac), ALU.mult, ALU.mult)
                    v.tensor_tensor(Ssl, tmp,
                                    M01[:, 1408 - dprime:1920 - dprime], ALU.mult)
                else:
                    v.tensor_scalar(Ssl, ps_s[:, :], cc.l1s[:, h:h + 1],
                                    float(tfac), ALU.mult, ALU.mult)
                exp_toks[ki] = cc.tok("v")
                cc.wait(t, exp_toks[ki])
                i_ = t.matmul(cc.big2[0:64, :], vr[:, par * 64:par * 64 + 64],
                              Ssl, start=(ki == 0), stop=(ki == len(kts) - 1))
                av_toks[ki] = cc.inc(i_, "p")
                kused[par] = av_toks[ki]
            cc.wait(a, av_toks[len(kts) - 1])
            a.activation(avb[:, qtile * T + c0:qtile * T + c0 + 512][qb:qb + 64, :],
                         cc.big2[0:64, :], AF.Copy)
            avev = cc.tok("a")
    return _l1_gn_out(cc, W, avb, wring, cc.tok("a"))


def _l1_gn_out(cc, W, avb, wring, tok_av):
    nc = cc.nc
    t, v, a, g = nc.tensor, cc.V, cc.A, nc.gpsimd
    P1 = cc.l1s[:, 64:72]
    P2 = cc.l1s[:, 72:80]
    for ti in range(DT):
        cc.wait(v, tok_av)
        v.tensor_reduce(P1[:, ti:ti + 1], avb[:, ti * T:(ti + 1) * T],
                        axis=AX.X, op=ALU.add)
        for c0 in range(0, T, 512):
            sq = cc.sq2[:, 0:512]
            cc.wait(a, tok_av)
            cc.wait(a, cc.tok("v"))
            a.activation(sq, avb[:, ti * T + c0:ti * T + c0 + 512], AF.Square)
            cc.wait(v, cc.tok("a"))
            if c0 == 0:
                v.tensor_reduce(P2[:, ti:ti + 1], sq, axis=AX.X, op=ALU.add)
            else:
                v.tensor_reduce(cc.l1s[:, 80:81], sq, axis=AX.X, op=ALU.add)
                v.tensor_tensor(P2[:, ti:ti + 1], P2[:, ti:ti + 1],
                                cc.l1s[:, 80:81], ALU.add)
    tok_p = cc.tok("v")
    Ppe = cc.l1s[:, 81:97]
    rdt = exchange(cc, "l1st", [(Ppe, cc.l1s[:, 64:80])], wait_tokens=(tok_p,))
    cc.wait(v, rdt)
    v.tensor_tensor(P1, P1, Ppe[:, 0:8], ALU.add)
    v.tensor_tensor(P2, P2, Ppe[:, 8:16], ALU.add)
    tok_c = cc.tok("v")
    for st_, src in ((0, P1), (16, P2)):
        for hh in range(16):
            qb = 64 * (hh % 2)
            ti = hh // 2
            cc.wait(t, tok_c)
            i_ = t.matmul(cc.row0[0:1, st_ + hh:st_ + hh + 1],
                          cc.onescf[qb:qb + 64, 0:1], src[qb:qb + 64, ti:ti + 1],
                          start=True, stop=True)
    tok_s = cc.inc(i_, "p")
    cnt = 64.0 * 4096.0
    mean = cc.rowf[0:1, 0:16]
    var = cc.rowf[0:1, 16:32]
    rstd = cc.rowf[0:1, 32:48]
    cc.wait(v, tok_s)
    v.tensor_scalar(mean, cc.row0[0:1, 0:16], 1.0 / cnt, None, ALU.mult)
    v.tensor_tensor(var, mean, mean, ALU.mult)
    v.scalar_tensor_tensor(var, cc.row0[0:1, 16:32], 1.0 / cnt, var,
                           ALU.mult, ALU.subtract)
    tv = cc.tok("v")
    cc.wait(a, tv)
    a.activation(rstd, var, AF.Sqrt, bias=cc.constt[0:1, 0:1])
    cc.wait(v, cc.tok("a"))
    v.reciprocal(rstd, rstd)
    v.tensor_copy(cc.rowf[0:1, 48:64], mean)
    v.tensor_copy(cc.rowf[0:1, 64:80], rstd)
    tvb = cc.tok("v")
    bnc = cc.nc.dram_tensor("l1_bounce", [2, 16], F32, kind="Internal").ap()
    cc.wait(g, tvb)
    t1 = dma_g(cc, bnc.rearrange("a b -> (a b)")[None, :], cc.rowf[0:1, 48:80])
    cc.wait(g, t1)
    t2 = dma_g(cc, cc.l1s[0:16, 104:106], bnc.rearrange("a b -> b a"), slow=True)
    cc.wait(v, t2)
    v.tensor_scalar(cc.e2t[0:16, 128:136], cc.l1s[0:16, 32:40],
                    cc.l1s[0:16, 105:106], None, ALU.mult)
    v.tensor_scalar(cc.e2t[0:16, 136:144], cc.l1s[0:16, 32:40],
                    cc.l1s[0:16, 104:105], None, ALU.mult)
    tv2 = cc.tok("v")
    e2 = cc.l1s[0:16, 122:129] if False else None
    tok_e2 = dma_g(cc, cc.e2t[0:16, 0:128], W["l1_e2"][:, :])
    cc.wait(t, tv2)
    cc.wait(t, tok_e2)
    t.matmul(cc.big0[:, 0:8], cc.e2t[0:16, 0:128], cc.e2t[0:16, 128:136],
             start=True, stop=True)
    i_ = t.matmul(cc.big1[:, 0:8], cc.e2t[0:16, 0:128], cc.e2t[0:16, 136:144],
                  start=True, stop=True)
    tok_bc = cc.inc(i_, "p")
    cc.wait(v, tok_bc)
    A_ = cc.l1s[:, 40:48]
    B_ = cc.l1s[:, 48:56]
    v.tensor_tensor(A_, cc.l1s[:, 16:24], cc.big0[:, 0:8], ALU.mult)
    v.tensor_tensor(B_, cc.big1[:, 0:8], A_, ALU.mult)
    v.scalar_tensor_tensor(B_, B_, -1.0, cc.l1s[:, 24:32], ALU.mult, ALU.add)
    tok_ab = cc.tok("v")

    prev_ln = None
    slab_used = [None, None]
    yevac = [None, None]
    cn = pv(cc, 16384, 4096)
    for qc in range(NCH):
        c0 = qc * CH
        for ti in range(DT):
            cc.wait(v, tok_ab)
            if prev_ln is not None:
                cc.wait(v, cc.tok("p"))
            v.tensor_scalar(cn[:, ti * 512:(ti + 1) * 512],
                            avb[:, ti * T + c0:ti * T + c0 + 512],
                            A_[:, ti:ti + 1], B_[:, ti:ti + 1],
                            ALU.mult, ALU.add)
        tok_cn = cc.tok("v")
        tok_y = None
        for ot in range(DT):
            slab = wring[:, (ot % 2) * 1024:(ot % 2) * 1024 + 1024]
            cc.wait(g, slab_used[ot % 2])
            tok_l = None
            for dt_ in range(DT):
                tok_l = dma_g(cc, slab[:, dt_ * 128:(dt_ + 1) * 128],
                              W["L1_attn_Wout"][dt_ * 128:(dt_ + 1) * 128,
                                                ot * 128:(ot + 1) * 128])
            ps_o = cc.big4 if ot % 2 == 0 else cc.big5
            i_ = None
            for dt_ in range(DT):
                cc.wait(t, tok_cn)
                cc.wait(t, tok_l)
                if dt_ == 0:
                    cc.wait(t, yevac[ot % 2])
                if prev_ln is not None:
                    cc.wait(t, prev_ln)
                i_ = t.matmul(ps_o[:, :], slab[:, dt_ * 128:(dt_ + 1) * 128],
                              cn[:, dt_ * 512:(dt_ + 1) * 512],
                              start=(dt_ == 0), stop=(dt_ == DT - 1))
            tp = cc.inc(i_, "p")
            slab_used[ot % 2] = tp
            cc.wait(a, tp)
            cc.wait(a, prev_ln)
            a.activation(cc.ych[:, ot * 512:(ot + 1) * 512], ps_o[:, :], AF.Copy)
            tok_y = cc.tok("a")
            yevac[ot % 2] = tok_y
        prev_ln = emit_ln_cols(cc, lambda i: cc.ych[:, i * 512:(i + 1) * 512],
                               lambda i: xres_t(cc, i), "add", c0, src_c0=0,
                               src_is_bf16=True, wait_tokens=(tok_y,))
    return prev_ln


def _quant_weights_to_hbm(cc, W, wname, rows, cols, qbuf, recip_dst):
    nc = cc.nc
    t, v, a, g = nc.tensor, cc.V, cc.A, nc.gpsimd
    RT = rows // 128
    cc.wait(nc.sync, cc.tok("v"))
    cc.wait(nc.sync, cc.tok("a"))
    acc = cc.l1s[:, 0:1]
    first = [True]
    toks = []
    for rt in range(RT):
        for c0 in range(0, cols, 512):
            sl = cc.rowf[:, (len(toks) % 2) * 512:(len(toks) % 2) * 512 + 512]
            if len(toks) >= 2:
                cc.wait(nc.sync, cc.tok("a"))
            tok_l = dma_w(cc, sl, W[wname][rt * 128:(rt + 1) * 128, c0:c0 + 512])
            toks.append(tok_l)
            cc.wait(a, tok_l)
            cc.wait(a, cc.tok("v"))
            a.activation(cc.qswp[:, 0:512], sl, AF.Abs,
                         accum_out=cc.l1s[:, 1:2])
            cc.wait(v, cc.tok("a"))
            if first[0]:
                v.tensor_copy(acc, cc.l1s[:, 1:2])
                first[0] = False
            else:
                v.tensor_tensor(acc, acc, cc.l1s[:, 1:2], ALU.add)
    tok_acc = cc.tok("v")
    cc.wait(t, tok_acc)
    i_ = t.matmul(cc.row0[0:1, 0:1], cc.onescf[:, 0:1], acc, start=True, stop=True)
    tp = cc.inc(i_, "p")
    cc.wait(v, tp)
    v.tensor_scalar(cc.rowf[0:1, 1024:1025], cc.row0[0:1, 0:1],
                    1.0 / float(rows * cols), None, ALU.mult)
    v.tensor_scalar(cc.rowf[0:1, 1024:1025], cc.rowf[0:1, 1024:1025], 1e-5,
                    None, ALU.max)
    v.tensor_copy(recip_dst, cc.rowf[0:1, 1024:1025])
    v.reciprocal(cc.rowf[0:1, 1025:1026], cc.rowf[0:1, 1024:1025])
    v.tensor_copy(cc.rowR[0:1, 0:1], cc.rowf[0:1, 1025:1026])
    tv = cc.tok("v")
    cc.wait(t, tv)
    i_ = t.matmul(cc.big3[:, 0:1], cc.ones1[0:1, :], cc.rowR[0:1, 0:1],
                  start=True, stop=True)
    tp = cc.inc(i_, "p")
    cc.wait(v, tp)
    v.tensor_copy(cc.l1s[:, 2:3], cc.big3[:, 0:1])
    tok_s = cc.tok("v")
    cc.wait(nc.sync, cc.tok("a"))
    qi = cc.lntmp[:, 0:512]
    tok_qb = getattr(cc, "last_qbuf", None)
    for rt in range(RT):
        for c0 in range(0, cols, 512):
            sl = cc.rowf[:, ((rt * (cols // 512) + c0 // 512) % 2) * 512:
                         ((rt * (cols // 512) + c0 // 512) % 2) * 512 + 512]
            cc.wait(nc.sync, cc.tok("v"))
            tok_l = dma_w(cc, sl, W[wname][rt * 128:(rt + 1) * 128, c0:c0 + 512])
            cc.wait(v, tok_l)
            cc.wait(v, tok_s)
            cc.wait(v, tok_qb)
            v.tensor_scalar(qi, sl, cc.l1s[:, 2:3], None, ALU.mult)
            v.tensor_copy(cc.qint2[:, 0:512], qi)
            v.tensor_scalar(cc.qint2[:, 0:512], cc.qint2[:, 0:512], 1,
                            None, ALU.min)
            v.tensor_scalar(cc.qint2[:, 0:512], cc.qint2[:, 0:512], -1,
                            None, ALU.max)
            v.tensor_copy(cc.xb2[:, 0:512], cc.qint2[:, 0:512])
            tv = cc.tok("v")
            cc.wait(g, tv)
            tok_qb = dma_g(cc, qbuf[rt * 128:(rt + 1) * 128, c0:c0 + 512],
                           cc.xb2[:, 0:512])
    cc.last_qbuf = tok_qb
    return tok_qb


def emit_l2_ffn(cc, W):
    nc = cc.nc
    t, v, a, g = nc.tensor, cc.V, cc.A, nc.gpsimd
    HT = 32
    w1q = nc.dram_tensor("l2f_w1q", [1024, 4096], BF16, kind="Internal").ap()
    w2q = nc.dram_tensor("l2f_w2q", [4096, 1024], BF16, kind="Internal").ap()
    hsp = nc.dram_tensor("l2f_hsp", [NCH, HT, 128, 512], F32, kind="Internal").ap()
    ssp = nc.dram_tensor("l2f_ssp", [HT, 128, 512], F32, kind="Internal").ap()
    _quant_weights_to_hbm(cc, W, "l2_w1T", 1024, 4096, w1q, cc.l1s[0:1, 8:9])
    tok_q2 = _quant_weights_to_hbm(cc, W, "l2_w2T", 4096, 1024, w2q,
                                   cc.l1s[0:1, 9:10])
    tok_cw = dma_g(cc, cc.l1s[:, 16:112], W["l2_convw"][:, :])
    tok_cb = dma_g(cc, cc.scl[:, 0:32], W["l2_convb"][:, :])
    cc.wait(nc.vector, cc.last_qbuf)

    xq2 = pv(cc, 0, HT * 512)
    ring = pv(cc, 33800, 8192)
    edges = pv(cc, 16384, 257)
    v.memset(edges[:, 256:257], 0.0)

    def colminmax(src_fp32, first):
        ta_ = cc.tok("v")
        cc.wait(g, ta_)
        cc.wait(g, cc.tok("a"))
        i1 = g.tensor_reduce(cc.rowf[0:1, 1024:1536], src_fp32, axis=AX.C,
                             op=ALU.max)
        cc.inc(i1, "gp")
        cc.wait(v, cc.tok("gp"))
        if first:
            v.tensor_copy(cc.rowf[0:1, 0:512], cc.rowf[0:1, 1024:1536])
        else:
            v.tensor_tensor(cc.rowf[0:1, 0:512], cc.rowf[0:1, 0:512],
                            cc.rowf[0:1, 1024:1536], ALU.max)
        v.tensor_scalar(cc.lntmp[:, 0:512], src_fp32, -1.0, None, ALU.mult)
        tv_ = cc.tok("v")
        cc.wait(g, tv_)
        i2 = g.tensor_reduce(cc.rowf[0:1, 1024:1536], cc.lntmp[:, 0:512],
                             axis=AX.C, op=ALU.max)
        cc.inc(i2, "gp")
        cc.wait(v, cc.tok("gp"))
        if first:
            v.tensor_copy(cc.rowf[0:1, 512:1024], cc.rowf[0:1, 1024:1536])
        else:
            v.tensor_tensor(cc.rowf[0:1, 512:1024], cc.rowf[0:1, 512:1024],
                            cc.rowf[0:1, 1024:1536], ALU.max)

    def finish_scale(recip_cell):
        v.tensor_tensor(cc.rowf[0:1, 0:512], cc.rowf[0:1, 0:512],
                        cc.rowf[0:1, 512:1024], ALU.max)
        v.tensor_scalar(cc.rowf[0:1, 0:512], cc.rowf[0:1, 0:512], 1e-5,
                        None, ALU.max)
        v.tensor_scalar(cc.rowf[0:1, 1024:1536], cc.rowf[0:1, 0:512],
                        recip_cell, 1.0 / 127.0, ALU.mult, ALU.mult)
        v.reciprocal(cc.rowf[0:1, 0:512], cc.rowf[0:1, 0:512])
        v.tensor_scalar(cc.rowR[0:1, 0:512], cc.rowf[0:1, 0:512], 127.0,
                        None, ALU.mult)
        tv_ = cc.tok("v")
        cc.wait(t, tv_)
        i_ = t.matmul(cc.big3[:, :], cc.ones1[0:1, :], cc.rowR[0:1, :],
                      start=True, stop=True)
        return cc.inc(i_, "p")

    def quantize_tile(src_fp32, dst_bf16, tok_bc):
        cc.wait(v, tok_bc)
        v.tensor_tensor(cc.lntmp[:, 0:512], src_fp32, cc.big3[:, :], ALU.mult)
        v.tensor_copy(cc.qint2[:, 0:512], cc.lntmp[:, 0:512])
        v.tensor_scalar(cc.qint2[:, 0:512], cc.qint2[:, 0:512], 127,
                        None, ALU.min)
        v.tensor_scalar(cc.qint2[:, 0:512], cc.qint2[:, 0:512], -128,
                        None, ALU.max)
        v.tensor_copy(dst_bf16, cc.qint2[:, 0:512])

    prev_ln = None
    slab_tok = [None, None]
    yevac = [None, None]
    for qc in range(NCH):
        c0 = qc * CH
        wt = [] if prev_ln is None else [prev_ln]
        wt.append(getattr(cc, "_l2f_hsp_tok", None))
        tok_ln = emit_ln_cols(cc, lambda i: xres_t(cc, i),
                              lambda i: xn_t(cc, i), "set", c0,
                              wait_tokens=tuple(wt))
        for ti in range(DT):
            fl = cc.lntmp2[:, 0:512]
            cc.wait(v, tok_ln)
            v.tensor_tensor(fl, xres_t(cc, ti)[:, c0:c0 + 512], cc.big0[:, :],
                            ALU.mult)
            v.tensor_tensor(fl, fl, cc.big1[:, :], ALU.add)
            colminmax(fl, ti == 0)
        tok_bc = finish_scale(cc.l1s[0:1, 8:9])
        for ti in range(DT):
            fl = cc.lntmp2[:, 0:512]
            cc.wait(v, cc.tok("gp"))
            v.tensor_tensor(fl, xres_t(cc, ti)[:, c0:c0 + 512], cc.big0[:, :],
                            ALU.mult)
            v.tensor_tensor(fl, fl, cc.big1[:, :], ALU.add)
            quantize_tile(fl, xn_t(cc, ti)[:, c0:c0 + 512], tok_bc)
        tok_xq = cc.tok("v")
        cc.wait(v, cc.tok("p"))
        v.tensor_copy(cc.rowR[0:1, :], cc.rowf[0:1, 1024:1536])
        tv = cc.tok("v")
        cc.wait(t, tv)
        i_ = t.matmul(cc.big2[:, :], cc.ones1[0:1, :], cc.rowR[0:1, :],
                      start=True, stop=True)
        tok_u1 = cc.inc(i_, "p")
        for m in range(HT):
            par = m % 2
            ps = cc.big4 if par == 0 else cc.big5
            slab = ring[:, par * 1024:par * 1024 + 1024]
            cc.wait(g, slab_tok[par])
            tok_l = dma_g(cc, slab.rearrange("p (k m) -> p k m", k=DT),
                          w1q[:, m * 128:(m + 1) * 128]
                          .rearrange("(k p) m -> p k m", p=128))
            i_ = None
            for dt_ in range(DT):
                cc.wait(t, tok_l)
                cc.wait(t, tok_xq)
                if dt_ == 0:
                    cc.wait(t, cc.tok("a"))
                i_ = t.matmul(ps[:, :], slab[:, dt_ * 128:(dt_ + 1) * 128],
                              xn_t(cc, dt_)[:, c0:c0 + 512],
                              start=(dt_ == 0), stop=(dt_ == DT - 1))
            tp = cc.inc(i_, "p")
            slab_tok[par] = tp
            cc.wait(a, tp)
            a.activation(cc.lntmp[:, 0:512], ps[:, :], AF.Copy)
            ta = cc.tok("a")
            cc.wait(v, ta)
            cc.wait(v, tok_u1)
            cc.wait(v, getattr(cc, "_l2f_hsp_tok", None))
            v.tensor_tensor(cc.lntmp2[:, 0:512], cc.lntmp[:, 0:512],
                            cc.big2[:, :], ALU.mult)
            v.tensor_copy(edges[:, qc * 64 + m * 2:qc * 64 + m * 2 + 1],
                          cc.lntmp2[:, 0:1])
            v.tensor_copy(edges[:, qc * 64 + m * 2 + 1:qc * 64 + m * 2 + 2],
                          cc.lntmp2[:, 511:512])
            th = cc.tok("v")
            cc.wait(g, th)
            cc._l2f_hsp_tok = dma_g(cc, hsp[qc, m], cc.lntmp2[:, 0:512])
    tok_sp = cc.tok("g")
    tok_eg = cc.tok("v")
    ebuf, eoffs, ertok = publish(cc, "l2fe", [edges],
                                 wait_tokens=(tok_sp, tok_eg))
    for qc in range(NCH):
        c0 = qc * CH
        for m in range(HT):
            B = cc.lntmp2[:, 0:512]
            acc = cc.lntmp[:, 0:512]
            cc.wait(g, cc.tok("v"))
            cc.wait(g, cc.tok("gp"))
            cc.wait(g, ertok)
            gbar(cc, g)
            tok_hr = dma_g(cc, B, hsp[qc, m])

            def halo(c, m=m, qc=qc):
                pr = (c ^ 1) % 2
                p_ = c % 2
                if p_ == 0:
                    lsrc = (ebuf[pr, :, (qc - 1) * 64 + m * 2 + 1:
                                 (qc - 1) * 64 + m * 2 + 2]
                            if qc > 0 else ebuf[pr, :, 256:257])
                    rsrc = ebuf[pr, :, qc * 64 + m * 2:qc * 64 + m * 2 + 1]
                else:
                    lsrc = ebuf[pr, :, qc * 64 + m * 2 + 1:qc * 64 + m * 2 + 2]
                    rsrc = (ebuf[pr, :, (qc + 1) * 64 + m * 2:
                                 (qc + 1) * 64 + m * 2 + 1]
                            if qc < 3 else ebuf[pr, :, 256:257])
                return [(cc.l1s[:, 120:121], lsrc), (cc.l1s[:, 121:122], rsrc)]
            tok_halo = ex_dma(cc, halo, slow=True)
            w0 = cc.l1s[:, 16 + m * 3:17 + m * 3]
            w1_ = cc.l1s[:, 17 + m * 3:18 + m * 3]
            w2_ = cc.l1s[:, 18 + m * 3:19 + m * 3]
            cc.wait(v, tok_hr)
            cc.wait(v, tok_halo)
            cc.wait(v, tok_cw)
            v.tensor_scalar(acc[:, 1:512] if False else cc.lntmp[:, 1:512],
                            B[:, 0:511], w0, None, ALU.mult)
            v.tensor_scalar(cc.lntmp[:, 0:1], cc.l1s[:, 120:121], w0,
                            None, ALU.mult)
            v.scalar_tensor_tensor(acc, B, w1_, acc, ALU.mult, ALU.add)
            v.scalar_tensor_tensor(cc.lntmp[:, 0:511], B[:, 1:512], w2_,
                                   cc.lntmp[:, 0:511], ALU.mult, ALU.add)
            v.scalar_tensor_tensor(cc.lntmp[:, 511:512], cc.l1s[:, 121:122],
                                   w2_, cc.lntmp[:, 511:512], ALU.mult, ALU.add)
            tv = cc.tok("v")
            cc.wait(a, tv)
            cc.wait(a, tok_cb)
            cc.wait(a, getattr(cc, "_l2f_ssp_tok", None))
            a.activation(B, acc, AF.Silu, bias=cc.scl[:, m:m + 1])
            ta = cc.tok("a")
            cc.wait(v, ta)
            colminmax(B, m == 0)
            tv = cc.tok("v")
            cc.wait(g, tv)
            cc.wait(g, cc.tok("gp"))
            cc._l2f_ssp_tok = dma_g(cc, ssp[m], B)
        tok_bc2 = finish_scale(cc.l1s[0:1, 9:10])
        tok_ss = cc.tok("g")
        for m in range(HT):
            fl = cc.lntmp2[:, 0:512]
            cc.wait(g, cc.tok("v"))
            gbar(cc, g)
            tok_r = dma_g(cc, fl, ssp[m])
            cc.wait(v, tok_r)
            quantize_tile(fl, xq2[:, m * 512:(m + 1) * 512], tok_bc2)
        tok_xq2 = cc.tok("v")
        cc.wait(v, cc.tok("p"))
        v.tensor_copy(cc.rowR[0:1, :], cc.rowf[0:1, 1024:1536])
        tv = cc.tok("v")
        cc.wait(t, tv)
        i_ = t.matmul(cc.big2[:, :], cc.ones1[0:1, :], cc.rowR[0:1, :],
                      start=True, stop=True)
        tok_u2 = cc.inc(i_, "p")
        tok_y = None
        for ot in range(DT):
            par = ot % 2
            ps_o = cc.big4 if par == 0 else cc.big5
            slab = ring[:, par * 4096:par * 4096 + 4096]
            cc.wait(g, slab_tok[par])
            tok_l = dma_g(cc, slab.rearrange("p (k m) -> p k m", k=HT),
                          w2q[:, ot * 128:(ot + 1) * 128]
                          .rearrange("(k p) m -> p k m", p=128))
            i_ = None
            for k in range(HT):
                cc.wait(t, tok_l)
                cc.wait(t, tok_xq2)
                cc.wait(t, tok_q2)
                if k == 0:
                    cc.wait(t, yevac[par])
                i_ = t.matmul(ps_o[:, :], slab[:, k * 128:(k + 1) * 128],
                              xq2[:, k * 512:(k + 1) * 512],
                              start=(k == 0), stop=(k == HT - 1))
            tp = cc.inc(i_, "p")
            slab_tok[par] = tp
            cc.wait(a, tp)
            cc.wait(a, prev_ln)
            cc.wait(a, tok_y)
            a.activation(cc.xb2[:, 0:512], ps_o[:, :], AF.Copy)
            ta = cc.tok("a")
            cc.wait(v, ta)
            cc.wait(v, tok_u2)
            v.tensor_tensor(cc.ych[:, ot * 512:(ot + 1) * 512],
                            cc.xb2[:, 0:512], cc.big2[:, :], ALU.mult)
            tok_y = cc.tok("v")
            yevac[par] = tok_y
        prev_ln = emit_ln_cols(cc, lambda i: cc.ych[:, i * 512:(i + 1) * 512],
                               lambda i: xres_t(cc, i), "add", c0, src_c0=0,
                               src_is_bf16=True, wait_tokens=(tok_y,))
    return prev_ln


def emit_l2_attn(cc, W):
    nc = cc.nc
    t, v, a, g = nc.tensor, cc.V, cc.A, nc.gpsimd
    HT2 = 12
    ubuf = nc.dram_tensor("l2a_ubuf", [NCH, HT2, 128, 512], F32,
                          kind="Internal").ap()
    h0buf = nc.dram_tensor("l2a_h0buf", [NCH, HT2, 128, 512], F32,
                           kind="Internal").ap()
    pbuf = nc.dram_tensor("l2a_pbuf", [NCH, HT2, 128, 512], F32,
                          kind="Internal").ap()
    ybf = pv(cc, 0, HT2 * 512)
    zeros = pv(cc, 6144, 512)
    estage = pv(cc, 6656, 147)
    halo = pv(cc, 6816, 144)
    hg = pv(cc, 7168, HT2 * 512)
    ring = pv(cc, 33800, 8192)
    zero0 = cc.constt[:, 3:4]
    one0 = cc.constt[:, 4:5]

    tok_xn = emit_ln_in(cc)
    v.memset(zero0, 0.0)
    v.memset(one0, 1.0)
    v.memset(zeros[:, :], 0.0)
    v.memset(estage[:, 144:147], 0.0)
    v.memset(halo[:, 0:36], 0.0)
    tok_sc = [dma_w(cc, cc.scl[:, 0:48], W["l2a_cw"][:, :]),
              dma_w(cc, cc.scl[:, 48:60], W["l2a_cb"][:, :]),
              dma_w(cc, cc.scl[:, 60:72], W["l2a_lam"][:, :]),
              dma_w(cc, cc.scl[:, 84:96], W["l2a_bi"][:, :]),
              dma_w(cc, cc.scl[:, 96:108], W["l2a_bg"][:, :]),
              dma_w(cc, cc.scl[:, 108:109], W["l2_parf"][:, :])]
    lam = cc.scl[:, 60:72]
    spl = cc.scl[:, 72:84]
    tmpc = cc.scl[:, 112:124]
    for tk in tok_sc:
        cc.wait(a, tk)
        cc.wait(v, tk)
    cc.wait(a, cc.tok("v"))
    a.activation(tmpc, lam, AF.Abs, bias=zero0)
    cc.wait(v, cc.tok("a"))
    v.tensor_scalar(tmpc, tmpc, -1.0, None, ALU.mult)
    cc.wait(a, cc.tok("v"))
    a.activation(tmpc, tmpc, AF.Exp, bias=zero0)
    cc.wait(v, cc.tok("a"))
    v.tensor_scalar(tmpc, tmpc, 1.0, None, ALU.add)
    cc.wait(a, cc.tok("v"))
    a.activation(tmpc, tmpc, AF.Ln, bias=zero0)
    v.tensor_scalar(spl, lam, 0.0, None, ALU.max)
    cc.wait(v, cc.tok("a"))
    v.tensor_tensor(spl, spl, tmpc, ALU.add)
    v.tensor_scalar(spl, spl, -8.0, None, ALU.mult)

    rtok = {}
    spill_tok = [None, None]
    for k in range(NCH):
        ck = k * CH
        for m in range(HT2):
            par = m % 2
            ps = cc.big4 if par == 0 else cc.big5
            roff = par * 1024
            slab = ring[:, roff:roff + 1024]
            cc.wait(g, rtok.get(roff))
            tok_l = dma_g(cc, slab.rearrange("p (kt m) -> p kt m", kt=DT),
                          W["l2_wiT"][:, m * 128:(m + 1) * 128]
                          .rearrange("(kt p) m -> p kt m", p=128))
            i_ = None
            for kt in range(DT):
                cc.wait(t, tok_l)
                cc.wait(t, tok_xn)
                if kt == 0:
                    cc.wait(t, cc.tok("a"))
                i_ = t.matmul(ps[:, :], slab[:, kt * 128:(kt + 1) * 128],
                              xn_t(cc, kt)[:, ck:ck + 512],
                              start=(kt == 0), stop=(kt == DT - 1))
            tp = cc.inc(i_, "p")
            rtok[roff] = tp
            cc.wait(a, tp)
            cc.wait(a, spill_tok[par])
            cc.wait(a, cc.tok("v"))
            dst = cc.lntmp if par == 0 else cc.lntmp2
            a.activation(dst[:, 0:512], ps[:, :], AF.Copy)
            ta = cc.tok("a")
            cc.wait(v, ta)
            eb = 36 * k + 3 * m
            v.tensor_copy(estage[:, eb:eb + 3], dst[:, 509:512])
            tv = cc.tok("v")
            cc.wait(g, tv)
            spill_tok[par] = dma_g(cc, ubuf[k, m], dst[:, 0:512])
    ertok = None
    ebuf, _, ertok = publish(cc, "l2ae", [estage],
                             wait_tokens=(cc.tok("v"),))
    cc.wait(g, ertok)
    cc.wait(g, cc.tok("v"))
    tok_halo = ex_dma(cc, lambda c: [
        ((halo[:, 36:144], ebuf[(c % 2) ^ 1, :, 0:108]) if c % 2 == 0
         else (halo[:, 0:144], ebuf[(c % 2) ^ 1, :, 0:144]))])

    rspill = [None, None]
    for k in range(NCH):
        for m in range(HT2):
            B = cc.lntmp2[:, 0:512]
            acc = cc.lntmp[:, 0:512]
            cc.wait(g, cc.tok("v"))
            cc.wait(g, cc.tok("a"))
            gbar(cc, g)
            tok_u = dma_g(cc, B, ubuf[k, m])
            hb = 36 * k + 3 * m
            w0 = cc.scl[:, 4 * m:4 * m + 1]
            w1 = cc.scl[:, 4 * m + 1:4 * m + 2]
            w2 = cc.scl[:, 4 * m + 2:4 * m + 3]
            w3 = cc.scl[:, 4 * m + 3:4 * m + 4]
            cb = cc.scl[:, 48 + m:49 + m]
            cc.wait(v, tok_u)
            cc.wait(v, tok_halo)
            v.tensor_scalar(acc, B, w3, None, ALU.mult)
            v.scalar_tensor_tensor(acc[:, 1:512], B[:, 0:511], w2,
                                   acc[:, 1:512], ALU.mult, ALU.add)
            v.scalar_tensor_tensor(acc[:, 0:1], halo[:, hb + 2:hb + 3], w2,
                                   acc[:, 0:1], ALU.mult, ALU.add)
            v.scalar_tensor_tensor(acc[:, 2:512], B[:, 0:510], w1,
                                   acc[:, 2:512], ALU.mult, ALU.add)
            v.scalar_tensor_tensor(acc[:, 0:2], halo[:, hb + 1:hb + 3], w1,
                                   acc[:, 0:2], ALU.mult, ALU.add)
            v.scalar_tensor_tensor(acc[:, 3:512], B[:, 0:509], w0,
                                   acc[:, 3:512], ALU.mult, ALU.add)
            v.scalar_tensor_tensor(acc[:, 0:3], halo[:, hb:hb + 3], w0,
                                   acc[:, 0:3], ALU.mult, ALU.add)
            v.tensor_scalar(ybf[:, m * 512:(m + 1) * 512], acc, cb,
                            None, ALU.add)
        tok_y = cc.tok("v")
        for m in range(HT2):
            par = m % 2
            rgf = cc.lntmp[:, 0:512]
            alf = cc.lntmp2[:, 0:512]
            igf = cc.rowf[:, 0:512]
            btf = cc.rowf[:, 512:1024]
            xbf = cc.rowf[:, 1024:1536]
            ps = cc.big4
            roff = 2048 + par * 1536
            slab = ring[:, roff:roff + 1536]
            cc.wait(g, rtok.get(roff))
            tok_l = dma_g(cc, slab.rearrange("p (kt m) -> p kt m", kt=HT2),
                          W["l2_lwgT"][:, m * 128:(m + 1) * 128]
                          .rearrange("(kt p) m -> p kt m", p=128))
            i_ = None
            for kt in range(HT2):
                cc.wait(t, tok_l)
                cc.wait(t, tok_y)
                if kt == 0:
                    cc.wait(t, cc.tok("a"))
                i_ = t.matmul(ps[:, :], slab[:, kt * 128:(kt + 1) * 128],
                              ybf[:, kt * 512:(kt + 1) * 512],
                              start=(kt == 0), stop=(kt == HT2 - 1))
            tp = cc.inc(i_, "p")
            rtok[roff] = tp
            cc.wait(a, tp)
            cc.wait(a, rspill[0])
            cc.wait(a, cc.tok("v"))
            a.activation(rgf, ps[:, :], AF.Sigmoid,
                         bias=cc.scl[:, 96 + m:97 + m])
            cc.wait(v, cc.tok("a"))
            v.tensor_scalar(alf, rgf, cc.scl[:, 72 + m:73 + m], None, ALU.mult)
            cc.wait(a, cc.tok("v"))
            a.activation(alf, alf, AF.Exp, bias=zero0)
            ps2 = cc.big5
            roff2 = 5120 + par * 1536
            slab2 = ring[:, roff2:roff2 + 1536]
            cc.wait(g, rtok.get(roff2))
            tok_l2 = dma_g(cc, slab2.rearrange("p (kt m) -> p kt m", kt=HT2),
                           W["l2_lwiT"][:, m * 128:(m + 1) * 128]
                           .rearrange("(kt p) m -> p kt m", p=128))
            i_ = None
            for kt in range(HT2):
                cc.wait(t, tok_l2)
                if kt == 0:
                    cc.wait(t, cc.tok("a"))
                i_ = t.matmul(ps2[:, :], slab2[:, kt * 128:(kt + 1) * 128],
                              ybf[:, kt * 512:(kt + 1) * 512],
                              start=(kt == 0), stop=(kt == HT2 - 1))
            tp2 = cc.inc(i_, "p")
            rtok[roff2] = tp2
            cc.wait(a, tp2)
            cc.wait(a, cc.tok("v"))
            a.activation(igf, ps2[:, :], AF.Sigmoid,
                         bias=cc.scl[:, 84 + m:85 + m])
            cc.wait(v, cc.tok("a"))
            cc.wait(v, rspill[1])
            v.tensor_tensor(btf, alf, alf, ALU.mult)
            v.tensor_scalar(btf, btf, -1.0, 1.0 + 1e-6, ALU.mult, ALU.add)
            cc.wait(a, cc.tok("v"))
            a.activation(btf, btf, AF.Sqrt, bias=zero0)
            cc.wait(v, cc.tok("a"))
            v.tensor_tensor(xbf, btf, igf, ALU.mult)
            v.tensor_tensor(xbf, xbf, ybf[:, m * 512:(m + 1) * 512], ALU.mult)
            v.tensor_tensor_scan(rgf, alf, xbf, zero0[:, 0:1],
                                 ALU.mult, ALU.add)
            v.tensor_copy(cc.l1s[:, 48 + 12 * k + m:49 + 12 * k + m],
                          rgf[:, 511:512])
            tv = cc.tok("v")
            cc.wait(g, tv)
            rspill[0] = dma_g(cc, h0buf[k, m], rgf)
            v.tensor_tensor_scan(btf, alf, zeros[:, 0:512], one0[:, 0:1],
                                 ALU.mult, ALU.add)
            v.tensor_copy(cc.l1s[:, 12 * k + m:12 * k + m + 1],
                          btf[:, 511:512])
            tv = cc.tok("v")
            cc.wait(g, tv)
            rspill[1] = dma_g(cc, pbuf[k, m], btf)

    sbuf, soffs, srtok = publish(cc, "l2as", [cc.l1s[:, 0:96]],
                                 wait_tokens=(cc.tok("v"),))
    cc.wait(g, srtok)
    cc.wait(g, cc.tok("v"))
    gbar(cc, g)
    tok_rA = dma_g(cc, cc.rowf[:, 0:96], sbuf[0, :, 0:96])
    tok_rB = dma_g(cc, cc.rowf[:, 96:192], sbuf[1, :, 0:96])
    cc.wait(v, tok_rA)
    cc.wait(v, tok_rB)
    Hcur = cc.rowf[:, 192:204]
    Hodd = cc.rowf[:, 204:216]
    tmp = cc.rowf[:, 216:228]
    tmp2 = cc.rowf[:, 228:240]
    parf = cc.scl[:, 108:109]
    v.memset(Hcur, 0.0)
    for k in range(NCH):
        rE = cc.rowf[:, 0:96]
        A_e = rE[:, 12 * k:12 * k + 12]
        h_e = rE[:, 48 + 12 * k:48 + 12 * k + 12]
        rO = cc.rowf[:, 96:192]
        A_o = rO[:, 12 * k:12 * k + 12]
        h_o = rO[:, 48 + 12 * k:48 + 12 * k + 12]
        v.tensor_tensor(tmp, A_e, Hcur, ALU.mult)
        v.tensor_tensor(Hodd, tmp, h_e, ALU.add)
        v.tensor_tensor(tmp2, Hodd, Hcur, ALU.subtract)
        v.tensor_scalar(tmp2, tmp2, parf, None, ALU.mult)
        v.tensor_tensor(cc.l1s[:, 12 * k:12 * k + 12], Hcur, tmp2, ALU.add)
        if k < NCH - 1:
            v.tensor_tensor(tmp, A_o, Hodd, ALU.mult)
            v.tensor_tensor(Hcur, tmp, h_o, ALU.add)

    prev_ln = None
    yevac = [None, None]
    for k in range(NCH):
        ck = k * CH
        for m in range(HT2):
            par = m % 2
            hf = cc.rowf[:, 0:512]
            ggf = cc.rowf[:, 512:1024]
            cc.wait(g, cc.tok("v"))
            gbar(cc, g)
            tok_s0 = dma_g(cc, cc.lntmp[:, 0:512], h0buf[k, m])
            tok_pp = dma_g(cc, cc.lntmp2[:, 0:512], pbuf[k, m])
            ps = cc.big4 if par == 0 else cc.big5
            roff = par * 1024
            slab = ring[:, roff:roff + 1024]
            cc.wait(g, rtok.get(roff))
            tok_l = dma_g(cc, slab.rearrange("p (kt m) -> p kt m", kt=DT),
                          W["l2_wgT"][:, m * 128:(m + 1) * 128]
                          .rearrange("(kt p) m -> p kt m", p=128))
            i_ = None
            for kt in range(DT):
                cc.wait(t, tok_l)
                if kt == 0:
                    cc.wait(t, cc.tok("a"))
                i_ = t.matmul(ps[:, :], slab[:, kt * 128:(kt + 1) * 128],
                              xn_t(cc, kt)[:, ck:ck + 512],
                              start=(kt == 0), stop=(kt == DT - 1))
            tp = cc.inc(i_, "p")
            rtok[roff] = tp
            cc.wait(a, tp)
            cc.wait(a, cc.tok("v"))
            a.activation(ggf, ps[:, :], AF.Gelu, bias=zero0)
            cc.wait(v, tok_s0)
            cc.wait(v, tok_pp)
            v.scalar_tensor_tensor(hf, cc.lntmp2[:, 0:512],
                                   cc.l1s[:, 12 * k + m:12 * k + m + 1],
                                   cc.lntmp[:, 0:512], ALU.mult, ALU.add)
            cc.wait(v, cc.tok("a"))
            v.tensor_tensor(hg[:, m * 512:(m + 1) * 512], ggf, hf, ALU.mult)
        tok_hg = cc.tok("v")
        for o in range(DT):
            par = o % 2
            ps = cc.big4 if par == 0 else cc.big5
            roff = 2048 + par * 1536
            slab = ring[:, roff:roff + 1536]
            cc.wait(g, rtok.get(roff))
            tok_l = dma_g(cc, slab.rearrange("p (kt m) -> p kt m", kt=HT2),
                          W["l2_woT"][:, o * 128:(o + 1) * 128]
                          .rearrange("(kt p) m -> p kt m", p=128))
            i_ = None
            for kt in range(HT2):
                cc.wait(t, tok_l)
                cc.wait(t, tok_hg)
                if kt == 0:
                    cc.wait(t, yevac[par])
                    cc.wait(t, cc.tok("a"))
                i_ = t.matmul(ps[:, :], slab[:, kt * 128:(kt + 1) * 128],
                              hg[:, kt * 512:(kt + 1) * 512],
                              start=(kt == 0), stop=(kt == HT2 - 1))
            tp = cc.inc(i_, "p")
            rtok[roff] = tp
            cc.wait(a, tp)
            cc.wait(a, prev_ln)
            a.activation(cc.ych[:, o * 512:(o + 1) * 512], ps[:, :], AF.Copy)
            yevac[par] = cc.tok("a")
        prev_ln = emit_ln_cols(cc, lambda i: cc.ych[:, i * 512:(i + 1) * 512],
                               lambda i: xres_t(cc, i), "add", ck, src_c0=0,
                               src_is_bf16=True, wait_tokens=(cc.tok("a"),))
    return prev_ln


BLOCK_EMITTERS = {"l0_attn": emit_l0_attn, "l1_ffn": emit_l1_ffn,
                  "l3_ffn": emit_l3_ffn, "l0_ffn": emit_l0_ffn,
                  "l3_attn": emit_l3_attn, "l1_attn": emit_l1_attn,
                  "l2_ffn": emit_l2_ffn, "l2_attn": emit_l2_attn}


def build_program(block_names, weight_specs, shared_specs):
    nc = bass.Bass("TRN2", target_bir_lowering=False)
    cc = CC(nc)
    xin = nc.dram_tensor("xin", [D, T], F32, kind="ExternalInput").ap()
    xout = nc.dram_tensor("xout", [D, T], F32, kind="ExternalOutput").ap()
    cfg = nc.dram_tensor("cfg", [1, 8], I32, kind="ExternalInput").ap()
    W = {}
    for name, (shape, dtname) in weight_specs.items():
        W[name] = nc.dram_tensor(
            name, list(shape),
            {"float32": F32, "int32": I32, "bfloat16": BF16}[dtname],
            kind="ExternalInput").ap()
    for name, shape, dt_ in shared_specs:
        setattr(cc, name, nc.dram_tensor(name, list(shape), dt_, kind="Internal",
                                         addr_space="Shared").ap())
    cc.sh_flg = nc.dram_tensor("sh_flg", [8, 64], I32, kind="Internal",
                               addr_space="Shared").ap()

    st = _ExitStack()

    def sb(name, cols, dtype, parts=128):
        cm = nc.sbuf_tensor(name, [parts, cols], dtype)
        h = cm.__enter__()
        st.push(cm)
        return h[:, :]

    try:
        cc.xres = sb("xres", 8 * T, F32)
        cc.xn = sb("xn", 8 * T, BF16)
        cc.pool = sb("pool", POOL_COLS, BF16)
        cc.ych = sb("ych", 4096, BF16)
        cc.xb2 = sb("xb2", 1024, BF16)
        cc.sq2 = sb("sq2", 1024, BF16)
        cc.lntmp = sb("lntmp", 512, F32)
        cc.lntmp2 = sb("lntmp2", 512, F32)
        cc.qswp = sb("qswp", 512, BF16)
        cc.rowf = sb("rowf", 1536, F32)
        cc.rowR = sb("rowR", 512, BF16)
        cc.rowM = sb("rowM", 512, BF16)
        cc.ones1 = sb("ones1", 128, BF16, parts=1)
        cc.onesc = sb("onesc", 1, BF16)
        cc.onescf = sb("onescf", 1, F32)
        cc.constt = sb("constt", 16, F32)
        cc.scl = sb("scl", 136, F32)
        cc.l1s = sb("l1s", 128, F32)
        cc.qint2 = sb("qint2", 512, I32)
        cc.e2t = sb("e2t", 160, BF16)
        cc.cfgt = sb("cfgt", 8, I32, parts=1)
        cc.flagtile = sb("flagtile", 32, I32, parts=1)
        cc.noncetile = sb("noncetile", 1, I32, parts=1)

        with (
            nc.psum_tensor("big0", [128, 512], F32) as b0,
            nc.psum_tensor("big1", [128, 512], F32) as b1,
            nc.psum_tensor("big2", [128, 512], F32) as b2,
            nc.psum_tensor("big3", [128, 512], F32) as b3,
            nc.psum_tensor("big4", [128, 512], F32) as b4,
            nc.psum_tensor("big5", [128, 512], F32) as b5,
            nc.psum_tensor("row0", [1, 512], F32) as r0,
            nc.psum_tensor("row1", [1, 512], F32) as r1,
            nc.Block(),
        ):
            cc.big0, cc.big1, cc.big2 = b0[:, :], b1[:, :], b2[:, :]
            cc.big3, cc.big4, cc.big5 = b3[:, :], b4[:, :], b5[:, :]
            cc.row0, cc.row1 = r0[:, :], r1[:, :]

            cc.V = ChainedEngine(cc, nc.vector, "v", _DVE_OPS)
            cc.A = ChainedEngine(cc, nc.scalar, "a", _ACT_OPS)
            g, v = nc.gpsimd, cc.V
            t_cfg = cc.inc(nc.gpsimd.dma_start(out=cc.cfgt[0:1, :],
                                               in_=cfg[0:1, :]), "cfg", 16)
            t_x = dma_w(cc, cc.xres.rearrange("p (n m) -> p n m", n=DT),
                        xin.rearrange("(n p) m -> p n m", p=128))
            cc.wait(g, t_cfg)
            me_cm = g.register("me")
            cc.me_reg = me_cm.__enter__()
            nonce_cm = g.register("nonce")
            cc.nonce_reg = nonce_cm.__enter__()
            g.load(cc.me_reg, cc.cfgt[0:1, 0:1])
            g.load(cc.nonce_reg, cc.cfgt[0:1, 1:2])
            g.tensor_copy(cc.noncetile[0:1, 0:1], cc.cfgt[0:1, 1:2])
            g.memset(cc.flagtile[0:1, :], 0)
            v.memset(cc.ones1[0:1, :], 1.0)
            v.memset(cc.constt[:, 0:1], float(LN_EPS))
            v.memset(cc.constt[:, 1:2], -L0_SHIFT)
            v.memset(cc.constt[:, 2:3], -L3_SHIFT)
            v.memset(cc.onesc[:, 0:1], 1.0)
            v.memset(cc.onescf[:, 0:1], 1.0)
            tv = cc.tok("v")
            cc.wait(nc.tensor, tv)
            for eng in (nc.tensor, v, nc.scalar):
                cc.wait(eng, t_x)

            for bi, bn in enumerate(block_names):
                if bi > 0:
                    block_fence(cc)
                BLOCK_EMITTERS[bn](cc, W)

            sc = nc.scalar
            for nm in ("v", "a", "p"):
                cc.wait(sc, cc.tok(nm))
            t_out = dma_s(cc, xout.rearrange("(n p) m -> p n m", p=128),
                          cc.xres.rearrange("p (n m) -> p n m", n=DT))
            cc.wait(sc, t_out)
            me_cm.__exit__(None, None, None)
            nonce_cm.__exit__(None, None, None)
    finally:
        st.close()
    return nc


def _rope_tables(p):
    half = 32
    inv_freq = 1.0 / (10000.0 ** (np.arange(half, dtype=np.float64) * 2.0 / 64.0))
    pos = own_positions(p).astype(np.float64)
    fr = pos[None, :] * inv_freq[:, None]
    cos, sin = np.cos(fr), np.sin(fr)
    Ctab = np.zeros((128, T), np.float32)
    Stab = np.zeros((128, T), np.float32)
    for blk in range(2):
        r = blk * 64
        Ctab[r:r + 32] = cos
        Ctab[r + 32:r + 64] = cos
        Stab[r:r + 32] = -sin
        Stab[r + 32:r + 64] = sin
    return Ctab.astype(BF), Stab.astype(BF)


def _p128swap():
    P = np.zeros((128, 128), np.float32)
    for i in range(128):
        blk, off = divmod(i, 32)
        j = (blk ^ 1) * 32 + off
        P[j, i] = 1.0
    return P.astype(BF)


L0A_WEIGHTS = {
    "L0_attn_w_q": ((1024, 1024), "float32"),
    "L0_attn_w_k": ((1024, 256), "float32"),
    "L0_attn_w_v": ((1024, 256), "float32"),
    "L0_attn_w_o": ((1024, 1024), "float32"),
    "rope_c": ((128, T), "bfloat16"),
    "rope_s": ((128, T), "bfloat16"),
    "p128": ((128, 128), "bfloat16"),
}

L0A_SHARED = []

_NC_CACHE = {}


def _common_inputs(nonce):
    outs = []
    p128 = _p128swap()
    for c in range(8):
        p = c % 2
        ct, stt = _rope_tables(p)
        outs.append({
            "cfg": np.array([[c, nonce, c // 2, p, 0, 0, 0, 0]], np.int32),
            "rope_c": ct, "rope_s": stt, "p128": p128,
        })
    return outs


def _l0_head_col_perm():
    idx = []
    for s_ in range(16):
        h = L0_HEAD_PERM[s_]
        idx.extend(range(h * 64, h * 64 + 64))
    return np.array(idx)


WEIGHT_TRANSFORMS = {
    "L0_attn_w_q": lambda w: np.ascontiguousarray(w[:, _l0_head_col_perm()]),
    "L0_attn_w_o": lambda w: np.ascontiguousarray(w[_l0_head_col_perm(), :]),
}

DERIVED_INPUTS = {
    "l1_wgT": lambda ins: np.ascontiguousarray(ins["L1_ffn_w_gate"].T),
    "l1_wuT": lambda ins: np.ascontiguousarray(ins["L1_ffn_w_up"].T),
    "l1_wdT": lambda ins: np.ascontiguousarray(ins["L1_ffn_w_down"].T),
}

L1F_WEIGHTS = {
    "l1_wgT": ((1024, 3072), "int32"),
    "l1_wuT": ((1024, 3072), "int32"),
    "l1_wdT": ((3072, 1024), "int32"),
}


def _tile_t1(G1, c):
    t = np.asarray(G1)[0][:, c, :]
    return np.ascontiguousarray(
        np.broadcast_to(t.T[:, None, :, None], (32, 16, 32, 16))
        .reshape(512, 512)).astype(np.float32)


def _tile_t2(G2, c):
    u = np.asarray(G2)[c, :, 0, :]
    return np.ascontiguousarray(
        np.broadcast_to(u.T[None, :, None, :], (32, 16, 32, 16))
        .reshape(512, 512)).astype(np.float32)


L0F_WEIGHTS = {
    "l0_w1T": ((1024, 4096), "float32"),
    "l0_w2T": ((1024, 4096), "float32"),
    "l0_h1T": ((4096, 4096), "float32"),
    "l0_h2T": ((4096, 4096), "float32"),
    "l0_w3T": ((4096, 1024), "float32"),
    "l0_scales": ((128, 136), "float32"),
}
DERIVED_INPUTS["l0_w1T"] = lambda ins: np.ascontiguousarray(np.asarray(ins["L0_ffn_w1_w"]).T)
DERIVED_INPUTS["l0_w2T"] = lambda ins: np.ascontiguousarray(np.asarray(ins["L0_ffn_w2_w"]).T)
DERIVED_INPUTS["l0_h1T"] = lambda ins: np.ascontiguousarray(np.asarray(ins["L0_ffn_h1_w"]).T)
DERIVED_INPUTS["l0_h2T"] = lambda ins: np.ascontiguousarray(np.asarray(ins["L0_ffn_h2_w"]).T)
DERIVED_INPUTS["l0_w3T"] = lambda ins: np.ascontiguousarray(np.asarray(ins["L0_ffn_w3_w"]).T)


def _l0_scales(ins):
    out = np.zeros((128, 136), np.float32)
    out[:, 0:32] = np.asarray(ins["L0_ffn_w1_s"]).reshape(32, 128).T
    out[:, 32:64] = np.asarray(ins["L0_ffn_w2_s"]).reshape(32, 128).T
    out[:, 64:96] = np.asarray(ins["L0_ffn_h1_s"]).reshape(32, 128).T
    out[:, 96:128] = np.asarray(ins["L0_ffn_h2_s"]).reshape(32, 128).T
    out[:, 128:136] = np.asarray(ins["L0_ffn_w3_s"]).reshape(8, 128).T
    return out


DERIVED_INPUTS["l0_scales"] = _l0_scales

L3A_WEIGHTS = {
    "l3_wq2": ((128, 64), "float32"),
    "l3_wk2": ((128, 64), "float32"),
    "l3_wv2": ((128, 64), "float32"),
    "l3_m3": ((128, 1920), "bfloat16"),
    "l3_woT": ((1024, 1024), "float32"),
}
DERIVED_INPUTS["l3_wq2"] = lambda ins: np.ascontiguousarray(
    np.concatenate([np.asarray(ins["L3_attn_Wq"]).T] * 2, axis=0).astype(np.float32))
DERIVED_INPUTS["l3_wk2"] = lambda ins: np.ascontiguousarray(
    np.concatenate([np.asarray(ins["L3_attn_Wk"]).T] * 2, axis=0).astype(np.float32))
DERIVED_INPUTS["l3_wv2"] = lambda ins: np.ascontiguousarray(
    np.concatenate([np.asarray(ins["L3_attn_Wv"]).T] * 2, axis=0).astype(np.float32))
DERIVED_INPUTS["l3_woT"] = lambda ins: np.ascontiguousarray(
    np.asarray(ins["L3_attn_Wo"]).T.astype(np.float32))


def _l3_m3(p):
    kk = np.arange(128)[:, None]
    u = np.arange(1920)[None, :]
    return np.where(u >= kk + 1408 - 512 * p, 0.0, -240.0).astype(BF)


L1A_WEIGHTS = {
    "l1_wqcat": ((1024, 1024), "float32"),
    "l1_wkcat": ((1024, 1024), "float32"),
    "l1_wvcat": ((1024, 1024), "float32"),
    "L1_attn_Wout": ((1024, 1024), "float32"),
    "l1_gq": ((128, 4096), "bfloat16"),
    "l1_m01": ((128, 1920), "bfloat16"),
    "l1_gk": ((128, 16), "float32"),
    "l1_gnw": ((128, 8), "float32"),
    "l1_gnb": ((128, 8), "float32"),
    "l1_msk16": ((16, 8), "float32"),
    "l1_e2": ((16, 128), "bfloat16"),
}


def _l1_cat(w):
    w = np.asarray(w)
    return np.ascontiguousarray(w.transpose(1, 0, 2).reshape(1024, 1024)
                                .astype(np.float32))


DERIVED_INPUTS["l1_wqcat"] = lambda ins: _l1_cat(ins["L1_attn_Wq"])
DERIVED_INPUTS["l1_wkcat"] = lambda ins: _l1_cat(ins["L1_attn_Wk"])
DERIVED_INPUTS["l1_wvcat"] = lambda ins: _l1_cat(ins["L1_attn_Wv"])
DERIVED_INPUTS["l1_gnw"] = lambda ins: np.ascontiguousarray(
    np.asarray(ins["L1_attn_gn_weight"]).astype(np.float32).reshape(8, 128).T)
DERIVED_INPUTS["l1_gnb"] = lambda ins: np.ascontiguousarray(
    np.asarray(ins["L1_attn_gn_bias"]).astype(np.float32).reshape(8, 128).T)

L1_GAMMAS = [1.0 - 2.0 ** (-5 - (h // 2)) for h in range(16)]


def _l1_gq(p):
    out = np.zeros((128, 4096), np.float64)
    qq = np.arange(512, dtype=np.float64)
    for t_ in range(8):
        for par in range(2):
            gam = L1_GAMMAS[2 * t_ + par]
            out[par * 64:(par + 1) * 64, t_ * 512:(t_ + 1) * 512] = \
                gam ** (qq + 512.0 * p) * 0.125
    return out.astype(BF)


def _l1_gk():
    out = np.zeros((128, 16), np.float64)
    kk = np.arange(128, dtype=np.float64)
    for h in range(16):
        out[:, h] = L1_GAMMAS[h] ** (-kk)
    return out.astype(np.float32)


def _l1_m01(p):
    kk = np.arange(128)[:, None]
    u = np.arange(1920)[None, :]
    return np.where(u >= kk + 1408 - 512 * p, 1.0, 0.0).astype(BF)


def _l1_msk16():
    out = np.zeros((16, 8), np.float32)
    for h in range(16):
        out[h, h // 2] = 1.0
    return out


def _l1_e2():
    out = np.zeros((16, 128), np.float32)
    for h in range(16):
        out[h, 64 * (h % 2):64 * (h % 2) + 64] = 1.0
    return out.astype(BF)


DERIVED_INPUTS["l1_gk"] = lambda ins: _l1_gk()
DERIVED_INPUTS["l1_msk16"] = lambda ins: _l1_msk16()
DERIVED_INPUTS["l1_e2"] = lambda ins: _l1_e2()

L2F_WEIGHTS = {
    "l2_w1T": ((1024, 4096), "float32"),
    "l2_w2T": ((4096, 1024), "float32"),
    "l2_convw": ((128, 96), "float32"),
    "l2_convb": ((128, 32), "float32"),
}
DERIVED_INPUTS["l2_w1T"] = lambda ins: np.ascontiguousarray(
    np.asarray(ins["L2_ffn_w1"]).T.astype(np.float32))
DERIVED_INPUTS["l2_w2T"] = lambda ins: np.ascontiguousarray(
    np.asarray(ins["L2_ffn_w2"]).T.astype(np.float32))


def _l2_convw(ins):
    w = np.asarray(ins["L2_ffn_conv_w"]).astype(np.float32)
    out = np.zeros((128, 96), np.float32)
    for m in range(32):
        for j in range(3):
            out[:, m * 3 + j] = w[m * 128:(m + 1) * 128, 0, j]
    return out


DERIVED_INPUTS["l2_convw"] = _l2_convw
DERIVED_INPUTS["l2_convb"] = lambda ins: np.ascontiguousarray(
    np.asarray(ins["L2_ffn_conv_b"]).astype(np.float32).reshape(32, 128).T)

L2A_WEIGHTS = {
    "l2_wgT": ((1024, 1536), "float32"),
    "l2_wiT": ((1024, 1536), "float32"),
    "l2_lwgT": ((1536, 1536), "float32"),
    "l2_lwiT": ((1536, 1536), "float32"),
    "l2_woT": ((1536, 1024), "float32"),
    "l2a_cw": ((128, 48), "float32"),
    "l2a_cb": ((128, 12), "float32"),
    "l2a_lam": ((128, 12), "float32"),
    "l2a_bi": ((128, 12), "float32"),
    "l2a_bg": ((128, 12), "float32"),
    "l2_parf": ((128, 1), "float32"),
}
DERIVED_INPUTS["l2_wgT"] = lambda ins: np.ascontiguousarray(
    np.asarray(ins["L2_attn_Wg"]).T.astype(np.float32))
DERIVED_INPUTS["l2_wiT"] = lambda ins: np.ascontiguousarray(
    np.asarray(ins["L2_attn_Wi"]).T.astype(np.float32))
DERIVED_INPUTS["l2_lwgT"] = lambda ins: np.ascontiguousarray(
    np.asarray(ins["L2_attn_lru_Wg"]).T.astype(np.float32))
DERIVED_INPUTS["l2_lwiT"] = lambda ins: np.ascontiguousarray(
    np.asarray(ins["L2_attn_lru_Wi"]).T.astype(np.float32))
DERIVED_INPUTS["l2_woT"] = lambda ins: np.ascontiguousarray(
    np.asarray(ins["L2_attn_Wo"]).T.astype(np.float32))


def _l2a_cw(ins):
    w = np.asarray(ins["L2_attn_conv_w"]).astype(np.float32)
    out = np.zeros((128, 48), np.float32)
    for m in range(12):
        for j in range(4):
            out[:, 4 * m + j] = w[m * 128:(m + 1) * 128, 0, j]
    return out


DERIVED_INPUTS["l2a_cw"] = _l2a_cw
DERIVED_INPUTS["l2a_cb"] = lambda ins: np.ascontiguousarray(
    np.asarray(ins["L2_attn_conv_b"]).astype(np.float32).reshape(12, 128).T)
DERIVED_INPUTS["l2a_lam"] = lambda ins: np.ascontiguousarray(
    np.asarray(ins["L2_attn_lam"]).astype(np.float32).reshape(12, 128).T)
DERIVED_INPUTS["l2a_bi"] = lambda ins: np.ascontiguousarray(
    np.asarray(ins["L2_attn_lru_bi"]).astype(np.float32).reshape(12, 128).T)
DERIVED_INPUTS["l2a_bg"] = lambda ins: np.ascontiguousarray(
    np.asarray(ins["L2_attn_lru_bg"]).astype(np.float32).reshape(12, 128).T)


def _l2_parf(c):
    return np.full((128, 1), float(c % 2), np.float32)


L3F_WEIGHTS = {"l3_bias": ((128, 8), "float32")}
for _c in range(4):
    for _nm in ("t1r", "t2r", "t1i", "t2i"):
        L3F_WEIGHTS[f"l3_{_nm}_{_c}"] = ((512, 512), "float32")
for _c in range(4):
    DERIVED_INPUTS[f"l3_t1r_{_c}"] = (lambda ins, c=_c: _tile_t1(ins["L3_ffn_G1_R"], c))
    DERIVED_INPUTS[f"l3_t2r_{_c}"] = (lambda ins, c=_c: _tile_t2(ins["L3_ffn_G2_R"], c))
    DERIVED_INPUTS[f"l3_t1i_{_c}"] = (lambda ins, c=_c: _tile_t1(ins["L3_ffn_G1_I"], c))
    DERIVED_INPUTS[f"l3_t2i_{_c}"] = (lambda ins, c=_c: _tile_t2(ins["L3_ffn_G2_I"], c))
DERIVED_INPUTS["l3_bias"] = lambda ins: np.ascontiguousarray(
    np.concatenate([np.asarray(ins["L3_ffn_bias_real"]),
                    np.asarray(ins["L3_ffn_bias_imag"])]).astype(np.float32)
    .reshape(8, 128).T)


def run_blocks(key, block_names, wspecs, sspecs, x_shards, weights,
               extra_inputs=None):
    _NONCE[0] += 1
    nonce = _NONCE[0]
    base = _common_inputs(nonce)
    in_maps = []
    for c in range(8):
        m = {k: v for k, v in base[c].items() if k in wspecs or k == "cfg"}
        m["xin"] = x_shards[c]
        for k in wspecs:
            if k in ("rope_c", "rope_s", "p128"):
                m[k] = base[c][k]
            elif extra_inputs and k in extra_inputs:
                m[k] = extra_inputs[k] if not isinstance(extra_inputs[k], list) \
                    else extra_inputs[k][c]
            elif k in DERIVED_INPUTS:
                m[k] = DERIVED_INPUTS[k](weights)
            else:
                wv_ = np.ascontiguousarray(weights[k])
                if k in WEIGHT_TRANSFORMS:
                    wv_ = WEIGHT_TRANSFORMS[k](wv_)
                m[k] = wv_
        in_maps.append(m)
    if key not in _NC_CACHE:
        _NC_CACHE[key] = build_program(block_names, wspecs, sspecs)
    res = run_bass_kernel_spmd(_NC_CACHE[key], in_maps, core_ids=list(range(8)))
    return [res.results[c]["xout"] for c in range(8)]


def _shard_x(x):
    shards = []
    for c in range(8):
        b, p = c // 2, c % 2
        xt = np.ascontiguousarray(x[b].T)
        shards.append(np.ascontiguousarray(xt[:, own_positions(p)]))
    return shards


def _unshard_x(shards):
    out = np.zeros((B, S, D), np.float32)
    for c in range(8):
        b, p = c // 2, c % 2
        out[b, own_positions(p), :] = shards[c].T
    return out


ALL_BLOCKS = ["l0_attn", "l0_ffn", "l1_attn", "l1_ffn",
              "l2_attn", "l2_ffn", "l3_attn", "l3_ffn"]
ALL_WEIGHTS = {}
for _d in (L0A_WEIGHTS, L0F_WEIGHTS, L1A_WEIGHTS, L1F_WEIGHTS,
           L2A_WEIGHTS, L2F_WEIGHTS, L3A_WEIGHTS, L3F_WEIGHTS):
    for _k, _v in _d.items():
        assert _k not in ALL_WEIGHTS or ALL_WEIGHTS[_k] == _v, _k
        ALL_WEIGHTS[_k] = _v


def _all_extra():
    return {"l1_gq": [_l1_gq(c % 2) for c in range(8)],
            "l1_m01": [_l1_m01(c % 2) for c in range(8)],
            "l2_parf": [_l2_parf(c) for c in range(8)],
            "l3_m3": [_l3_m3(c % 2) for c in range(8)]}


def kernel(**inputs):
    x = np.asarray(inputs["L0_attn_x"], np.float32)
    shards = _shard_x(x)
    shards = run_blocks("fused", ALL_BLOCKS, ALL_WEIGHTS, [],
                        shards, inputs, extra_inputs=_all_extra())
    return _unshard_x(shards)


# version 06
# speedup vs baseline: 1.7737x; 1.7737x over previous
import numpy as np
import ml_dtypes
import concourse.bass as bass
import concourse.mybir as mybir
from concourse.bass_utils import run_bass_kernel_spmd

F32 = mybir.dt.float32
BF16 = mybir.dt.bfloat16
I32 = mybir.dt.int32
AF = mybir.ActivationFunctionType
ALU = mybir.AluOpType
AX = mybir.AxisListType
BF = ml_dtypes.bfloat16

D, S, B, T = 1024, 4096, 4, 2048
NCH, CH, DT = 4, 512, 8
LN_EPS = 1e-5
L0_SHIFT = 14.0
L3_SHIFT = 4.0
POOL_COLS = 42240

_NONCE = [int(np.random.default_rng().integers(1, 2 ** 24)) * 64]
L0_HEAD_PERM = [0, 4, 1, 5, 2, 6, 3, 7, 8, 12, 9, 13, 10, 14, 11, 15]


def own_positions(p):
    pos = []
    for k in range(NCH):
        base = 1024 * k + 512 * p
        pos.extend(range(base, base + CH))
    return np.array(pos, dtype=np.int64)



_DVE_OPS = {"tensor_tensor", "tensor_scalar", "tensor_copy", "tensor_add",
            "tensor_sub", "tensor_mul", "tensor_max", "tensor_relu",
            "scalar_tensor_tensor", "tensor_reduce", "reduce_max", "reduce_sum",
            "reciprocal", "memset", "memzero", "select", "tensor_tensor_scan",
            "stream_shuffle", "transpose", "copy_predicated", "bn_stats",
            "bn_aggr", "tensor_scalar_add", "tensor_scalar_mul",
            "tensor_scalar_max", "tensor_scalar_min", "tensor_scalar_sub"}
_ACT_OPS = {"activation", "copy", "add", "mul", "sqrt", "square", "sign",
            "memzero"}


class ChainedEngine:

    def __init__(self, cc, eng, semname, ops):
        self._cc, self._eng, self._sem, self._ops = cc, eng, semname, ops

    def __getattr__(self, name):
        eng = self._eng
        f = getattr(eng, name)
        if name not in self._ops:
            return f
        cc, semname = self._cc, self._sem

        def wrapped(*a, **k):
            cc.wait(eng, cc.tok(semname))
            inst = f(*a, **k)
            cc.inc(inst, semname)
            return inst
        return wrapped


class _ExitStack:
    def __init__(self):
        self._cms = []

    def push(self, cm):
        self._cms.append(cm)

    def close(self):
        for cm in reversed(self._cms):
            cm.__exit__(None, None, None)
        self._cms = []


class CC:
    def __init__(self, nc):
        self.nc = nc
        self._sems = {}
        self._cms = []
        self.flag_word = 0
        self.spin_id = 0

    def sem(self, name):
        if name not in self._sems:
            cm = self.nc.semaphore(name)
            h = cm.__enter__()
            self._cms.append(cm)
            self._sems[name] = [h, 0]
        return self._sems[name]

    def inc(self, inst, name, n=1):
        s = self.sem(name)
        inst.then_inc(s[0], n)
        s[1] += n
        return (name, s[1])

    def tok(self, name):
        return (name, self.sem(name)[1])

    def wait(self, eng, token):
        if token is None:
            return
        name, cnt = token
        if name == "multi":
            for t_ in cnt:
                self.wait(eng, t_)
            return
        if cnt > 0:
            eng.wait_ge(self._sems[name][0], cnt)


def _dma_on(cc, eng, sem, out, in_, slow=False):
    cc.wait(eng, cc.tok(sem))
    return cc.inc(eng.dma_start(out=out, in_=in_,
                                allow_slow_non_contiguous=slow), sem, 16)


def dma_w(cc, out, in_):
    cc._wrr = (getattr(cc, "_wrr", 0) + 1) % 2
    return _dma_on(cc, cc.nc.sync, f"w{cc._wrr}", out, in_)


def dma_g(cc, out, in_, slow=False):
    cc._grr = (getattr(cc, "_grr", 0) + 1) % 4
    return _dma_on(cc, cc.nc.gpsimd, f"g{cc._grr}", out, in_, slow=slow)


def dma_s(cc, out, in_):
    return _dma_on(cc, cc.nc.scalar, "s0", out, in_)


def gbar(cc, eng):
    for j in range(4):
        cc.wait(eng, cc.tok(f"g{j}"))
        cc.wait(eng, cc.tok(f"gx{j}"))


def block_fence(cc):
    nc = cc.nc
    toks = ([cc.tok(nm) for nm in ("v", "a", "p")]
            + [cc.tok(f"g{j}") for j in range(4)]
            + [cc.tok(f"gx{j}") for j in range(4)]
            + [cc.tok("w0"), cc.tok("w1"), cc.tok("s0")])
    for eng in (nc.tensor, nc.vector, nc.scalar, nc.gpsimd, nc.sync):
        for tk in toks:
            cc.wait(eng, tk)


def percore(cc, fn):
    g = cc.nc.gpsimd
    for c in range(8):
        with g.If_eq(cc.me_reg, c):
            fn(c)


def ex_dma(cc, plan, slow=False):
    g = cc.nc.gpsimd
    s = cc.sem("gx")
    n = [None]

    sems = [cc.sem(f"gx{j}") for j in range(4)]
    n0 = [None]

    base = [cc.tok(f"gx{j}")[1] for j in range(4)]

    def body(c):
        pairs = plan(c)
        if n0[0] is None:
            n0[0] = len(pairs)
        assert len(pairs) == n0[0]
        uses = [0, 0, 0, 0]
        for j, (o, i) in enumerate(pairs):
            sj = j % 4
            if uses[sj] > 0:
                g.wait_ge(sems[sj][0], base[sj] + 16 * uses[sj])
            g.dma_start(out=o, in_=i,
                        allow_slow_non_contiguous=slow
                        ).then_inc(sems[sj][0], 16)
            uses[sj] += 1
    for j in range(min(4, 999)):
        cc.wait(g, cc.tok(f"gx{j}"))
    percore(cc, body)
    for j, (h, _) in enumerate(sems):
        pass
    cnt = [0, 0, 0, 0]
    for j in range(n0[0]):
        cnt[j % 4] += 16
    toks = []
    for j in range(4):
        if cnt[j]:
            sems[j][1] += cnt[j]
            toks.append((f"gx{j}", sems[j][1]))
    return ("multi", tuple(toks))


import os as _os
SIM_EXCHANGE = (_os.environ.get("BASS_SIM_FP64") == "1"
                or _os.environ.get("KERNEL_EXCHANGE") == "rdma")


def publish(cc, name, pairs, wait_tokens=()):
    g = cc.nc.gpsimd
    for t_ in wait_tokens:
        cc.wait(g, t_)
    srcs = [p if not isinstance(p, tuple) else p[0] for p in pairs]
    dt_ = srcs[0].dtype
    offs, o = [], 0
    for sp in srcs:
        offs.append(o)
        o += int(sp.shape[-1])
    if SIM_EXCHANGE:
        buf = cc.nc.dram_tensor(f"exb_{name}", [2, 128, o], dt_,
                                kind="Internal").ap()
        from concourse.libnrt import node_relative_to_routing_address
        from concourse.library_config import remote_dma as _RDMA_LIB
        from concourse.library_config import standard as _STD_LIB
        g.load_library(_RDMA_LIB)
        rs = cc.sem("rsem")
        ls = cc.sem("lsem")
        rdy = cc.sem("rdy")
        wtoks = ex_dma(cc, lambda c: [
            (buf[c % 2, :, offs[i]:offs[i] + int(srcs[i].shape[-1])], srcs[i])
            for i in range(len(srcs))])
        cc.wait(g, wtoks)

        def body0(c):
            peer = c ^ 1
            rid, pnc = node_relative_to_routing_address(peer, 8)
            g.remote_sem_update(rdy[0], ls[0], pid=pnc, routing_id=rid,
                                dma_engine_mask=0xFFFF)
        percore(cc, body0)
        g.trigger_dma()
        ls[1] += 16
        rdy[1] += 16
        cc.wait(g, ("rdy", rdy[1]))

        chunks = []
        for i, sp in enumerate(srcs):
            ncols = int(sp.shape[-1])
            for cstart in range(0, ncols, 2048):
                chunks.append((i, cstart, min(2048, ncols - cstart)))
        bounce = cc.ych
        for (i, cstart, n_) in chunks:

            def push(c, i=i, cstart=cstart, n_=n_):
                peer = c ^ 1
                rid, pnc = node_relative_to_routing_address(peer, 8)
                g.remote_dma(bounce[:, 0:n_], srcs[i][:, cstart:cstart + n_],
                             rs[0], ls[0], pid=pnc, routing_id=rid,
                             dma_engine_mask=0xFFFF)
            percore(cc, push)
            g.trigger_dma()
            rs[1] += 16
            ls[1] += 16
            cc.wait(g, ("rsem", rs[1]))
            ct = ex_dma(cc, lambda c, i=i, cstart=cstart, n_=n_: [
                (buf[(c % 2) ^ 1, :, offs[i] + cstart:offs[i] + cstart + n_],
                 bounce[:, 0:n_])])
            cc.wait(g, ct)

            def ack(c):
                peer = c ^ 1
                rid, pnc = node_relative_to_routing_address(peer, 8)
                g.remote_sem_update(rdy[0], ls[0], pid=pnc, routing_id=rid,
                                    dma_engine_mask=0xFFFF)
            percore(cc, ack)
            g.trigger_dma()
            ls[1] += 16
            rdy[1] += 16
            cc.wait(g, ("rdy", rdy[1]))
        g.load_library(_STD_LIB)
        return buf, offs, cc.tok("gx")
    buf = cc.nc.dram_tensor(f"exb_{name}", [2, 128, o], dt_,
                            kind="Internal", addr_space="Shared").ap()
    wtoks = ex_dma(cc, lambda c: [
        (buf[c % 2, :, offs[i]:offs[i] + int(srcs[i].shape[-1])], srcs[i])
        for i in range(len(srcs))])
    word = new_word(cc)
    post_flag(cc, word, [wtoks])
    spin_flag(cc, word)
    return buf, offs, cc.tok("gx")


def exchange(cc, name, pairs, wait_tokens=()):
    buf, offs, rtok = publish(cc, name, [p[1] for p in pairs], wait_tokens)
    g = cc.nc.gpsimd
    cc.wait(g, rtok)
    rdt = ex_dma(cc, lambda c: [
        (pairs[i][0],
         buf[(c % 2) ^ 1, :, offs[i]:offs[i] + int(pairs[i][1].shape[-1])])
        for i in range(len(pairs))])
    return rdt


def post_flag(cc, word, wait_tokens):
    g = cc.nc.gpsimd
    for t in wait_tokens:
        cc.wait(g, t)
    return ex_dma(cc, lambda c: [(cc.sh_flg[c, word:word + 1][None, :],
                                  cc.noncetile[0:1, 0:1])])


def spin_flag(cc, word):
    g = cc.nc.gpsimd
    sid = cc.spin_id
    cc.spin_id += 1
    assert sid < 32
    sp = cc.sem(f"sp{sid}")[0]
    ft = cc.flagtile[0:1, sid:sid + 1]

    def body(c):
        peer = c ^ 1
        with g.register(f"c{sid}_{c}") as creg, g.register(f"f{sid}_{c}") as fv:
            g.reg_mov(creg, 1)
            with g.While(creg):
                g.dma_start(out=ft, in_=cc.sh_flg[peer, word:word + 1][None, :]
                            ).then_inc(sp, 16)
                g.wait_ge(sp, 16)
                g.load(fv, ft)
                g.reg_sub(creg, fv, cc.nonce_reg)
    percore(cc, body)


def new_word(cc):
    w = cc.flag_word
    cc.flag_word += 1
    assert w < 63
    return w


def pv(cc, off, ncols):
    return cc.pool[:, off:off + ncols]


def chain(cc, eng, inst, name):
    tok = cc.inc(inst, name)
    cc.wait(eng, tok)
    return tok


def emit_ln_cols(cc, src_t, dst_t, mode, c0, src_c0=None, src_is_bf16=False,
                 wait_tokens=()):
    nc = cc.nc
    t, v, a = nc.tensor, cc.V, cc.A
    s0 = c0 if src_c0 is None else src_c0
    rows_guard = cc.tok("v")
    for tk in wait_tokens:
        cc.wait(v, tk)
        cc.wait(a, tk)
    mm_toks = {}
    for ti in range(DT):
        src = src_t(ti)[:, s0:s0 + 512]
        if src_is_bf16:
            xb = src
        else:
            xb = cc.xb2[:, (ti % 2) * 512:(ti % 2) * 512 + 512]
            if ti >= 2:
                cc.wait(v, mm_toks[ti - 2])
            v.tensor_copy(xb, src)
        sq = cc.sq2[:, (ti % 2) * 512:(ti % 2) * 512 + 512]
        if ti >= 2:
            cc.wait(a, mm_toks[ti - 2])
        a.activation(sq, src, AF.Square)
        ta = cc.tok("a")
        if not src_is_bf16:
            cc.wait(t, cc.tok("v"))
        cc.wait(t, ta)
        if ti == 0:
            cc.wait(t, rows_guard)
        t.matmul(cc.row0[0:1, :], cc.onesc[:, 0:1], xb,
                 start=(ti == 0), stop=(ti == DT - 1))
        i_ = t.matmul(cc.row1[0:1, :], cc.onesc[:, 0:1], sq,
                      start=(ti == 0), stop=(ti == DT - 1))
        mm_toks[ti] = cc.inc(i_, "p")
    tok_stats = mm_toks[DT - 1]
    mean = cc.rowf[0:1, 0:512]
    var = cc.rowf[0:1, 512:1024]
    rstd = cc.rowf[0:1, 1024:1536]
    cc.wait(v, tok_stats)
    v.tensor_scalar(mean, cc.row0[0:1, :], 1.0 / D, None, ALU.mult)
    v.tensor_tensor(var, mean, mean, ALU.mult)
    v.scalar_tensor_tensor(var, cc.row1[0:1, :], 1.0 / D, var,
                           ALU.mult, ALU.subtract)
    tv = cc.tok("v")
    cc.wait(a, tv)
    a.activation(rstd, var, AF.Sqrt, bias=cc.constt[0:1, 0:1])
    ta = cc.tok("a")
    cc.wait(v, ta)
    v.reciprocal(rstd, rstd)
    v.tensor_copy(cc.rowR[0:1, :], rstd)
    v.scalar_tensor_tensor(mean, mean, -1.0, rstd, ALU.mult, ALU.mult)
    v.tensor_copy(cc.rowM[0:1, :], mean)
    tv = cc.tok("v")
    cc.wait(t, tv)
    t.matmul(cc.big0[:, :], cc.ones1[0:1, :], cc.rowR[0:1, :], start=True, stop=True)
    i_ = t.matmul(cc.big1[:, :], cc.ones1[0:1, :], cc.rowM[0:1, :], start=True, stop=True)
    tok_bc = cc.inc(i_, "p")
    cc.wait(v, tok_bc)
    for tp_ in range(0, DT, 2):
        srcs = [src_t(tp_ + j)[:, s0:s0 + 512] for j in (0, 1)]
        dsts = [dst_t(tp_ + j)[:, c0:c0 + 512] for j in (0, 1)]
        tmps = [cc.lntmp[:, 0:512], cc.lntmp2[:, 0:512]]
        v.tensor_tensor(tmps[0], srcs[0], cc.big0[:, :], ALU.mult)
        v.tensor_tensor(tmps[1], srcs[1], cc.big0[:, :], ALU.mult)
        if mode == "set":
            v.tensor_tensor(dsts[0], tmps[0], cc.big1[:, :], ALU.add)
            v.tensor_tensor(dsts[1], tmps[1], cc.big1[:, :], ALU.add)
        else:
            v.tensor_tensor(tmps[0], tmps[0], cc.big1[:, :], ALU.add)
            v.tensor_tensor(tmps[1], tmps[1], cc.big1[:, :], ALU.add)
            v.tensor_tensor(dsts[0], dsts[0], tmps[0], ALU.add)
            v.tensor_tensor(dsts[1], dsts[1], tmps[1], ALU.add)
    return cc.tok("v")


def xres_t(cc, i):
    return cc.xres[:, i * T:(i + 1) * T]


def xn_t(cc, i):
    return cc.xn[:, i * T:(i + 1) * T]


def emit_ln_in(cc, wait_tokens=()):
    tk = None
    for c0 in range(0, T, 512):
        tk = emit_ln_cols(cc, lambda i: xres_t(cc, i), lambda i: xn_t(cc, i),
                          "set", c0, wait_tokens=wait_tokens)
    return tk


def emit_l0_attn(cc, W):
    import os
    PHASE = int(os.environ.get("L0A_PHASE", "9"))
    nc = cc.nc
    t, v, a, g = nc.tensor, cc.V, cc.A, nc.gpsimd

    tok_xn = emit_ln_in(cc)
    if PHASE <= 0:
        return tok_xn

    wq = pv(cc, 0, 8192)
    Ctab = pv(cc, 8192, T)
    Stab = pv(cc, 10240, T)
    P128 = pv(cc, 12288, 128)
    wk = pv(cc, 12416, 2048)
    wv = pv(cc, 14464, 2048)
    Ksb = pv(cc, 16512, 2 * T)
    Vsb = pv(cc, 20608, 16 * 260)
    Qf = pv(cc, 24768, 8 * T)

    def dma_wt(out, in_):
        return cc.inc(cc.nc.sync.dma_start(out=out, in_=in_), "wt", 16)
    dma_wt(Ctab, W["rope_c"][:, :])
    dma_wt(Stab, W["rope_s"][:, :])
    tok_tab3 = dma_wt(P128, W["p128"][:, :])
    tok_tab = [tok_tab3, tok_tab3, tok_tab3]
    tokw = []
    for dt_ in range(DT):
        tokw.append(dma_g(cc, wq[:, dt_ * 1024:(dt_ + 1) * 1024],
                          W["L0_attn_w_q"][dt_ * 128:(dt_ + 1) * 128, :]))
        tokw.append(dma_g(cc, wk[:, dt_ * 256:(dt_ + 1) * 256],
                          W["L0_attn_w_k"][dt_ * 128:(dt_ + 1) * 128, :]))
        tokw.append(dma_g(cc, wv[:, dt_ * 256:(dt_ + 1) * 256],
                          W["L0_attn_w_v"][dt_ * 128:(dt_ + 1) * 128, :]))
    tok_wlast = tokw[-1]

    v.memset(Vsb, 1.0)
    tok_vset = cc.tok("v")

    def rope_tile(dst, c0):
        cc.wait(a, cc.tok("p"))
        a.activation(cc.qswp[:, 0:512], cc.big2[:, :], AF.Copy)
        ta = cc.tok("a")
        cc.wait(t, ta)
        cc.wait(t, tok_tab[2])
        tp = cc.inc(t.matmul(cc.big3[:, :], P128, cc.qswp[:, 0:512],
                             start=True, stop=True), "p")
        cc.wait(v, tp)
        cc.wait(v, tok_tab[0])
        cc.wait(v, tok_tab[1])
        v.tensor_tensor(cc.lntmp[:, 0:512], cc.big2[:, :], Ctab[:, c0:c0 + 512],
                        ALU.mult)
        v.tensor_tensor(cc.lntmp2[:, 0:512], cc.big3[:, :],
                        Stab[:, c0:c0 + 512], ALU.mult)
        v.tensor_tensor(dst, cc.lntmp[:, 0:512], cc.lntmp2[:, 0:512], ALU.add)
        return cc.tok("v")

    tok_k = None
    for kt in range(2):
        for c0 in range(0, T, 512):
            for dt_ in range(DT):
                cc.wait(t, tok_wlast)
                cc.wait(t, tok_xn)
                if dt_ == 0:
                    cc.wait(t, tok_k)
                i_ = t.matmul(cc.big2[:, :],
                              wk[:, dt_ * 256 + kt * 128:dt_ * 256 + (kt + 1) * 128],
                              xn_t(cc, dt_)[:, c0:c0 + 512],
                              start=(dt_ == 0), stop=(dt_ == DT - 1))
            cc.inc(i_, "p")
            tok_k = rope_tile(Ksb[:, kt * T + c0:kt * T + c0 + 512], c0)

    tok_vv = None
    for tt in range(16):
        for dt_ in range(DT):
            cc.wait(t, tok_xn)
            cc.wait(t, tok_wlast)
            if dt_ == 0:
                cc.wait(t, tok_k)
                cc.wait(t, tok_vv)
            i_ = t.matmul(cc.big2[:, 0:256],
                          xn_t(cc, dt_)[:, tt * 128:(tt + 1) * 128],
                          wv[:, dt_ * 256:(dt_ + 1) * 256],
                          start=(dt_ == 0), stop=(dt_ == DT - 1))
        tp = cc.inc(i_, "p")
        cc.wait(a, tp)
        cc.wait(a, tok_vset)
        for kh in range(4):
            a.activation(Vsb[:, tt * 260 + kh * 65:tt * 260 + kh * 65 + 64],
                         cc.big2[:, kh * 64:(kh + 1) * 64], AF.Copy)
        tok_vv = cc.tok("a")

    tok_q = None
    for qt in range(DT):
        for c0 in range(0, T, 512):
            for dt_ in range(DT):
                cc.wait(t, tok_wlast)
                cc.wait(t, tok_xn)
                if dt_ == 0:
                    cc.wait(t, tok_q if tok_q is not None else tok_vv)
                i_ = t.matmul(cc.big2[:, :],
                              wq[:, dt_ * 1024 + qt * 128:dt_ * 1024 + (qt + 1) * 128],
                              xn_t(cc, dt_)[:, c0:c0 + 512],
                              start=(dt_ == 0), stop=(dt_ == DT - 1))
            cc.inc(i_, "p")
            tok_q = rope_tile(Qf[:, qt * T + c0:qt * T + c0 + 512], c0)

    if PHASE <= 1:
        return tok_q
    cc.wait(g, tok_q)
    Kpe = pv(cc, 0, 2 * T)
    Vpe = pv(cc, 4096, 16 * 260)
    woring = pv(cc, 8256, 2048)
    Pt0 = pv(cc, 10304, 512)
    Pt1 = pv(cc, 10816, 512)
    AVb = pv(cc, 11328, 8 * 512)
    ychunk = cc.ych
    tok_rd = exchange(cc, "l0kv", [
        (Kpe[:, 0:T], Ksb[:, 0:T]),
        (Kpe[:, T:2 * T], Ksb[:, T:2 * T]),
        (Vpe, Vsb),
    ], wait_tokens=(tok_k, tok_vv))

    if PHASE <= 2:
        return tok_rd
    ktile_src = [("own", k) for k in range(16)] + [("peer", k) for k in range(16)]
    prev_ln = None
    slab_used = [None, None]
    evac_toks = [None, None]
    for qc in range(NCH):
        c0 = qc * CH
        prev_evac = None
        for qs in range(16):
            qh = L0_HEAD_PERM[qs]
            kh = qh // 4
            qtile, qb = qs // 2, 64 * (qs % 2)
            ktt, kb = kh // 2, 64 * (kh % 2)
            assert qb == kb
            exp_toks, av_toks = {}, {}
            for ki, (srcid, kt) in enumerate(ktile_src):
                K_ = Ksb if srcid == "own" else Kpe
                V_ = Vsb if srcid == "own" else Vpe
                ps_s = cc.big0 if ki % 2 == 0 else cc.big1
                pt = Pt0 if ki % 2 == 0 else Pt1
                cc.wait(t, tok_q)
                if srcid == "peer":
                    cc.wait(t, tok_rd)
                if ki >= 2:
                    cc.wait(t, exp_toks[ki - 2])
                if ki == 0 and prev_evac is not None:
                    cc.wait(t, prev_evac)
                if ki == 0 and prev_ln is not None:
                    cc.wait(t, prev_ln)
                i_ = t.matmul(ps_s[:, :],
                              K_[kb:kb + 64, ktt * T + kt * 128:ktt * T + (kt + 1) * 128],
                              Qf[qb:qb + 64, qtile * T + c0:qtile * T + c0 + 512],
                              start=True, stop=True)
                ts_ = cc.inc(i_, "p")
                cc.wait(a, ts_)
                if ki >= 2:
                    cc.wait(a, av_toks[ki - 2])
                a.activation(pt, ps_s[:, :], AF.Exp, scale=0.125,
                             bias=cc.constt[:, 1:2])
                exp_toks[ki] = cc.tok("a")
                cc.wait(t, exp_toks[ki])
                i_ = t.matmul(cc.big2[0:65, :],
                              V_[:, kt * 260 + kh * 65:kt * 260 + kh * 65 + 65],
                              pt, start=(ki == 0), stop=(ki == 31))
                av_toks[ki] = cc.inc(i_, "p")
            cc.wait(v, av_toks[31])
            rec = cc.rowf[0:1, 0:512]
            v.reciprocal(rec, cc.big2[64:65, :])
            v.tensor_copy(cc.rowR[0:1, :], rec)
            tv = cc.tok("v")
            cc.wait(t, tv)
            tb = cc.inc(t.matmul(cc.big3[:, :], cc.ones1[0:1, :],
                                 cc.rowR[0:1, :], start=True, stop=True), "p")
            cc.wait(a, av_toks[31])
            cc.wait(a, prev_evac)
            a.activation(cc.qswp[0:64, 0:512], cc.big2[0:64, :], AF.Copy)
            ta2 = cc.tok("a")
            cc.wait(v, tb)
            cc.wait(v, ta2)
            v.tensor_tensor(AVb[qb:qb + 64, qtile * 512:(qtile + 1) * 512],
                            cc.qswp[0:64, 0:512], cc.big3[0:64, :], ALU.mult)
            prev_evac = cc.tok("v")
        tok_av = cc.tok("v")
        tok_y = None
        for ot in range(DT):
            slab = woring[:, (ot % 2) * 1024:(ot % 2) * 1024 + 1024]
            cc.wait(g, slab_used[ot % 2])
            rsem = "r0" if ot % 2 == 0 else "r1"
            tok_l = None
            for dt_ in range(DT):
                tok_l = cc.inc(g.dma_start(
                    out=slab[:, dt_ * 128:(dt_ + 1) * 128],
                    in_=W["L0_attn_w_o"][dt_ * 128:(dt_ + 1) * 128,
                                         ot * 128:(ot + 1) * 128]), rsem, 16)
            ps_o = cc.big4 if ot % 2 == 0 else cc.big5
            for dvt in range(DT):
                cc.wait(t, tok_av)
                cc.wait(t, tok_l)
                if dvt == 0:
                    cc.wait(t, evac_toks[ot % 2])
                if prev_ln is not None:
                    cc.wait(t, prev_ln)
                i_ = t.matmul(ps_o[:, :], slab[:, dvt * 128:(dvt + 1) * 128],
                              AVb[:, dvt * 512:(dvt + 1) * 512],
                              start=(dvt == 0), stop=(dvt == DT - 1))
            tp = cc.inc(i_, "p")
            slab_used[ot % 2] = tp
            cc.wait(a, tp)
            cc.wait(a, prev_ln)
            a.activation(ychunk[:, ot * 512:(ot + 1) * 512], ps_o[:, :], AF.Copy)
            tok_y = cc.tok("a")
            evac_toks[ot % 2] = tok_y
        prev_ln = emit_ln_cols(cc, lambda i: ychunk[:, i * 512:(i + 1) * 512],
                               lambda i: xres_t(cc, i), "add", c0, src_c0=0,
                               src_is_bf16=True, wait_tokens=(tok_y,))
    return prev_ln




def emit_l1_ffn(cc, W):
    nc = cc.nc
    t, v, a, g = nc.tensor, cc.V, cc.A, nc.gpsimd
    tok_xn = emit_ln_in(cc)
    HT = 24
    wgT = pv(cc, 0, 24576)
    wuring = pv(cc, 24576, 2048)
    wdring = pv(cc, 26624, 2048)
    h = pv(cc, 28672, HT * 512)
    tokw = None
    for dt_ in range(DT):
        tokw = dma_g(cc, wgT[:, dt_ * 3072:(dt_ + 1) * 3072],
                     W["l1_wgT"][dt_ * 128:(dt_ + 1) * 128, :])
    tok_wg = tokw

    prev_ln = None
    slabu = [None, None]
    slabd = [None, None]
    psev = [None, None]
    dvev = [None, None]
    yevac = [None, None]
    for qc in range(NCH):
        c0 = qc * CH
        for m in range(HT):
            par = m % 2
            psg = cc.big0 if par == 0 else cc.big1
            psu = cc.big2 if par == 0 else cc.big3
            tA = cc.lntmp[:, 0:512] if par == 0 else cc.xb2[:, 0:512]
            tB = cc.lntmp2[:, 0:512] if par == 0 else cc.sq2[:, 0:512]
            slab = wuring[:, par * 1024:par * 1024 + 1024]
            cc.wait(g, slabu[par])
            tok_u = dma_g(cc, slab.rearrange("p (k m) -> p k m", k=DT),
                          W["l1_wuT"][:, m * 128:(m + 1) * 128]
                          .rearrange("(k p) m -> p k m", p=128))
            i_ = None
            for dt_ in range(DT):
                cc.wait(t, tok_wg)
                cc.wait(t, tok_xn)
                if dt_ == 0:
                    cc.wait(t, psev[par])
                    if prev_ln is not None:
                        cc.wait(t, prev_ln)
                t.matmul(psg[:, :],
                         wgT[:, dt_ * 3072 + m * 128:dt_ * 3072 + (m + 1) * 128],
                         xn_t(cc, dt_)[:, c0:c0 + 512],
                         start=(dt_ == 0), stop=(dt_ == DT - 1))
                cc.wait(t, tok_u)
                i_ = t.matmul(psu[:, :], slab[:, dt_ * 128:(dt_ + 1) * 128],
                              xn_t(cc, dt_)[:, c0:c0 + 512],
                              start=(dt_ == 0), stop=(dt_ == DT - 1))
            tokp = cc.inc(i_, "p")
            slabu[par] = tokp
            cc.wait(a, tokp)
            cc.wait(a, dvev[par])
            a.activation(tA, psg[:, :], AF.Silu, scale=1.0 / 127.0)
            a.activation(tB, psu[:, :], AF.Copy, scale=1.0 / 127.0)
            psev[par] = cc.tok("a")
            cc.wait(v, psev[par])
            v.tensor_tensor(h[:, m * 512:(m + 1) * 512], tA, tB, ALU.mult)
            dvev[par] = cc.tok("v")
        tok_h = cc.tok("v")
        tok_y = None
        for ot in range(DT):
            ps_o = cc.big4 if ot % 2 == 0 else cc.big5
            slab = wdring[:, (ot % 2) * 1024:(ot % 2) * 1024 + 1024]
            i_ = None
            for sub in range(3):
                cc.wait(g, slabd[ot % 2])
                tok_d = dma_g(cc, slab.rearrange("p (k m) -> p k m", k=8),
                              W["l1_wdT"][sub * 1024:(sub + 1) * 1024,
                                          ot * 128:(ot + 1) * 128]
                              .rearrange("(k p) m -> p k m", p=128))
                for kk in range(8):
                    k = sub * 8 + kk
                    cc.wait(t, tok_d)
                    cc.wait(t, tok_h)
                    if k == 0:
                        cc.wait(t, yevac[ot % 2])
                    i_ = t.matmul(ps_o[:, :], slab[:, kk * 128:(kk + 1) * 128],
                                  h[:, k * 512:(k + 1) * 512],
                                  start=(k == 0), stop=(k == HT - 1))
                slabd[ot % 2] = cc.inc(i_, "p")
            tokp = slabd[ot % 2]
            cc.wait(a, tokp)
            if prev_ln is not None:
                cc.wait(a, prev_ln)
            a.activation(cc.ych[:, ot * 512:(ot + 1) * 512], ps_o[:, :],
                         AF.Copy, scale=1.0 / 127.0)
            tok_y = cc.tok("a")
            yevac[ot % 2] = tok_y
        prev_ln = emit_ln_cols(cc, lambda i: cc.ych[:, i * 512:(i + 1) * 512],
                               lambda i: xres_t(cc, i), "add", c0, src_c0=0,
                               src_is_bf16=True, wait_tokens=(tok_y,))
    return prev_ln



def emit_l3_ffn(cc, W):
    nc = cc.nc
    t, v, a, g = nc.tensor, cc.V, cc.A, nc.gpsimd
    tok_xn = emit_ln_in(cc)
    WcT = pv(cc, 0, 8192)
    stag = pv(cc, 8192, 2048)
    biascol = pv(cc, 10240, 16)
    accA = cc.lntmp[:, 0:512]
    accB = cc.lntmp2[:, 0:512]
    tok_b = dma_g(cc, cc.constt[:, 8:16], W["l3_bias"][:, :])
    for part in range(2):
        pre = "l3_t1r" if part == 0 else "l3_t1i"
        pre2 = "l3_t2r" if part == 0 else "l3_t2i"
        for rt in range(4):
            acc = accA
            for c4 in range(4):
                s1 = stag[:, 0:512]
                s2 = stag[:, 512:1024]
                tokl = None
                cc.wait(g, cc.tok("v"))
                dma_g(cc, s1, W[f"{pre}_{c4}"][rt * 128:(rt + 1) * 128, :])
                tokl = dma_g(cc, s2, W[f"{pre2}_{c4}"][rt * 128:(rt + 1) * 128, :])
                cc.wait(v, tokl)
                if c4 == 0:
                    v.tensor_tensor(acc, s1, s2, ALU.mult)
                else:
                    v.tensor_tensor(accB, s1, s2, ALU.mult)
                    v.tensor_tensor(acc, acc, accB, ALU.add)
            if part == 0:
                v.tensor_copy(WcT[:, rt * 1024:rt * 1024 + 512], acc)
                v.tensor_copy(WcT[:, (rt + 4) * 1024 + 512:(rt + 5) * 1024], acc)
            else:
                v.tensor_copy(WcT[:, rt * 1024 + 512:(rt + 1) * 1024], acc)
                v.tensor_scalar(WcT[:, (rt + 4) * 1024:(rt + 4) * 1024 + 512],
                                acc, -1.0, None, ALU.mult)
    tok_w = cc.tok("v")

    prev_ln = None
    yevac = [None, None]
    for qc in range(NCH):
        c0 = qc * CH
        tok_y = None
        for ot in range(DT):
            ps_o = cc.big4 if ot % 2 == 0 else cc.big5
            i_ = None
            for dt_ in range(DT):
                cc.wait(t, tok_w)
                cc.wait(t, tok_xn)
                if dt_ == 0:
                    cc.wait(t, yevac[ot % 2])
                    if prev_ln is not None:
                        cc.wait(t, prev_ln)
                i_ = t.matmul(ps_o[:, :],
                              WcT[:, dt_ * 1024 + ot * 128:dt_ * 1024 + (ot + 1) * 128],
                              xn_t(cc, dt_)[:, c0:c0 + 512],
                              start=(dt_ == 0), stop=(dt_ == DT - 1))
            tokp = cc.inc(i_, "p")
            cc.wait(v, tokp)
            cc.wait(v, tok_b)
            if prev_ln is not None:
                cc.wait(v, prev_ln)
            v.tensor_scalar(cc.ych[:, ot * 512:(ot + 1) * 512], ps_o[:, :],
                            cc.constt[:, 8 + ot:9 + ot], None, ALU.add)
            tok_y = cc.tok("v")
            yevac[ot % 2] = tok_y
        prev_ln = emit_ln_cols(cc, lambda i: cc.ych[:, i * 512:(i + 1) * 512],
                               lambda i: xres_t(cc, i), "add", c0, src_c0=0,
                               src_is_bf16=True, wait_tokens=(tok_y,))
    return prev_ln


def emit_l0_ffn(cc, W):
    nc = cc.nc
    t, v, a, g = nc.tensor, cc.V, cc.A, nc.gpsimd
    tok_xn = emit_ln_in(cc)
    HT = 32
    ha = pv(cc, 0, HT * 512)
    hb = pv(cc, 16384, HT * 512)
    stream = pv(cc, 32768, 8192)
    tok_s = dma_g(cc, cc.scl[:, :], W["l0_scales"][:, :])

    prev_ln = None
    psev = [None, None]
    dvev = [None, None]
    yevac = [None, None]
    for qc in range(NCH):
        c0 = qc * CH
        ring_tok = [cc.tok("p"), cc.tok("p")]
        for m in range(HT):
            par = m % 2
            psg = cc.big0 if par == 0 else cc.big1
            psu = cc.big2 if par == 0 else cc.big3
            tA = cc.lntmp[:, 0:512] if par == 0 else cc.xb2[:, 0:512]
            tB = cc.lntmp2[:, 0:512] if par == 0 else cc.sq2[:, 0:512]
            s1m = cc.scl[:, m:m + 1]
            s2m = cc.scl[:, 32 + m:33 + m]
            w1s = stream[:, par * 2048:par * 2048 + 1024]
            w2s = stream[:, par * 2048 + 1024:par * 2048 + 2048]
            cc.wait(g, ring_tok[par])
            dma_g(cc, w1s.rearrange("p (k m) -> p k m", k=DT),
                  W["l0_w1T"][:, m * 128:(m + 1) * 128]
                  .rearrange("(k p) m -> p k m", p=128))
            tok_l = dma_g(cc, w2s.rearrange("p (k m) -> p k m", k=DT),
                          W["l0_w2T"][:, m * 128:(m + 1) * 128]
                          .rearrange("(k p) m -> p k m", p=128))
            i_ = None
            for dt_ in range(DT):
                cc.wait(t, tok_l)
                cc.wait(t, tok_xn)
                if dt_ == 0:
                    cc.wait(t, psev[par])
                    if prev_ln is not None:
                        cc.wait(t, prev_ln)
                t.matmul(psg[:, :], w1s[:, dt_ * 128:(dt_ + 1) * 128],
                         xn_t(cc, dt_)[:, c0:c0 + 512],
                         start=(dt_ == 0), stop=(dt_ == DT - 1))
                i_ = t.matmul(psu[:, :], w2s[:, dt_ * 128:(dt_ + 1) * 128],
                              xn_t(cc, dt_)[:, c0:c0 + 512],
                              start=(dt_ == 0), stop=(dt_ == DT - 1))
            tokp = cc.inc(i_, "p")
            ring_tok[par] = tokp
            cc.wait(a, tokp)
            cc.wait(a, tok_s)
            cc.wait(a, dvev[par])
            a.activation(tA, psg[:, :], AF.Silu, scale=s1m)
            a.activation(tB, psu[:, :], AF.Copy, scale=s2m)
            psev[par] = cc.tok("a")
            cc.wait(v, psev[par])
            v.tensor_tensor(ha[:, m * 512:(m + 1) * 512], tA, tB, ALU.mult)
            dvev[par] = cc.tok("v")
        for stage, (wname, scoff, src, dst) in enumerate(
                [("l0_h1T", 64, ha, hb), ("l0_h2T", 96, hb, ha)]):
            tok_src = cc.tok("v") if stage == 0 else cc.tok("a")
            ring_tok = [cc.tok("p"), cc.tok("p")]
            for m in range(HT):
                par = m % 2
                ps = cc.big0 if par == 0 else cc.big1
                slab = stream[:, par * 4096:par * 4096 + 4096]
                cc.wait(g, ring_tok[par])
                tok_l = dma_g(cc, slab.rearrange("p (k m) -> p k m", k=HT),
                              W[wname][:, m * 128:(m + 1) * 128]
                              .rearrange("(k p) m -> p k m", p=128))
                i_ = None
                for k in range(HT):
                    cc.wait(t, tok_l)
                    cc.wait(t, tok_src)
                    if k == 0:
                        cc.wait(t, psev[par])
                    i_ = t.matmul(ps[:, :], slab[:, k * 128:(k + 1) * 128],
                                  src[:, k * 512:(k + 1) * 512],
                                  start=(k == 0), stop=(k == HT - 1))
                tokp = cc.inc(i_, "p")
                ring_tok[par] = tokp
                cc.wait(a, tokp)
                cc.wait(a, cc.tok("v"))
                a.activation(dst[:, m * 512:(m + 1) * 512], ps[:, :], AF.Gelu,
                             scale=cc.scl[:, scoff + m:scoff + m + 1])
                psev[par] = cc.tok("a")
        tok_hc = cc.tok("a")
        ring_tok = [cc.tok("p"), cc.tok("p")]
        tok_y = None
        for ot in range(DT):
            par = ot % 2
            ps_o = cc.big4 if par == 0 else cc.big5
            slab = stream[:, par * 4096:par * 4096 + 4096]
            cc.wait(g, ring_tok[par])
            tok_l = dma_g(cc, slab.rearrange("p (k m) -> p k m", k=HT),
                          W["l0_w3T"][:, ot * 128:(ot + 1) * 128]
                          .rearrange("(k p) m -> p k m", p=128))
            i_ = None
            for k in range(HT):
                cc.wait(t, tok_l)
                cc.wait(t, tok_hc)
                if k == 0:
                    cc.wait(t, yevac[par])
                i_ = t.matmul(ps_o[:, :], slab[:, k * 128:(k + 1) * 128],
                              ha[:, k * 512:(k + 1) * 512],
                              start=(k == 0), stop=(k == HT - 1))
            tokp = cc.inc(i_, "p")
            ring_tok[par] = tokp
            cc.wait(a, tokp)
            if prev_ln is not None:
                cc.wait(a, prev_ln)
            a.activation(cc.ych[:, ot * 512:(ot + 1) * 512], ps_o[:, :],
                         AF.Copy, scale=cc.scl[:, 128 + ot:129 + ot])
            tok_y = cc.tok("a")
            yevac[par] = tok_y
        prev_ln = emit_ln_cols(cc, lambda i: cc.ych[:, i * 512:(i + 1) * 512],
                               lambda i: xres_t(cc, i), "add", c0, src_c0=0,
                               src_is_bf16=True, wait_tokens=(tok_y,))
    return prev_ln


def emit_l3_attn(cc, W):
    nc = cc.nc
    t, v, a, g = nc.tensor, cc.V, cc.A, nc.gpsimd
    p_par = None
    tok_xn = emit_ln_in(cc)
    Ksb = pv(cc, 0, 8 * T)
    Vsb = pv(cc, 16384, 16 * 1040)
    wq2 = pv(cc, 40960, 64)
    wk2 = pv(cc, 41024, 64)
    wv2 = pv(cc, 41088, 64)
    m3c = pv(cc, 33024, 1920)
    tok_w = [dma_g(cc, wq2, W["l3_wq2"][:, :]),
             dma_g(cc, wk2, W["l3_wk2"][:, :]),
             dma_g(cc, wv2, W["l3_wv2"][:, :]),
             dma_w(cc, m3c, W["l3_m3"][:, :])]
    v.memset(Vsb, 1.0)
    tok_vset = cc.tok("v")

    tok_k = None
    for h in range(16):
        qb = 64 * (h % 2)
        ht = h // 2
        for c0 in range(0, T, 512):
            ps = cc.big0 if (c0 // 512) % 2 == 0 else cc.big1
            cc.wait(t, tok_w[1])
            cc.wait(t, tok_xn)
            cc.wait(t, cc.tok("a"))
            i_ = t.matmul(ps[0:64, :], wk2[qb:qb + 64, :],
                          xn_t(cc, ht)[qb:qb + 64, c0:c0 + 512],
                          start=True, stop=True)
            tp = cc.inc(i_, "p")
            cc.wait(a, tp)
            a.activation(Ksb[:, ht * T + c0:ht * T + c0 + 512][qb:qb + 64, :],
                         ps[0:64, :], AF.Copy)
            tok_k = cc.tok("a")
    tok_vv = None
    for h in range(16):
        qb = 64 * (h % 2)
        ht = h // 2
        for tt in range(16):
            ps = cc.big2 if tt % 2 == 0 else cc.big3
            cc.wait(t, tok_w[2])
            cc.wait(t, tok_xn)
            cc.wait(t, cc.tok("a"))
            i_ = t.matmul(ps[:, 0:64],
                          xn_t(cc, ht)[qb:qb + 64, tt * 128:(tt + 1) * 128],
                          wv2[qb:qb + 64, :], start=True, stop=True)
            tp = cc.inc(i_, "p")
            cc.wait(a, tp)
            cc.wait(a, tok_vset)
            a.activation(Vsb[:, tt * 1040 + h * 65:tt * 1040 + h * 65 + 64],
                         ps[:, 0:64], AF.Copy)
            tok_vv = cc.tok("a")

    srcs = [Ksb[:, i * T:(i + 1) * T] for i in range(8)] + \
           [Vsb[:, i * 1040:(i + 1) * 1040] for i in range(16)]
    buf, offs, rtok = publish(cc, "l3kv", srcs, wait_tokens=(tok_k, tok_vv))
    koff = offs[:8]
    voff = offs[8:]

    Qf = pv(cc, 0, 8 * T)
    cc.wait(a, rtok)
    tok_q = None
    for h in range(16):
        qb = 64 * (h % 2)
        ht = h // 2
        for c0 in range(0, T, 512):
            ps = cc.big0 if (c0 // 512) % 2 == 0 else cc.big1
            cc.wait(t, tok_w[0])
            cc.wait(t, cc.tok("a"))
            i_ = t.matmul(ps[0:64, :], wq2[qb:qb + 64, :],
                          xn_t(cc, ht)[qb:qb + 64, c0:c0 + 512],
                          start=True, stop=True)
            tp = cc.inc(i_, "p")
            cc.wait(a, tp)
            a.activation(Qf[:, ht * T + c0:ht * T + c0 + 512][qb:qb + 64, :],
                         ps[0:64, :], AF.Copy)
            tok_q = cc.tok("a")

    kring = pv(cc, 16384, 1024)
    vring = pv(cc, 17408, 520)
    AVb = pv(cc, 17928, 4096)
    woring = pv(cc, 22024, 2048)
    Pt0 = pv(cc, 24072, 512)
    Pt1 = pv(cc, 24584, 512)

    prev_ln = None
    slab_used = [None, None]
    yevac = [None, None]
    kused = [None, None]
    for qc in range(NCH):
        c0 = qc * CH
        prev_evac = None
        for qh in range(16):
            qb, qtile = 64 * (qh % 2), qh // 2
            nb = 2 * qc + 2
            exp_toks, av_toks = {}, {}
            ki = 0
            nkt = nb * 4
            for b in range(nb):
                kr = kring[:, (b % 2) * 512:(b % 2) * 512 + 512]
                vr = vring[:, (b % 2) * 260:(b % 2) * 260 + 260]
                cc.wait(g, kused[b % 2])
                def plan(c, b=b, qtile=qtile, qh=qh, kr=kr, vr=vr):
                    par = c % 2
                    w_ = b % 2
                    kk = b // 2
                    outs = [(kr, buf[w_, :, koff[qtile] + kk * 512:
                                    koff[qtile] + kk * 512 + 512])]
                    for j in range(4):
                        tt = kk * 4 + j
                        outs.append((vr[:, j * 65:(j + 1) * 65],
                                     buf[w_, :, voff[tt] + qh * 65:
                                         voff[tt] + qh * 65 + 65]))
                    return outs
                tok_kv = ex_dma(cc, plan)
                for j in range(4):
                    ps_s = cc.big0 if ki % 2 == 0 else cc.big1
                    pt = Pt0 if ki % 2 == 0 else Pt1
                    cc.wait(t, tok_q)
                    cc.wait(t, tok_kv)
                    if ki >= 2:
                        cc.wait(t, exp_toks[ki - 2])
                    if ki == 0:
                        if prev_evac is not None:
                            cc.wait(t, prev_evac)
                        if prev_ln is not None:
                            cc.wait(t, prev_ln)
                    i_ = t.matmul(ps_s[:, :], kr[qb:qb + 64, j * 128:(j + 1) * 128],
                                  Qf[qb:qb + 64, qtile * T + c0:qtile * T + c0 + 512],
                                  start=True, stop=True)
                    ts_ = cc.inc(i_, "p")
                    if b >= 2 * qc:
                        dd = (b - 2 * qc) * 512 + j * 128
                        cc.wait(v, ts_)
                        cc.wait(v, tok_w[3])
                        v.tensor_tensor(ps_s[:, :], ps_s[:, :],
                                        m3c[:, 1408 - dd:1408 - dd + 512],
                                        ALU.add)
                        cc.wait(a, cc.tok("v"))
                    cc.wait(a, ts_)
                    if ki >= 2:
                        cc.wait(a, av_toks[ki - 2])
                    a.activation(pt, ps_s[:, :], AF.Exp, scale=0.125,
                                 bias=cc.constt[:, 2:3])
                    exp_toks[ki] = cc.tok("a")
                    cc.wait(t, exp_toks[ki])
                    i_ = t.matmul(cc.big2[0:65, :], vr[:, j * 65:(j + 1) * 65],
                                  pt, start=(ki == 0), stop=(ki == nkt - 1))
                    av_toks[ki] = cc.inc(i_, "p")
                    ki += 1
                kused[b % 2] = av_toks[ki - 1]
            cc.wait(v, av_toks[nkt - 1])
            rec = cc.rowf[0:1, 0:512]
            v.reciprocal(rec, cc.big2[64:65, :])
            v.tensor_copy(cc.rowR[0:1, :], rec)
            tv = cc.tok("v")
            cc.wait(t, tv)
            tb = cc.inc(t.matmul(cc.big3[:, :], cc.ones1[0:1, :],
                                 cc.rowR[0:1, :], start=True, stop=True), "p")
            cc.wait(a, av_toks[nkt - 1])
            cc.wait(a, prev_evac)
            a.activation(cc.qswp[0:64, 0:512], cc.big2[0:64, :], AF.Copy)
            ta2 = cc.tok("a")
            cc.wait(v, tb)
            cc.wait(v, ta2)
            v.tensor_tensor(AVb[qb:qb + 64, qtile * 512:(qtile + 1) * 512],
                            cc.qswp[0:64, 0:512], cc.big3[0:64, :], ALU.mult)
            prev_evac = cc.tok("v")
        tok_av = cc.tok("v")
        tok_y = None
        for ot in range(DT):
            slab = woring[:, (ot % 2) * 1024:(ot % 2) * 1024 + 1024]
            cc.wait(g, slab_used[ot % 2])
            tok_l = None
            for dt_ in range(DT):
                tok_l = dma_g(cc, slab[:, dt_ * 128:(dt_ + 1) * 128],
                              W["l3_woT"][dt_ * 128:(dt_ + 1) * 128,
                                          ot * 128:(ot + 1) * 128])
            ps_o = cc.big4 if ot % 2 == 0 else cc.big5
            i_ = None
            for dvt in range(DT):
                cc.wait(t, tok_av)
                cc.wait(t, tok_l)
                if dvt == 0:
                    cc.wait(t, yevac[ot % 2])
                if prev_ln is not None:
                    cc.wait(t, prev_ln)
                i_ = t.matmul(ps_o[:, :], slab[:, dvt * 128:(dvt + 1) * 128],
                              AVb[:, dvt * 512:(dvt + 1) * 512],
                              start=(dvt == 0), stop=(dvt == DT - 1))
            tp = cc.inc(i_, "p")
            slab_used[ot % 2] = tp
            cc.wait(a, tp)
            cc.wait(a, prev_ln)
            a.activation(cc.ych[:, ot * 512:(ot + 1) * 512], ps_o[:, :], AF.Copy)
            tok_y = cc.tok("a")
            yevac[ot % 2] = tok_y
        prev_ln = emit_ln_cols(cc, lambda i: cc.ych[:, i * 512:(i + 1) * 512],
                               lambda i: xres_t(cc, i), "add", c0, src_c0=0,
                               src_is_bf16=True, wait_tokens=(tok_y,))
    return prev_ln


def emit_l1_attn(cc, W):
    nc = cc.nc
    t, v, a, g = nc.tensor, cc.V, cc.A, nc.gpsimd
    tok_xn = emit_ln_in(cc)
    GAM = [1.0 - 2.0 ** (-5 - (h // 2)) for h in range(16)]

    Ksb = pv(cc, 0, 8 * T)
    Vsb = pv(cc, 16384, 16 * 1024)
    ring = pv(cc, 32768, 2048)
    GQ = pv(cc, 34816, 4096)
    M01 = pv(cc, 38912, 1920)
    wring = pv(cc, 32768, 2048)
    tok_tab = [dma_w(cc, GQ, W["l1_gq"][:, :]),
               dma_w(cc, M01, W["l1_m01"][:, :]),
               dma_g(cc, cc.l1s[:, 0:16], W["l1_gk"][:, :]),
               dma_g(cc, cc.l1s[:, 16:24], W["l1_gnw"][:, :]),
               dma_g(cc, cc.l1s[:, 24:32], W["l1_gnb"][:, :]),
               dma_g(cc, cc.l1s[0:16, 32:40], W["l1_msk16"][:, :])]

    slab_tok = [None, None]
    tok_k = None
    for m in range(DT):
        par = m % 2
        slab = ring[:, par * 1024:par * 1024 + 1024]
        cc.wait(g, slab_tok[par])
        tok_l = dma_g(cc, slab.rearrange("p (k m) -> p k m", k=DT),
                      W["l1_wkcat"][:, m * 128:(m + 1) * 128]
                      .rearrange("(k p) m -> p k m", p=128))
        for c0 in range(0, T, 512):
            ps = cc.big0 if (c0 // 512) % 2 == 0 else cc.big1
            i_ = None
            for dt_ in range(DT):
                cc.wait(t, tok_l)
                cc.wait(t, tok_xn)
                if dt_ == 0:
                    cc.wait(t, cc.tok("a"))
                i_ = t.matmul(ps[:, :], slab[:, dt_ * 128:(dt_ + 1) * 128],
                              xn_t(cc, dt_)[:, c0:c0 + 512],
                              start=(dt_ == 0), stop=(dt_ == DT - 1))
            tp = cc.inc(i_, "p")
            slab_tok[par] = tp
            cc.wait(a, tp)
            a.activation(Ksb[:, m * T + c0:m * T + c0 + 512], ps[:, :], AF.Copy)
            tok_k = cc.tok("a")

    tok_vv = None
    vprev = None
    for q4 in range(4):
        cc.wait(g, slab_tok[0])
        cc.wait(g, slab_tok[1])
        cc.wait(g, vprev)
        tok_l = dma_g(cc, ring.rearrange("p (k m) -> p k m", k=DT),
                      W["l1_wvcat"][:, q4 * 256:(q4 + 1) * 256]
                      .rearrange("(k p) m -> p k m", p=128))
        for tt in range(16):
            ps = cc.big2 if tt % 2 == 0 else cc.big3
            i_ = None
            for dt_ in range(DT):
                cc.wait(t, tok_l)
                cc.wait(t, tok_xn)
                if dt_ == 0:
                    cc.wait(t, cc.tok("a"))
                i_ = t.matmul(ps[:, 0:256],
                              xn_t(cc, dt_)[:, tt * 128:(tt + 1) * 128],
                              ring[:, dt_ * 256:(dt_ + 1) * 256],
                              start=(dt_ == 0), stop=(dt_ == DT - 1))
            tp = cc.inc(i_, "p")
            vprev = tp
            cc.wait(a, tp)
            a.activation(Vsb[:, tt * 1024 + q4 * 256:tt * 1024 + (q4 + 1) * 256],
                         ps[:, 0:256], AF.Copy)
            tok_vv = cc.tok("a")
        slab_tok = [cc.tok("p"), cc.tok("p")]

    srcs = [Ksb[:, i * T:(i + 1) * T] for i in range(8)] + \
           [Vsb[:, i * 1024:(i + 1) * 1024] for i in range(16)]
    buf, offs, rtok = publish(cc, "l1kv", srcs, wait_tokens=(tok_k, tok_vv))
    koff = offs[:8]
    voff = offs[8:]

    avb = pv(cc, 0, 8 * T)
    Qc = pv(cc, 16384, 4096)
    kr = pv(cc, 20480, 256)
    vr = pv(cc, 20736, 128)
    qring = pv(cc, 22016, 2048)
    cc.wait(a, rtok)
    cc.wait(v, rtok)

    kused = [None, None]
    avev = None
    qslab_tok = [None, None]
    for qc in range(NCH):
        c0 = qc * CH
        tok_q = None
        for m in range(DT):
            par = m % 2
            slab = qring[:, par * 1024:par * 1024 + 1024]
            cc.wait(g, qslab_tok[par])
            tok_l = dma_g(cc, slab.rearrange("p (k m) -> p k m", k=DT),
                          W["l1_wqcat"][:, m * 128:(m + 1) * 128]
                          .rearrange("(k p) m -> p k m", p=128))
            ps = cc.big0 if par == 0 else cc.big1
            i_ = None
            for dt_ in range(DT):
                cc.wait(t, tok_l)
                cc.wait(t, tok_xn)
                if dt_ == 0:
                    cc.wait(t, cc.tok("v"))
                    cc.wait(t, avev)
                i_ = t.matmul(ps[:, :], slab[:, dt_ * 128:(dt_ + 1) * 128],
                              xn_t(cc, dt_)[:, c0:c0 + 512],
                              start=(dt_ == 0), stop=(dt_ == DT - 1))
            tp = cc.inc(i_, "p")
            qslab_tok[par] = tp
            cc.wait(v, tp)
            cc.wait(v, tok_tab[0])
            v.tensor_tensor(Qc[:, m * 512:(m + 1) * 512], ps[:, :],
                            GQ[:, m * 512:(m + 1) * 512], ALU.mult)
            tok_q = cc.tok("v")
        for h in range(16):
            qb, qtile = 64 * (h % 2), h // 2
            gam = GAM[h]
            kts = [kt for kt in range(8 * qc + 8)
                   if gam ** (1024 * qc - 128 * kt + 512) >= 1e-28]
            exp_toks, av_toks = {}, {}
            for ki, kt in enumerate(kts):
                par = ki % 2
                ps_s = cc.big0 if par == 0 else cc.big1
                bb = kt // 4
                w_ = bb % 2
                vtt = (bb // 2) * 4 + kt % 4
                kcol = koff[qtile] + (bb // 2) * 512 + (kt % 4) * 128
                cc.wait(g, kused[par])
                tok_kv = ex_dma(cc, lambda c, kcol=kcol, w_=w_, vtt=vtt, h=h, par=par: [
                    (kr[qb:qb + 64, par * 128:par * 128 + 128],
                     buf[w_, qb:qb + 64, kcol:kcol + 128]),
                    (vr[:, par * 64:par * 64 + 64],
                     buf[w_, :, voff[vtt] + h * 64:voff[vtt] + h * 64 + 64]),
                ])
                cc.wait(t, tok_q)
                cc.wait(t, tok_kv)
                if ki >= 2:
                    cc.wait(t, exp_toks[ki - 2])
                if ki == 0:
                    cc.wait(t, avev)
                i_ = t.matmul(ps_s[:, :],
                              kr[qb:qb + 64, par * 128:par * 128 + 128],
                              Qc[qb:qb + 64, qtile * 512:(qtile + 1) * 512],
                              start=True, stop=True)
                ts_ = cc.inc(i_, "p")
                tfac = gam ** (1024 * qc - 128 * kt)
                Ssl = cc.xb2[:, par * 512:par * 512 + 512]
                tmp = cc.sq2[:, par * 512:par * 512 + 512]
                cc.wait(v, ts_)
                if ki >= 2:
                    cc.wait(v, av_toks[ki - 2])
                dprime = 128 * kt - 1024 * qc
                cc.wait(v, tok_tab[2])
                if -128 < dprime < 1024:
                    cc.wait(v, tok_tab[1])
                    v.tensor_scalar(tmp, ps_s[:, :], cc.l1s[:, h:h + 1],
                                    float(tfac), ALU.mult, ALU.mult)
                    v.tensor_tensor(Ssl, tmp,
                                    M01[:, 1408 - dprime:1920 - dprime], ALU.mult)
                else:
                    v.tensor_scalar(Ssl, ps_s[:, :], cc.l1s[:, h:h + 1],
                                    float(tfac), ALU.mult, ALU.mult)
                exp_toks[ki] = cc.tok("v")
                cc.wait(t, exp_toks[ki])
                i_ = t.matmul(cc.big2[0:64, :], vr[:, par * 64:par * 64 + 64],
                              Ssl, start=(ki == 0), stop=(ki == len(kts) - 1))
                av_toks[ki] = cc.inc(i_, "p")
                kused[par] = av_toks[ki]
            cc.wait(a, av_toks[len(kts) - 1])
            a.activation(avb[:, qtile * T + c0:qtile * T + c0 + 512][qb:qb + 64, :],
                         cc.big2[0:64, :], AF.Copy)
            avev = cc.tok("a")
    return _l1_gn_out(cc, W, avb, wring, cc.tok("a"))


def _l1_gn_out(cc, W, avb, wring, tok_av):
    nc = cc.nc
    t, v, a, g = nc.tensor, cc.V, cc.A, nc.gpsimd
    P1 = cc.l1s[:, 64:72]
    P2 = cc.l1s[:, 72:80]
    for ti in range(DT):
        cc.wait(v, tok_av)
        v.tensor_reduce(P1[:, ti:ti + 1], avb[:, ti * T:(ti + 1) * T],
                        axis=AX.X, op=ALU.add)
        for c0 in range(0, T, 512):
            sq = cc.sq2[:, 0:512]
            cc.wait(a, tok_av)
            cc.wait(a, cc.tok("v"))
            a.activation(sq, avb[:, ti * T + c0:ti * T + c0 + 512], AF.Square)
            cc.wait(v, cc.tok("a"))
            if c0 == 0:
                v.tensor_reduce(P2[:, ti:ti + 1], sq, axis=AX.X, op=ALU.add)
            else:
                v.tensor_reduce(cc.l1s[:, 80:81], sq, axis=AX.X, op=ALU.add)
                v.tensor_tensor(P2[:, ti:ti + 1], P2[:, ti:ti + 1],
                                cc.l1s[:, 80:81], ALU.add)
    tok_p = cc.tok("v")
    Ppe = cc.l1s[:, 81:97]
    rdt = exchange(cc, "l1st", [(Ppe, cc.l1s[:, 64:80])], wait_tokens=(tok_p,))
    cc.wait(v, rdt)
    v.tensor_tensor(P1, P1, Ppe[:, 0:8], ALU.add)
    v.tensor_tensor(P2, P2, Ppe[:, 8:16], ALU.add)
    tok_c = cc.tok("v")
    for st_, src in ((0, P1), (16, P2)):
        for hh in range(16):
            qb = 64 * (hh % 2)
            ti = hh // 2
            cc.wait(t, tok_c)
            i_ = t.matmul(cc.row0[0:1, st_ + hh:st_ + hh + 1],
                          cc.onescf[qb:qb + 64, 0:1], src[qb:qb + 64, ti:ti + 1],
                          start=True, stop=True)
    tok_s = cc.inc(i_, "p")
    cnt = 64.0 * 4096.0
    mean = cc.rowf[0:1, 0:16]
    var = cc.rowf[0:1, 16:32]
    rstd = cc.rowf[0:1, 32:48]
    cc.wait(v, tok_s)
    v.tensor_scalar(mean, cc.row0[0:1, 0:16], 1.0 / cnt, None, ALU.mult)
    v.tensor_tensor(var, mean, mean, ALU.mult)
    v.scalar_tensor_tensor(var, cc.row0[0:1, 16:32], 1.0 / cnt, var,
                           ALU.mult, ALU.subtract)
    tv = cc.tok("v")
    cc.wait(a, tv)
    a.activation(rstd, var, AF.Sqrt, bias=cc.constt[0:1, 0:1])
    cc.wait(v, cc.tok("a"))
    v.reciprocal(rstd, rstd)
    v.tensor_copy(cc.rowf[0:1, 48:64], mean)
    v.tensor_copy(cc.rowf[0:1, 64:80], rstd)
    tvb = cc.tok("v")
    bnc = cc.nc.dram_tensor("l1_bounce", [2, 16], F32, kind="Internal").ap()
    cc.wait(g, tvb)
    t1 = dma_g(cc, bnc.rearrange("a b -> (a b)")[None, :], cc.rowf[0:1, 48:80])
    cc.wait(g, t1)
    t2 = dma_g(cc, cc.l1s[0:16, 104:106], bnc.rearrange("a b -> b a"), slow=True)
    cc.wait(v, t2)
    v.tensor_scalar(cc.e2t[0:16, 128:136], cc.l1s[0:16, 32:40],
                    cc.l1s[0:16, 105:106], None, ALU.mult)
    v.tensor_scalar(cc.e2t[0:16, 136:144], cc.l1s[0:16, 32:40],
                    cc.l1s[0:16, 104:105], None, ALU.mult)
    tv2 = cc.tok("v")
    e2 = cc.l1s[0:16, 122:129] if False else None
    tok_e2 = dma_g(cc, cc.e2t[0:16, 0:128], W["l1_e2"][:, :])
    cc.wait(t, tv2)
    cc.wait(t, tok_e2)
    t.matmul(cc.big0[:, 0:8], cc.e2t[0:16, 0:128], cc.e2t[0:16, 128:136],
             start=True, stop=True)
    i_ = t.matmul(cc.big1[:, 0:8], cc.e2t[0:16, 0:128], cc.e2t[0:16, 136:144],
                  start=True, stop=True)
    tok_bc = cc.inc(i_, "p")
    cc.wait(v, tok_bc)
    A_ = cc.l1s[:, 40:48]
    B_ = cc.l1s[:, 48:56]
    v.tensor_tensor(A_, cc.l1s[:, 16:24], cc.big0[:, 0:8], ALU.mult)
    v.tensor_tensor(B_, cc.big1[:, 0:8], A_, ALU.mult)
    v.scalar_tensor_tensor(B_, B_, -1.0, cc.l1s[:, 24:32], ALU.mult, ALU.add)
    tok_ab = cc.tok("v")

    prev_ln = None
    slab_used = [None, None]
    yevac = [None, None]
    cn = pv(cc, 16384, 4096)
    for qc in range(NCH):
        c0 = qc * CH
        for ti in range(DT):
            cc.wait(v, tok_ab)
            if prev_ln is not None:
                cc.wait(v, cc.tok("p"))
            v.tensor_scalar(cn[:, ti * 512:(ti + 1) * 512],
                            avb[:, ti * T + c0:ti * T + c0 + 512],
                            A_[:, ti:ti + 1], B_[:, ti:ti + 1],
                            ALU.mult, ALU.add)
        tok_cn = cc.tok("v")
        tok_y = None
        for ot in range(DT):
            slab = wring[:, (ot % 2) * 1024:(ot % 2) * 1024 + 1024]
            cc.wait(g, slab_used[ot % 2])
            tok_l = None
            for dt_ in range(DT):
                tok_l = dma_g(cc, slab[:, dt_ * 128:(dt_ + 1) * 128],
                              W["L1_attn_Wout"][dt_ * 128:(dt_ + 1) * 128,
                                                ot * 128:(ot + 1) * 128])
            ps_o = cc.big4 if ot % 2 == 0 else cc.big5
            i_ = None
            for dt_ in range(DT):
                cc.wait(t, tok_cn)
                cc.wait(t, tok_l)
                if dt_ == 0:
                    cc.wait(t, yevac[ot % 2])
                if prev_ln is not None:
                    cc.wait(t, prev_ln)
                i_ = t.matmul(ps_o[:, :], slab[:, dt_ * 128:(dt_ + 1) * 128],
                              cn[:, dt_ * 512:(dt_ + 1) * 512],
                              start=(dt_ == 0), stop=(dt_ == DT - 1))
            tp = cc.inc(i_, "p")
            slab_used[ot % 2] = tp
            cc.wait(a, tp)
            cc.wait(a, prev_ln)
            a.activation(cc.ych[:, ot * 512:(ot + 1) * 512], ps_o[:, :], AF.Copy)
            tok_y = cc.tok("a")
            yevac[ot % 2] = tok_y
        prev_ln = emit_ln_cols(cc, lambda i: cc.ych[:, i * 512:(i + 1) * 512],
                               lambda i: xres_t(cc, i), "add", c0, src_c0=0,
                               src_is_bf16=True, wait_tokens=(tok_y,))
    return prev_ln


def _quant_weights_to_hbm(cc, W, wname, rows, cols, qbuf, recip_dst):
    nc = cc.nc
    t, v, a, g = nc.tensor, cc.V, cc.A, nc.gpsimd
    RT = rows // 128
    cc.wait(nc.sync, cc.tok("v"))
    cc.wait(nc.sync, cc.tok("a"))
    acc = cc.l1s[:, 0:1]
    first = [True]
    toks = []
    for rt in range(RT):
        for c0 in range(0, cols, 512):
            sl = cc.rowf[:, (len(toks) % 2) * 512:(len(toks) % 2) * 512 + 512]
            if len(toks) >= 2:
                cc.wait(nc.sync, cc.tok("a"))
            tok_l = dma_w(cc, sl, W[wname][rt * 128:(rt + 1) * 128, c0:c0 + 512])
            toks.append(tok_l)
            cc.wait(a, tok_l)
            cc.wait(a, cc.tok("v"))
            a.activation(cc.qswp[:, 0:512], sl, AF.Abs,
                         accum_out=cc.l1s[:, 1:2])
            cc.wait(v, cc.tok("a"))
            if first[0]:
                v.tensor_copy(acc, cc.l1s[:, 1:2])
                first[0] = False
            else:
                v.tensor_tensor(acc, acc, cc.l1s[:, 1:2], ALU.add)
    tok_acc = cc.tok("v")
    cc.wait(t, tok_acc)
    i_ = t.matmul(cc.row0[0:1, 0:1], cc.onescf[:, 0:1], acc, start=True, stop=True)
    tp = cc.inc(i_, "p")
    cc.wait(v, tp)
    v.tensor_scalar(cc.rowf[0:1, 1024:1025], cc.row0[0:1, 0:1],
                    1.0 / float(rows * cols), None, ALU.mult)
    v.tensor_scalar(cc.rowf[0:1, 1024:1025], cc.rowf[0:1, 1024:1025], 1e-5,
                    None, ALU.max)
    v.tensor_copy(recip_dst, cc.rowf[0:1, 1024:1025])
    v.reciprocal(cc.rowf[0:1, 1025:1026], cc.rowf[0:1, 1024:1025])
    v.tensor_copy(cc.rowR[0:1, 0:1], cc.rowf[0:1, 1025:1026])
    tv = cc.tok("v")
    cc.wait(t, tv)
    i_ = t.matmul(cc.big3[:, 0:1], cc.ones1[0:1, :], cc.rowR[0:1, 0:1],
                  start=True, stop=True)
    tp = cc.inc(i_, "p")
    cc.wait(v, tp)
    v.tensor_copy(cc.l1s[:, 2:3], cc.big3[:, 0:1])
    tok_s = cc.tok("v")
    cc.wait(nc.sync, cc.tok("a"))
    qi = cc.lntmp[:, 0:512]
    tok_qb = getattr(cc, "last_qbuf", None)
    for rt in range(RT):
        for c0 in range(0, cols, 512):
            sl = cc.rowf[:, ((rt * (cols // 512) + c0 // 512) % 2) * 512:
                         ((rt * (cols // 512) + c0 // 512) % 2) * 512 + 512]
            cc.wait(nc.sync, cc.tok("v"))
            tok_l = dma_w(cc, sl, W[wname][rt * 128:(rt + 1) * 128, c0:c0 + 512])
            cc.wait(v, tok_l)
            cc.wait(v, tok_s)
            cc.wait(v, tok_qb)
            v.tensor_scalar(qi, sl, cc.l1s[:, 2:3], None, ALU.mult)
            v.tensor_copy(cc.qint2[:, 0:512], qi)
            v.tensor_scalar(cc.qint2[:, 0:512], cc.qint2[:, 0:512], 1,
                            None, ALU.min)
            v.tensor_scalar(cc.qint2[:, 0:512], cc.qint2[:, 0:512], -1,
                            None, ALU.max)
            v.tensor_copy(cc.xb2[:, 0:512], cc.qint2[:, 0:512])
            tv = cc.tok("v")
            cc.wait(g, tv)
            tok_qb = dma_g(cc, qbuf[rt * 128:(rt + 1) * 128, c0:c0 + 512],
                           cc.xb2[:, 0:512])
    cc.last_qbuf = tok_qb
    return tok_qb


def emit_l2_ffn(cc, W):
    nc = cc.nc
    t, v, a, g = nc.tensor, cc.V, cc.A, nc.gpsimd
    HT = 32
    w1q = nc.dram_tensor("l2f_w1q", [1024, 4096], BF16, kind="Internal").ap()
    w2q = nc.dram_tensor("l2f_w2q", [4096, 1024], BF16, kind="Internal").ap()
    hsp = nc.dram_tensor("l2f_hsp", [NCH, HT, 128, 512], F32, kind="Internal").ap()
    ssp = nc.dram_tensor("l2f_ssp", [HT, 128, 512], F32, kind="Internal").ap()
    _quant_weights_to_hbm(cc, W, "l2_w1T", 1024, 4096, w1q, cc.l1s[0:1, 8:9])
    tok_q2 = _quant_weights_to_hbm(cc, W, "l2_w2T", 4096, 1024, w2q,
                                   cc.l1s[0:1, 9:10])
    tok_cw = dma_g(cc, cc.l1s[:, 16:112], W["l2_convw"][:, :])
    tok_cb = dma_g(cc, cc.scl[:, 0:32], W["l2_convb"][:, :])
    cc.wait(nc.vector, cc.last_qbuf)

    xq2 = pv(cc, 0, HT * 512)
    ring = pv(cc, 33800, 8192)
    edges = pv(cc, 16384, 257)
    v.memset(edges[:, 256:257], 0.0)

    def colacc(src_fp32, first):
        if first:
            v.tensor_copy(cc.rowf[:, 512:1024], src_fp32)
            v.tensor_copy(cc.rowf[:, 1024:1536], src_fp32)
        else:
            v.tensor_tensor(cc.rowf[:, 512:1024], cc.rowf[:, 512:1024],
                            src_fp32, ALU.max)
            v.tensor_tensor(cc.rowf[:, 1024:1536], cc.rowf[:, 1024:1536],
                            src_fp32, ALU.min)

    def colreduce(spill_wait=None):
        maxa = cc.rowf[:, 512:1024]
        tmp = cc.lntmp[:, 0:512]
        cc.wait(v, cc.tok("a"))
        cc.wait(v, spill_wait)
        v.tensor_scalar(tmp, cc.rowf[:, 1024:1536], -1.0, None, ALU.mult)
        v.tensor_tensor(maxa, maxa, tmp, ALU.max)
        for sft in (16, 8, 4, 2, 1):
            mask = [min(i + sft, 31) for i in range(32)]
            v.stream_shuffle(tmp, maxa, mask)
            v.tensor_tensor(maxa, maxa, tmp, ALU.max)
        quad = cc.lntmp2[0:32, 0:512]
        v.memset(quad, 0.0)
        v.tensor_copy(cc.lntmp2[0:1, 0:512], maxa[0:1, :])
        tv_ = cc.tok("v")
        cc.wait(nc.scalar, tv_)
        cc.wait(nc.scalar, spill_wait)
        tks = [dma_s(cc, cc.lntmp2[1:2, 0:512], maxa[32:33, :]),
               dma_s(cc, cc.lntmp2[2:3, 0:512], maxa[64:65, :]),
               dma_s(cc, cc.lntmp2[3:4, 0:512], maxa[96:97, :])]
        for tk in tks:
            cc.wait(v, tk)
        for sft in (2, 1):
            mask = [min(i + sft, 31) for i in range(32)]
            v.stream_shuffle(cc.lntmp[0:32, 0:512], quad, mask)
            v.tensor_tensor(quad, quad, cc.lntmp[0:32, 0:512], ALU.max)
        v.tensor_copy(cc.rowf[0:1, 0:512], cc.lntmp2[0:1, 0:512])

    def finish_scale(recip_cell):
        v.tensor_scalar(cc.rowf[0:1, 0:512], cc.rowf[0:1, 0:512], 1e-5,
                        None, ALU.max)
        v.tensor_scalar(cc.rowf[0:1, 1024:1536], cc.rowf[0:1, 0:512],
                        recip_cell, 1.0 / 127.0, ALU.mult, ALU.mult)
        v.reciprocal(cc.rowf[0:1, 0:512], cc.rowf[0:1, 0:512])
        v.tensor_scalar(cc.rowR[0:1, 0:512], cc.rowf[0:1, 0:512], 127.0,
                        None, ALU.mult)
        tv_ = cc.tok("v")
        cc.wait(t, tv_)
        i_ = t.matmul(cc.big3[:, :], cc.ones1[0:1, :], cc.rowR[0:1, :],
                      start=True, stop=True)
        return cc.inc(i_, "p")

    def quantize_tile(src_fp32, dst_bf16, tok_bc):
        cc.wait(v, tok_bc)
        v.tensor_tensor(cc.lntmp[:, 0:512], src_fp32, cc.big3[:, :], ALU.mult)
        v.tensor_copy(cc.qint2[:, 0:512], cc.lntmp[:, 0:512])
        v.tensor_scalar(cc.qint2[:, 0:512], cc.qint2[:, 0:512], 127,
                        None, ALU.min)
        v.tensor_scalar(cc.qint2[:, 0:512], cc.qint2[:, 0:512], -128,
                        None, ALU.max)
        v.tensor_copy(dst_bf16, cc.qint2[:, 0:512])

    prev_ln = None
    slab_tok = [None, None]
    yevac = [None, None]
    for qc in range(NCH):
        c0 = qc * CH
        wt = [] if prev_ln is None else [prev_ln]
        wt.append(getattr(cc, "_l2f_hsp_tok", None))
        tok_ln = emit_ln_cols(cc, lambda i: xres_t(cc, i),
                              lambda i: xn_t(cc, i), "set", c0,
                              wait_tokens=tuple(wt))
        for ti in range(DT):
            fl = cc.lntmp2[:, 0:512]
            cc.wait(v, tok_ln)
            v.tensor_tensor(fl, xres_t(cc, ti)[:, c0:c0 + 512], cc.big0[:, :],
                            ALU.mult)
            v.tensor_tensor(fl, fl, cc.big1[:, :], ALU.add)
            colacc(fl, ti == 0)
        colreduce()
        tok_bc = finish_scale(cc.l1s[0:1, 8:9])
        for ti in range(DT):
            fl = cc.lntmp2[:, 0:512]
            cc.wait(v, cc.tok("gp"))
            v.tensor_tensor(fl, xres_t(cc, ti)[:, c0:c0 + 512], cc.big0[:, :],
                            ALU.mult)
            v.tensor_tensor(fl, fl, cc.big1[:, :], ALU.add)
            quantize_tile(fl, xn_t(cc, ti)[:, c0:c0 + 512], tok_bc)
        tok_xq = cc.tok("v")
        cc.wait(v, cc.tok("p"))
        v.tensor_copy(cc.rowR[0:1, :], cc.rowf[0:1, 1024:1536])
        tv = cc.tok("v")
        cc.wait(t, tv)
        i_ = t.matmul(cc.big2[:, :], cc.ones1[0:1, :], cc.rowR[0:1, :],
                      start=True, stop=True)
        tok_u1 = cc.inc(i_, "p")
        for m in range(HT):
            par = m % 2
            ps = cc.big4 if par == 0 else cc.big5
            slab = ring[:, par * 1024:par * 1024 + 1024]
            cc.wait(g, slab_tok[par])
            tok_l = dma_g(cc, slab.rearrange("p (k m) -> p k m", k=DT),
                          w1q[:, m * 128:(m + 1) * 128]
                          .rearrange("(k p) m -> p k m", p=128))
            i_ = None
            for dt_ in range(DT):
                cc.wait(t, tok_l)
                cc.wait(t, tok_xq)
                if dt_ == 0:
                    cc.wait(t, cc.tok("a"))
                i_ = t.matmul(ps[:, :], slab[:, dt_ * 128:(dt_ + 1) * 128],
                              xn_t(cc, dt_)[:, c0:c0 + 512],
                              start=(dt_ == 0), stop=(dt_ == DT - 1))
            tp = cc.inc(i_, "p")
            slab_tok[par] = tp
            cc.wait(a, tp)
            a.activation(cc.lntmp[:, 0:512], ps[:, :], AF.Copy)
            ta = cc.tok("a")
            cc.wait(v, ta)
            cc.wait(v, tok_u1)
            cc.wait(v, getattr(cc, "_l2f_hsp_tok", None))
            v.tensor_tensor(cc.lntmp2[:, 0:512], cc.lntmp[:, 0:512],
                            cc.big2[:, :], ALU.mult)
            v.tensor_copy(edges[:, qc * 64 + m * 2:qc * 64 + m * 2 + 1],
                          cc.lntmp2[:, 0:1])
            v.tensor_copy(edges[:, qc * 64 + m * 2 + 1:qc * 64 + m * 2 + 2],
                          cc.lntmp2[:, 511:512])
            th = cc.tok("v")
            cc.wait(g, th)
            cc._l2f_hsp_tok = dma_g(cc, hsp[qc, m], cc.lntmp2[:, 0:512])
    tok_sp = cc.tok("g")
    tok_eg = cc.tok("v")
    ebuf, eoffs, ertok = publish(cc, "l2fe", [edges],
                                 wait_tokens=(tok_sp, tok_eg))
    for qc in range(NCH):
        c0 = qc * CH
        for m in range(HT):
            B = cc.lntmp2[:, 0:512]
            acc = cc.lntmp[:, 0:512]
            cc.wait(g, cc.tok("v"))
            cc.wait(g, cc.tok("gp"))
            cc.wait(g, ertok)
            gbar(cc, g)
            tok_hr = dma_g(cc, B, hsp[qc, m])

            def halo(c, m=m, qc=qc):
                pr = (c ^ 1) % 2
                p_ = c % 2
                if p_ == 0:
                    lsrc = (ebuf[pr, :, (qc - 1) * 64 + m * 2 + 1:
                                 (qc - 1) * 64 + m * 2 + 2]
                            if qc > 0 else ebuf[pr, :, 256:257])
                    rsrc = ebuf[pr, :, qc * 64 + m * 2:qc * 64 + m * 2 + 1]
                else:
                    lsrc = ebuf[pr, :, qc * 64 + m * 2 + 1:qc * 64 + m * 2 + 2]
                    rsrc = (ebuf[pr, :, (qc + 1) * 64 + m * 2:
                                 (qc + 1) * 64 + m * 2 + 1]
                            if qc < 3 else ebuf[pr, :, 256:257])
                return [(cc.l1s[:, 120:121], lsrc), (cc.l1s[:, 121:122], rsrc)]
            tok_halo = ex_dma(cc, halo, slow=True)
            w0 = cc.l1s[:, 16 + m * 3:17 + m * 3]
            w1_ = cc.l1s[:, 17 + m * 3:18 + m * 3]
            w2_ = cc.l1s[:, 18 + m * 3:19 + m * 3]
            cc.wait(v, tok_hr)
            cc.wait(v, tok_halo)
            cc.wait(v, tok_cw)
            v.tensor_scalar(acc[:, 1:512] if False else cc.lntmp[:, 1:512],
                            B[:, 0:511], w0, None, ALU.mult)
            v.tensor_scalar(cc.lntmp[:, 0:1], cc.l1s[:, 120:121], w0,
                            None, ALU.mult)
            v.scalar_tensor_tensor(acc, B, w1_, acc, ALU.mult, ALU.add)
            v.scalar_tensor_tensor(cc.lntmp[:, 0:511], B[:, 1:512], w2_,
                                   cc.lntmp[:, 0:511], ALU.mult, ALU.add)
            v.scalar_tensor_tensor(cc.lntmp[:, 511:512], cc.l1s[:, 121:122],
                                   w2_, cc.lntmp[:, 511:512], ALU.mult, ALU.add)
            tv = cc.tok("v")
            cc.wait(a, tv)
            cc.wait(a, tok_cb)
            cc.wait(a, getattr(cc, "_l2f_ssp_tok", None))
            a.activation(B, acc, AF.Silu, bias=cc.scl[:, m:m + 1])
            ta = cc.tok("a")
            cc.wait(v, ta)
            colacc(B, m == 0)
            tv = cc.tok("v")
            cc.wait(g, tv)
            cc._l2f_ssp_tok = dma_g(cc, ssp[m], B)
        colreduce(spill_wait=cc._l2f_ssp_tok)
        tok_bc2 = finish_scale(cc.l1s[0:1, 9:10])
        tok_ss = cc.tok("g")
        for m in range(HT):
            fl = cc.lntmp2[:, 0:512]
            cc.wait(g, cc.tok("v"))
            gbar(cc, g)
            tok_r = dma_g(cc, fl, ssp[m])
            cc.wait(v, tok_r)
            quantize_tile(fl, xq2[:, m * 512:(m + 1) * 512], tok_bc2)
        tok_xq2 = cc.tok("v")
        cc.wait(v, cc.tok("p"))
        v.tensor_copy(cc.rowR[0:1, :], cc.rowf[0:1, 1024:1536])
        tv = cc.tok("v")
        cc.wait(t, tv)
        i_ = t.matmul(cc.big2[:, :], cc.ones1[0:1, :], cc.rowR[0:1, :],
                      start=True, stop=True)
        tok_u2 = cc.inc(i_, "p")
        tok_y = None
        for ot in range(DT):
            par = ot % 2
            ps_o = cc.big4 if par == 0 else cc.big5
            slab = ring[:, par * 4096:par * 4096 + 4096]
            cc.wait(g, slab_tok[par])
            tok_l = dma_g(cc, slab.rearrange("p (k m) -> p k m", k=HT),
                          w2q[:, ot * 128:(ot + 1) * 128]
                          .rearrange("(k p) m -> p k m", p=128))
            i_ = None
            for k in range(HT):
                cc.wait(t, tok_l)
                cc.wait(t, tok_xq2)
                cc.wait(t, tok_q2)
                if k == 0:
                    cc.wait(t, yevac[par])
                i_ = t.matmul(ps_o[:, :], slab[:, k * 128:(k + 1) * 128],
                              xq2[:, k * 512:(k + 1) * 512],
                              start=(k == 0), stop=(k == HT - 1))
            tp = cc.inc(i_, "p")
            slab_tok[par] = tp
            cc.wait(a, tp)
            cc.wait(a, prev_ln)
            cc.wait(a, tok_y)
            a.activation(cc.xb2[:, 0:512], ps_o[:, :], AF.Copy)
            ta = cc.tok("a")
            cc.wait(v, ta)
            cc.wait(v, tok_u2)
            v.tensor_tensor(cc.ych[:, ot * 512:(ot + 1) * 512],
                            cc.xb2[:, 0:512], cc.big2[:, :], ALU.mult)
            tok_y = cc.tok("v")
            yevac[par] = tok_y
        prev_ln = emit_ln_cols(cc, lambda i: cc.ych[:, i * 512:(i + 1) * 512],
                               lambda i: xres_t(cc, i), "add", c0, src_c0=0,
                               src_is_bf16=True, wait_tokens=(tok_y,))
    return prev_ln


def emit_l2_attn(cc, W):
    nc = cc.nc
    t, v, a, g = nc.tensor, cc.V, cc.A, nc.gpsimd
    HT2 = 12
    ubuf = nc.dram_tensor("l2a_ubuf", [NCH, HT2, 128, 512], F32,
                          kind="Internal").ap()
    h0buf = nc.dram_tensor("l2a_h0buf", [NCH, HT2, 128, 512], F32,
                           kind="Internal").ap()
    pbuf = nc.dram_tensor("l2a_pbuf", [NCH, HT2, 128, 512], F32,
                          kind="Internal").ap()
    ybf = pv(cc, 0, HT2 * 512)
    zeros = pv(cc, 6144, 512)
    estage = pv(cc, 6656, 147)
    halo = pv(cc, 6816, 144)
    hg = pv(cc, 7168, HT2 * 512)
    ring = pv(cc, 33800, 8192)
    zero0 = cc.constt[:, 3:4]
    one0 = cc.constt[:, 4:5]

    tok_xn = emit_ln_in(cc)
    v.memset(zero0, 0.0)
    v.memset(one0, 1.0)
    v.memset(zeros[:, :], 0.0)
    v.memset(estage[:, 144:147], 0.0)
    v.memset(halo[:, 0:36], 0.0)
    tok_sc = [dma_w(cc, cc.scl[:, 0:48], W["l2a_cw"][:, :]),
              dma_w(cc, cc.scl[:, 48:60], W["l2a_cb"][:, :]),
              dma_w(cc, cc.scl[:, 60:72], W["l2a_lam"][:, :]),
              dma_w(cc, cc.scl[:, 84:96], W["l2a_bi"][:, :]),
              dma_w(cc, cc.scl[:, 96:108], W["l2a_bg"][:, :]),
              dma_w(cc, cc.scl[:, 108:109], W["l2_parf"][:, :])]
    lam = cc.scl[:, 60:72]
    spl = cc.scl[:, 72:84]
    tmpc = cc.scl[:, 112:124]
    for tk in tok_sc:
        cc.wait(a, tk)
        cc.wait(v, tk)
    cc.wait(a, cc.tok("v"))
    a.activation(tmpc, lam, AF.Abs, bias=zero0)
    cc.wait(v, cc.tok("a"))
    v.tensor_scalar(tmpc, tmpc, -1.0, None, ALU.mult)
    cc.wait(a, cc.tok("v"))
    a.activation(tmpc, tmpc, AF.Exp, bias=zero0)
    cc.wait(v, cc.tok("a"))
    v.tensor_scalar(tmpc, tmpc, 1.0, None, ALU.add)
    cc.wait(a, cc.tok("v"))
    a.activation(tmpc, tmpc, AF.Ln, bias=zero0)
    v.tensor_scalar(spl, lam, 0.0, None, ALU.max)
    cc.wait(v, cc.tok("a"))
    v.tensor_tensor(spl, spl, tmpc, ALU.add)
    v.tensor_scalar(spl, spl, -8.0, None, ALU.mult)

    rtok = {}
    spill_tok = [None, None]
    for k in range(NCH):
        ck = k * CH
        for m in range(HT2):
            par = m % 2
            ps = cc.big4 if par == 0 else cc.big5
            roff = par * 1024
            slab = ring[:, roff:roff + 1024]
            cc.wait(g, rtok.get(roff))
            tok_l = dma_g(cc, slab.rearrange("p (kt m) -> p kt m", kt=DT),
                          W["l2_wiT"][:, m * 128:(m + 1) * 128]
                          .rearrange("(kt p) m -> p kt m", p=128))
            i_ = None
            for kt in range(DT):
                cc.wait(t, tok_l)
                cc.wait(t, tok_xn)
                if kt == 0:
                    cc.wait(t, cc.tok("a"))
                i_ = t.matmul(ps[:, :], slab[:, kt * 128:(kt + 1) * 128],
                              xn_t(cc, kt)[:, ck:ck + 512],
                              start=(kt == 0), stop=(kt == DT - 1))
            tp = cc.inc(i_, "p")
            rtok[roff] = tp
            cc.wait(a, tp)
            cc.wait(a, spill_tok[par])
            cc.wait(a, cc.tok("v"))
            dst = cc.lntmp if par == 0 else cc.lntmp2
            a.activation(dst[:, 0:512], ps[:, :], AF.Copy)
            ta = cc.tok("a")
            cc.wait(v, ta)
            eb = 36 * k + 3 * m
            v.tensor_copy(estage[:, eb:eb + 3], dst[:, 509:512])
            tv = cc.tok("v")
            cc.wait(g, tv)
            spill_tok[par] = dma_g(cc, ubuf[k, m], dst[:, 0:512])
    ertok = None
    ebuf, _, ertok = publish(cc, "l2ae", [estage],
                             wait_tokens=(cc.tok("v"),))
    cc.wait(g, ertok)
    cc.wait(g, cc.tok("v"))
    tok_halo = ex_dma(cc, lambda c: [
        ((halo[:, 36:144], ebuf[(c % 2) ^ 1, :, 0:108]) if c % 2 == 0
         else (halo[:, 0:144], ebuf[(c % 2) ^ 1, :, 0:144]))])

    rspill = [None, None]
    for k in range(NCH):
        for m in range(HT2):
            B = cc.lntmp2[:, 0:512]
            acc = cc.lntmp[:, 0:512]
            cc.wait(g, cc.tok("v"))
            cc.wait(g, cc.tok("a"))
            gbar(cc, g)
            tok_u = dma_g(cc, B, ubuf[k, m])
            hb = 36 * k + 3 * m
            w0 = cc.scl[:, 4 * m:4 * m + 1]
            w1 = cc.scl[:, 4 * m + 1:4 * m + 2]
            w2 = cc.scl[:, 4 * m + 2:4 * m + 3]
            w3 = cc.scl[:, 4 * m + 3:4 * m + 4]
            cb = cc.scl[:, 48 + m:49 + m]
            cc.wait(v, tok_u)
            cc.wait(v, tok_halo)
            v.tensor_scalar(acc, B, w3, None, ALU.mult)
            v.scalar_tensor_tensor(acc[:, 1:512], B[:, 0:511], w2,
                                   acc[:, 1:512], ALU.mult, ALU.add)
            v.scalar_tensor_tensor(acc[:, 0:1], halo[:, hb + 2:hb + 3], w2,
                                   acc[:, 0:1], ALU.mult, ALU.add)
            v.scalar_tensor_tensor(acc[:, 2:512], B[:, 0:510], w1,
                                   acc[:, 2:512], ALU.mult, ALU.add)
            v.scalar_tensor_tensor(acc[:, 0:2], halo[:, hb + 1:hb + 3], w1,
                                   acc[:, 0:2], ALU.mult, ALU.add)
            v.scalar_tensor_tensor(acc[:, 3:512], B[:, 0:509], w0,
                                   acc[:, 3:512], ALU.mult, ALU.add)
            v.scalar_tensor_tensor(acc[:, 0:3], halo[:, hb:hb + 3], w0,
                                   acc[:, 0:3], ALU.mult, ALU.add)
            v.tensor_scalar(ybf[:, m * 512:(m + 1) * 512], acc, cb,
                            None, ALU.add)
        tok_y = cc.tok("v")
        for m in range(HT2):
            par = m % 2
            rgf = cc.lntmp[:, 0:512]
            alf = cc.lntmp2[:, 0:512]
            igf = cc.rowf[:, 0:512]
            btf = cc.rowf[:, 512:1024]
            xbf = cc.rowf[:, 1024:1536]
            ps = cc.big4
            roff = 2048 + par * 1536
            slab = ring[:, roff:roff + 1536]
            cc.wait(g, rtok.get(roff))
            tok_l = dma_g(cc, slab.rearrange("p (kt m) -> p kt m", kt=HT2),
                          W["l2_lwgT"][:, m * 128:(m + 1) * 128]
                          .rearrange("(kt p) m -> p kt m", p=128))
            i_ = None
            for kt in range(HT2):
                cc.wait(t, tok_l)
                cc.wait(t, tok_y)
                if kt == 0:
                    cc.wait(t, cc.tok("a"))
                i_ = t.matmul(ps[:, :], slab[:, kt * 128:(kt + 1) * 128],
                              ybf[:, kt * 512:(kt + 1) * 512],
                              start=(kt == 0), stop=(kt == HT2 - 1))
            tp = cc.inc(i_, "p")
            rtok[roff] = tp
            cc.wait(a, tp)
            cc.wait(a, rspill[0])
            cc.wait(a, cc.tok("v"))
            a.activation(rgf, ps[:, :], AF.Sigmoid,
                         bias=cc.scl[:, 96 + m:97 + m])
            cc.wait(v, cc.tok("a"))
            v.tensor_scalar(alf, rgf, cc.scl[:, 72 + m:73 + m], None, ALU.mult)
            cc.wait(a, cc.tok("v"))
            a.activation(alf, alf, AF.Exp, bias=zero0)
            ps2 = cc.big5
            roff2 = 5120 + par * 1536
            slab2 = ring[:, roff2:roff2 + 1536]
            cc.wait(g, rtok.get(roff2))
            tok_l2 = dma_g(cc, slab2.rearrange("p (kt m) -> p kt m", kt=HT2),
                           W["l2_lwiT"][:, m * 128:(m + 1) * 128]
                           .rearrange("(kt p) m -> p kt m", p=128))
            i_ = None
            for kt in range(HT2):
                cc.wait(t, tok_l2)
                if kt == 0:
                    cc.wait(t, cc.tok("a"))
                i_ = t.matmul(ps2[:, :], slab2[:, kt * 128:(kt + 1) * 128],
                              ybf[:, kt * 512:(kt + 1) * 512],
                              start=(kt == 0), stop=(kt == HT2 - 1))
            tp2 = cc.inc(i_, "p")
            rtok[roff2] = tp2
            cc.wait(a, tp2)
            cc.wait(a, cc.tok("v"))
            a.activation(igf, ps2[:, :], AF.Sigmoid,
                         bias=cc.scl[:, 84 + m:85 + m])
            cc.wait(v, cc.tok("a"))
            cc.wait(v, rspill[1])
            v.tensor_tensor(btf, alf, alf, ALU.mult)
            v.tensor_scalar(btf, btf, -1.0, 1.0 + 1e-6, ALU.mult, ALU.add)
            cc.wait(a, cc.tok("v"))
            a.activation(btf, btf, AF.Sqrt, bias=zero0)
            cc.wait(v, cc.tok("a"))
            v.tensor_tensor(xbf, btf, igf, ALU.mult)
            v.tensor_tensor(xbf, xbf, ybf[:, m * 512:(m + 1) * 512], ALU.mult)
            v.tensor_tensor_scan(rgf, alf, xbf, zero0[:, 0:1],
                                 ALU.mult, ALU.add)
            v.tensor_copy(cc.l1s[:, 48 + 12 * k + m:49 + 12 * k + m],
                          rgf[:, 511:512])
            tv = cc.tok("v")
            cc.wait(g, tv)
            rspill[0] = dma_g(cc, h0buf[k, m], rgf)
            v.tensor_tensor_scan(btf, alf, zeros[:, 0:512], one0[:, 0:1],
                                 ALU.mult, ALU.add)
            v.tensor_copy(cc.l1s[:, 12 * k + m:12 * k + m + 1],
                          btf[:, 511:512])
            tv = cc.tok("v")
            cc.wait(g, tv)
            rspill[1] = dma_g(cc, pbuf[k, m], btf)

    sbuf, soffs, srtok = publish(cc, "l2as", [cc.l1s[:, 0:96]],
                                 wait_tokens=(cc.tok("v"),))
    cc.wait(g, srtok)
    cc.wait(g, cc.tok("v"))
    gbar(cc, g)
    tok_rA = dma_g(cc, cc.rowf[:, 0:96], sbuf[0, :, 0:96])
    tok_rB = dma_g(cc, cc.rowf[:, 96:192], sbuf[1, :, 0:96])
    cc.wait(v, tok_rA)
    cc.wait(v, tok_rB)
    Hcur = cc.rowf[:, 192:204]
    Hodd = cc.rowf[:, 204:216]
    tmp = cc.rowf[:, 216:228]
    tmp2 = cc.rowf[:, 228:240]
    parf = cc.scl[:, 108:109]
    v.memset(Hcur, 0.0)
    for k in range(NCH):
        rE = cc.rowf[:, 0:96]
        A_e = rE[:, 12 * k:12 * k + 12]
        h_e = rE[:, 48 + 12 * k:48 + 12 * k + 12]
        rO = cc.rowf[:, 96:192]
        A_o = rO[:, 12 * k:12 * k + 12]
        h_o = rO[:, 48 + 12 * k:48 + 12 * k + 12]
        v.tensor_tensor(tmp, A_e, Hcur, ALU.mult)
        v.tensor_tensor(Hodd, tmp, h_e, ALU.add)
        v.tensor_tensor(tmp2, Hodd, Hcur, ALU.subtract)
        v.tensor_scalar(tmp2, tmp2, parf, None, ALU.mult)
        v.tensor_tensor(cc.l1s[:, 12 * k:12 * k + 12], Hcur, tmp2, ALU.add)
        if k < NCH - 1:
            v.tensor_tensor(tmp, A_o, Hodd, ALU.mult)
            v.tensor_tensor(Hcur, tmp, h_o, ALU.add)

    prev_ln = None
    yevac = [None, None]
    for k in range(NCH):
        ck = k * CH
        for m in range(HT2):
            par = m % 2
            hf = cc.rowf[:, 0:512]
            ggf = cc.rowf[:, 512:1024]
            cc.wait(g, cc.tok("v"))
            gbar(cc, g)
            tok_s0 = dma_g(cc, cc.lntmp[:, 0:512], h0buf[k, m])
            tok_pp = dma_g(cc, cc.lntmp2[:, 0:512], pbuf[k, m])
            ps = cc.big4 if par == 0 else cc.big5
            roff = par * 1024
            slab = ring[:, roff:roff + 1024]
            cc.wait(g, rtok.get(roff))
            tok_l = dma_g(cc, slab.rearrange("p (kt m) -> p kt m", kt=DT),
                          W["l2_wgT"][:, m * 128:(m + 1) * 128]
                          .rearrange("(kt p) m -> p kt m", p=128))
            i_ = None
            for kt in range(DT):
                cc.wait(t, tok_l)
                if kt == 0:
                    cc.wait(t, cc.tok("a"))
                i_ = t.matmul(ps[:, :], slab[:, kt * 128:(kt + 1) * 128],
                              xn_t(cc, kt)[:, ck:ck + 512],
                              start=(kt == 0), stop=(kt == DT - 1))
            tp = cc.inc(i_, "p")
            rtok[roff] = tp
            cc.wait(a, tp)
            cc.wait(a, cc.tok("v"))
            a.activation(ggf, ps[:, :], AF.Gelu, bias=zero0)
            cc.wait(v, tok_s0)
            cc.wait(v, tok_pp)
            v.scalar_tensor_tensor(hf, cc.lntmp2[:, 0:512],
                                   cc.l1s[:, 12 * k + m:12 * k + m + 1],
                                   cc.lntmp[:, 0:512], ALU.mult, ALU.add)
            cc.wait(v, cc.tok("a"))
            v.tensor_tensor(hg[:, m * 512:(m + 1) * 512], ggf, hf, ALU.mult)
        tok_hg = cc.tok("v")
        for o in range(DT):
            par = o % 2
            ps = cc.big4 if par == 0 else cc.big5
            roff = 2048 + par * 1536
            slab = ring[:, roff:roff + 1536]
            cc.wait(g, rtok.get(roff))
            tok_l = dma_g(cc, slab.rearrange("p (kt m) -> p kt m", kt=HT2),
                          W["l2_woT"][:, o * 128:(o + 1) * 128]
                          .rearrange("(kt p) m -> p kt m", p=128))
            i_ = None
            for kt in range(HT2):
                cc.wait(t, tok_l)
                cc.wait(t, tok_hg)
                if kt == 0:
                    cc.wait(t, yevac[par])
                    cc.wait(t, cc.tok("a"))
                i_ = t.matmul(ps[:, :], slab[:, kt * 128:(kt + 1) * 128],
                              hg[:, kt * 512:(kt + 1) * 512],
                              start=(kt == 0), stop=(kt == HT2 - 1))
            tp = cc.inc(i_, "p")
            rtok[roff] = tp
            cc.wait(a, tp)
            cc.wait(a, prev_ln)
            a.activation(cc.ych[:, o * 512:(o + 1) * 512], ps[:, :], AF.Copy)
            yevac[par] = cc.tok("a")
        prev_ln = emit_ln_cols(cc, lambda i: cc.ych[:, i * 512:(i + 1) * 512],
                               lambda i: xres_t(cc, i), "add", ck, src_c0=0,
                               src_is_bf16=True, wait_tokens=(cc.tok("a"),))
    return prev_ln


BLOCK_EMITTERS = {"l0_attn": emit_l0_attn, "l1_ffn": emit_l1_ffn,
                  "l3_ffn": emit_l3_ffn, "l0_ffn": emit_l0_ffn,
                  "l3_attn": emit_l3_attn, "l1_attn": emit_l1_attn,
                  "l2_ffn": emit_l2_ffn, "l2_attn": emit_l2_attn}


def build_program(block_names, weight_specs, shared_specs):
    nc = bass.Bass("TRN2", target_bir_lowering=False)
    cc = CC(nc)
    xin = nc.dram_tensor("xin", [D, T], F32, kind="ExternalInput").ap()
    xout = nc.dram_tensor("xout", [D, T], F32, kind="ExternalOutput").ap()
    cfg = nc.dram_tensor("cfg", [1, 8], I32, kind="ExternalInput").ap()
    W = {}
    for name, (shape, dtname) in weight_specs.items():
        W[name] = nc.dram_tensor(
            name, list(shape),
            {"float32": F32, "int32": I32, "bfloat16": BF16}[dtname],
            kind="ExternalInput").ap()
    for name, shape, dt_ in shared_specs:
        setattr(cc, name, nc.dram_tensor(name, list(shape), dt_, kind="Internal",
                                         addr_space="Shared").ap())
    cc.sh_flg = nc.dram_tensor("sh_flg", [8, 64], I32, kind="Internal",
                               addr_space="Shared").ap()

    st = _ExitStack()

    def sb(name, cols, dtype, parts=128):
        cm = nc.sbuf_tensor(name, [parts, cols], dtype)
        h = cm.__enter__()
        st.push(cm)
        return h[:, :]

    try:
        cc.xres = sb("xres", 8 * T, F32)
        cc.xn = sb("xn", 8 * T, BF16)
        cc.pool = sb("pool", POOL_COLS, BF16)
        cc.ych = sb("ych", 4096, BF16)
        cc.xb2 = sb("xb2", 1024, BF16)
        cc.sq2 = sb("sq2", 1024, BF16)
        cc.lntmp = sb("lntmp", 512, F32)
        cc.lntmp2 = sb("lntmp2", 512, F32)
        cc.qswp = sb("qswp", 512, BF16)
        cc.rowf = sb("rowf", 1536, F32)
        cc.rowR = sb("rowR", 512, BF16)
        cc.rowM = sb("rowM", 512, BF16)
        cc.ones1 = sb("ones1", 128, BF16, parts=1)
        cc.onesc = sb("onesc", 1, BF16)
        cc.onescf = sb("onescf", 1, F32)
        cc.constt = sb("constt", 16, F32)
        cc.scl = sb("scl", 136, F32)
        cc.l1s = sb("l1s", 128, F32)
        cc.qint2 = sb("qint2", 512, I32)
        cc.e2t = sb("e2t", 160, BF16)
        cc.cfgt = sb("cfgt", 8, I32, parts=1)
        cc.flagtile = sb("flagtile", 32, I32, parts=1)
        cc.noncetile = sb("noncetile", 1, I32, parts=1)

        with (
            nc.psum_tensor("big0", [128, 512], F32) as b0,
            nc.psum_tensor("big1", [128, 512], F32) as b1,
            nc.psum_tensor("big2", [128, 512], F32) as b2,
            nc.psum_tensor("big3", [128, 512], F32) as b3,
            nc.psum_tensor("big4", [128, 512], F32) as b4,
            nc.psum_tensor("big5", [128, 512], F32) as b5,
            nc.psum_tensor("row0", [1, 512], F32) as r0,
            nc.psum_tensor("row1", [1, 512], F32) as r1,
            nc.Block(),
        ):
            cc.big0, cc.big1, cc.big2 = b0[:, :], b1[:, :], b2[:, :]
            cc.big3, cc.big4, cc.big5 = b3[:, :], b4[:, :], b5[:, :]
            cc.row0, cc.row1 = r0[:, :], r1[:, :]

            cc.V = ChainedEngine(cc, nc.vector, "v", _DVE_OPS)
            cc.A = ChainedEngine(cc, nc.scalar, "a", _ACT_OPS)
            g, v = nc.gpsimd, cc.V
            t_cfg = cc.inc(nc.gpsimd.dma_start(out=cc.cfgt[0:1, :],
                                               in_=cfg[0:1, :]), "cfg", 16)
            t_x = dma_w(cc, cc.xres.rearrange("p (n m) -> p n m", n=DT),
                        xin.rearrange("(n p) m -> p n m", p=128))
            cc.wait(g, t_cfg)
            me_cm = g.register("me")
            cc.me_reg = me_cm.__enter__()
            nonce_cm = g.register("nonce")
            cc.nonce_reg = nonce_cm.__enter__()
            g.load(cc.me_reg, cc.cfgt[0:1, 0:1])
            g.load(cc.nonce_reg, cc.cfgt[0:1, 1:2])
            g.tensor_copy(cc.noncetile[0:1, 0:1], cc.cfgt[0:1, 1:2])
            g.memset(cc.flagtile[0:1, :], 0)
            v.memset(cc.ones1[0:1, :], 1.0)
            v.memset(cc.constt[:, 0:1], float(LN_EPS))
            v.memset(cc.constt[:, 1:2], -L0_SHIFT)
            v.memset(cc.constt[:, 2:3], -L3_SHIFT)
            v.memset(cc.onesc[:, 0:1], 1.0)
            v.memset(cc.onescf[:, 0:1], 1.0)
            tv = cc.tok("v")
            cc.wait(nc.tensor, tv)
            for eng in (nc.tensor, v, nc.scalar):
                cc.wait(eng, t_x)

            for bi, bn in enumerate(block_names):
                if bi > 0:
                    block_fence(cc)
                BLOCK_EMITTERS[bn](cc, W)

            sc = nc.scalar
            for nm in ("v", "a", "p"):
                cc.wait(sc, cc.tok(nm))
            t_out = dma_s(cc, xout.rearrange("(n p) m -> p n m", p=128),
                          cc.xres.rearrange("p (n m) -> p n m", n=DT))
            cc.wait(sc, t_out)
            me_cm.__exit__(None, None, None)
            nonce_cm.__exit__(None, None, None)
    finally:
        st.close()
    return nc


def _rope_tables(p):
    half = 32
    inv_freq = 1.0 / (10000.0 ** (np.arange(half, dtype=np.float64) * 2.0 / 64.0))
    pos = own_positions(p).astype(np.float64)
    fr = pos[None, :] * inv_freq[:, None]
    cos, sin = np.cos(fr), np.sin(fr)
    Ctab = np.zeros((128, T), np.float32)
    Stab = np.zeros((128, T), np.float32)
    for blk in range(2):
        r = blk * 64
        Ctab[r:r + 32] = cos
        Ctab[r + 32:r + 64] = cos
        Stab[r:r + 32] = -sin
        Stab[r + 32:r + 64] = sin
    return Ctab.astype(BF), Stab.astype(BF)


def _p128swap():
    P = np.zeros((128, 128), np.float32)
    for i in range(128):
        blk, off = divmod(i, 32)
        j = (blk ^ 1) * 32 + off
        P[j, i] = 1.0
    return P.astype(BF)


L0A_WEIGHTS = {
    "L0_attn_w_q": ((1024, 1024), "float32"),
    "L0_attn_w_k": ((1024, 256), "float32"),
    "L0_attn_w_v": ((1024, 256), "float32"),
    "L0_attn_w_o": ((1024, 1024), "float32"),
    "rope_c": ((128, T), "bfloat16"),
    "rope_s": ((128, T), "bfloat16"),
    "p128": ((128, 128), "bfloat16"),
}

L0A_SHARED = []

_NC_CACHE = {}


def _common_inputs(nonce):
    outs = []
    p128 = _p128swap()
    for c in range(8):
        p = c % 2
        ct, stt = _rope_tables(p)
        outs.append({
            "cfg": np.array([[c, nonce, c // 2, p, 0, 0, 0, 0]], np.int32),
            "rope_c": ct, "rope_s": stt, "p128": p128,
        })
    return outs


def _l0_head_col_perm():
    idx = []
    for s_ in range(16):
        h = L0_HEAD_PERM[s_]
        idx.extend(range(h * 64, h * 64 + 64))
    return np.array(idx)


WEIGHT_TRANSFORMS = {
    "L0_attn_w_q": lambda w: np.ascontiguousarray(w[:, _l0_head_col_perm()]),
    "L0_attn_w_o": lambda w: np.ascontiguousarray(w[_l0_head_col_perm(), :]),
}

DERIVED_INPUTS = {
    "l1_wgT": lambda ins: np.ascontiguousarray(ins["L1_ffn_w_gate"].T),
    "l1_wuT": lambda ins: np.ascontiguousarray(ins["L1_ffn_w_up"].T),
    "l1_wdT": lambda ins: np.ascontiguousarray(ins["L1_ffn_w_down"].T),
}

L1F_WEIGHTS = {
    "l1_wgT": ((1024, 3072), "int32"),
    "l1_wuT": ((1024, 3072), "int32"),
    "l1_wdT": ((3072, 1024), "int32"),
}


def _tile_t1(G1, c):
    t = np.asarray(G1)[0][:, c, :]
    return np.ascontiguousarray(
        np.broadcast_to(t.T[:, None, :, None], (32, 16, 32, 16))
        .reshape(512, 512)).astype(np.float32)


def _tile_t2(G2, c):
    u = np.asarray(G2)[c, :, 0, :]
    return np.ascontiguousarray(
        np.broadcast_to(u.T[None, :, None, :], (32, 16, 32, 16))
        .reshape(512, 512)).astype(np.float32)


L0F_WEIGHTS = {
    "l0_w1T": ((1024, 4096), "float32"),
    "l0_w2T": ((1024, 4096), "float32"),
    "l0_h1T": ((4096, 4096), "float32"),
    "l0_h2T": ((4096, 4096), "float32"),
    "l0_w3T": ((4096, 1024), "float32"),
    "l0_scales": ((128, 136), "float32"),
}
DERIVED_INPUTS["l0_w1T"] = lambda ins: np.ascontiguousarray(np.asarray(ins["L0_ffn_w1_w"]).T)
DERIVED_INPUTS["l0_w2T"] = lambda ins: np.ascontiguousarray(np.asarray(ins["L0_ffn_w2_w"]).T)
DERIVED_INPUTS["l0_h1T"] = lambda ins: np.ascontiguousarray(np.asarray(ins["L0_ffn_h1_w"]).T)
DERIVED_INPUTS["l0_h2T"] = lambda ins: np.ascontiguousarray(np.asarray(ins["L0_ffn_h2_w"]).T)
DERIVED_INPUTS["l0_w3T"] = lambda ins: np.ascontiguousarray(np.asarray(ins["L0_ffn_w3_w"]).T)


def _l0_scales(ins):
    out = np.zeros((128, 136), np.float32)
    out[:, 0:32] = np.asarray(ins["L0_ffn_w1_s"]).reshape(32, 128).T
    out[:, 32:64] = np.asarray(ins["L0_ffn_w2_s"]).reshape(32, 128).T
    out[:, 64:96] = np.asarray(ins["L0_ffn_h1_s"]).reshape(32, 128).T
    out[:, 96:128] = np.asarray(ins["L0_ffn_h2_s"]).reshape(32, 128).T
    out[:, 128:136] = np.asarray(ins["L0_ffn_w3_s"]).reshape(8, 128).T
    return out


DERIVED_INPUTS["l0_scales"] = _l0_scales

L3A_WEIGHTS = {
    "l3_wq2": ((128, 64), "float32"),
    "l3_wk2": ((128, 64), "float32"),
    "l3_wv2": ((128, 64), "float32"),
    "l3_m3": ((128, 1920), "bfloat16"),
    "l3_woT": ((1024, 1024), "float32"),
}
DERIVED_INPUTS["l3_wq2"] = lambda ins: np.ascontiguousarray(
    np.concatenate([np.asarray(ins["L3_attn_Wq"]).T] * 2, axis=0).astype(np.float32))
DERIVED_INPUTS["l3_wk2"] = lambda ins: np.ascontiguousarray(
    np.concatenate([np.asarray(ins["L3_attn_Wk"]).T] * 2, axis=0).astype(np.float32))
DERIVED_INPUTS["l3_wv2"] = lambda ins: np.ascontiguousarray(
    np.concatenate([np.asarray(ins["L3_attn_Wv"]).T] * 2, axis=0).astype(np.float32))
DERIVED_INPUTS["l3_woT"] = lambda ins: np.ascontiguousarray(
    np.asarray(ins["L3_attn_Wo"]).T.astype(np.float32))


def _l3_m3(p):
    kk = np.arange(128)[:, None]
    u = np.arange(1920)[None, :]
    return np.where(u >= kk + 1408 - 512 * p, 0.0, -240.0).astype(BF)


L1A_WEIGHTS = {
    "l1_wqcat": ((1024, 1024), "float32"),
    "l1_wkcat": ((1024, 1024), "float32"),
    "l1_wvcat": ((1024, 1024), "float32"),
    "L1_attn_Wout": ((1024, 1024), "float32"),
    "l1_gq": ((128, 4096), "bfloat16"),
    "l1_m01": ((128, 1920), "bfloat16"),
    "l1_gk": ((128, 16), "float32"),
    "l1_gnw": ((128, 8), "float32"),
    "l1_gnb": ((128, 8), "float32"),
    "l1_msk16": ((16, 8), "float32"),
    "l1_e2": ((16, 128), "bfloat16"),
}


def _l1_cat(w):
    w = np.asarray(w)
    return np.ascontiguousarray(w.transpose(1, 0, 2).reshape(1024, 1024)
                                .astype(np.float32))


DERIVED_INPUTS["l1_wqcat"] = lambda ins: _l1_cat(ins["L1_attn_Wq"])
DERIVED_INPUTS["l1_wkcat"] = lambda ins: _l1_cat(ins["L1_attn_Wk"])
DERIVED_INPUTS["l1_wvcat"] = lambda ins: _l1_cat(ins["L1_attn_Wv"])
DERIVED_INPUTS["l1_gnw"] = lambda ins: np.ascontiguousarray(
    np.asarray(ins["L1_attn_gn_weight"]).astype(np.float32).reshape(8, 128).T)
DERIVED_INPUTS["l1_gnb"] = lambda ins: np.ascontiguousarray(
    np.asarray(ins["L1_attn_gn_bias"]).astype(np.float32).reshape(8, 128).T)

L1_GAMMAS = [1.0 - 2.0 ** (-5 - (h // 2)) for h in range(16)]


def _l1_gq(p):
    out = np.zeros((128, 4096), np.float64)
    qq = np.arange(512, dtype=np.float64)
    for t_ in range(8):
        for par in range(2):
            gam = L1_GAMMAS[2 * t_ + par]
            out[par * 64:(par + 1) * 64, t_ * 512:(t_ + 1) * 512] = \
                gam ** (qq + 512.0 * p) * 0.125
    return out.astype(BF)


def _l1_gk():
    out = np.zeros((128, 16), np.float64)
    kk = np.arange(128, dtype=np.float64)
    for h in range(16):
        out[:, h] = L1_GAMMAS[h] ** (-kk)
    return out.astype(np.float32)


def _l1_m01(p):
    kk = np.arange(128)[:, None]
    u = np.arange(1920)[None, :]
    return np.where(u >= kk + 1408 - 512 * p, 1.0, 0.0).astype(BF)


def _l1_msk16():
    out = np.zeros((16, 8), np.float32)
    for h in range(16):
        out[h, h // 2] = 1.0
    return out


def _l1_e2():
    out = np.zeros((16, 128), np.float32)
    for h in range(16):
        out[h, 64 * (h % 2):64 * (h % 2) + 64] = 1.0
    return out.astype(BF)


DERIVED_INPUTS["l1_gk"] = lambda ins: _l1_gk()
DERIVED_INPUTS["l1_msk16"] = lambda ins: _l1_msk16()
DERIVED_INPUTS["l1_e2"] = lambda ins: _l1_e2()

L2F_WEIGHTS = {
    "l2_w1T": ((1024, 4096), "float32"),
    "l2_w2T": ((4096, 1024), "float32"),
    "l2_convw": ((128, 96), "float32"),
    "l2_convb": ((128, 32), "float32"),
}
DERIVED_INPUTS["l2_w1T"] = lambda ins: np.ascontiguousarray(
    np.asarray(ins["L2_ffn_w1"]).T.astype(np.float32))
DERIVED_INPUTS["l2_w2T"] = lambda ins: np.ascontiguousarray(
    np.asarray(ins["L2_ffn_w2"]).T.astype(np.float32))


def _l2_convw(ins):
    w = np.asarray(ins["L2_ffn_conv_w"]).astype(np.float32)
    out = np.zeros((128, 96), np.float32)
    for m in range(32):
        for j in range(3):
            out[:, m * 3 + j] = w[m * 128:(m + 1) * 128, 0, j]
    return out


DERIVED_INPUTS["l2_convw"] = _l2_convw
DERIVED_INPUTS["l2_convb"] = lambda ins: np.ascontiguousarray(
    np.asarray(ins["L2_ffn_conv_b"]).astype(np.float32).reshape(32, 128).T)

L2A_WEIGHTS = {
    "l2_wgT": ((1024, 1536), "float32"),
    "l2_wiT": ((1024, 1536), "float32"),
    "l2_lwgT": ((1536, 1536), "float32"),
    "l2_lwiT": ((1536, 1536), "float32"),
    "l2_woT": ((1536, 1024), "float32"),
    "l2a_cw": ((128, 48), "float32"),
    "l2a_cb": ((128, 12), "float32"),
    "l2a_lam": ((128, 12), "float32"),
    "l2a_bi": ((128, 12), "float32"),
    "l2a_bg": ((128, 12), "float32"),
    "l2_parf": ((128, 1), "float32"),
}
DERIVED_INPUTS["l2_wgT"] = lambda ins: np.ascontiguousarray(
    np.asarray(ins["L2_attn_Wg"]).T.astype(np.float32))
DERIVED_INPUTS["l2_wiT"] = lambda ins: np.ascontiguousarray(
    np.asarray(ins["L2_attn_Wi"]).T.astype(np.float32))
DERIVED_INPUTS["l2_lwgT"] = lambda ins: np.ascontiguousarray(
    np.asarray(ins["L2_attn_lru_Wg"]).T.astype(np.float32))
DERIVED_INPUTS["l2_lwiT"] = lambda ins: np.ascontiguousarray(
    np.asarray(ins["L2_attn_lru_Wi"]).T.astype(np.float32))
DERIVED_INPUTS["l2_woT"] = lambda ins: np.ascontiguousarray(
    np.asarray(ins["L2_attn_Wo"]).T.astype(np.float32))


def _l2a_cw(ins):
    w = np.asarray(ins["L2_attn_conv_w"]).astype(np.float32)
    out = np.zeros((128, 48), np.float32)
    for m in range(12):
        for j in range(4):
            out[:, 4 * m + j] = w[m * 128:(m + 1) * 128, 0, j]
    return out


DERIVED_INPUTS["l2a_cw"] = _l2a_cw
DERIVED_INPUTS["l2a_cb"] = lambda ins: np.ascontiguousarray(
    np.asarray(ins["L2_attn_conv_b"]).astype(np.float32).reshape(12, 128).T)
DERIVED_INPUTS["l2a_lam"] = lambda ins: np.ascontiguousarray(
    np.asarray(ins["L2_attn_lam"]).astype(np.float32).reshape(12, 128).T)
DERIVED_INPUTS["l2a_bi"] = lambda ins: np.ascontiguousarray(
    np.asarray(ins["L2_attn_lru_bi"]).astype(np.float32).reshape(12, 128).T)
DERIVED_INPUTS["l2a_bg"] = lambda ins: np.ascontiguousarray(
    np.asarray(ins["L2_attn_lru_bg"]).astype(np.float32).reshape(12, 128).T)


def _l2_parf(c):
    return np.full((128, 1), float(c % 2), np.float32)


L3F_WEIGHTS = {"l3_bias": ((128, 8), "float32")}
for _c in range(4):
    for _nm in ("t1r", "t2r", "t1i", "t2i"):
        L3F_WEIGHTS[f"l3_{_nm}_{_c}"] = ((512, 512), "float32")
for _c in range(4):
    DERIVED_INPUTS[f"l3_t1r_{_c}"] = (lambda ins, c=_c: _tile_t1(ins["L3_ffn_G1_R"], c))
    DERIVED_INPUTS[f"l3_t2r_{_c}"] = (lambda ins, c=_c: _tile_t2(ins["L3_ffn_G2_R"], c))
    DERIVED_INPUTS[f"l3_t1i_{_c}"] = (lambda ins, c=_c: _tile_t1(ins["L3_ffn_G1_I"], c))
    DERIVED_INPUTS[f"l3_t2i_{_c}"] = (lambda ins, c=_c: _tile_t2(ins["L3_ffn_G2_I"], c))
DERIVED_INPUTS["l3_bias"] = lambda ins: np.ascontiguousarray(
    np.concatenate([np.asarray(ins["L3_ffn_bias_real"]),
                    np.asarray(ins["L3_ffn_bias_imag"])]).astype(np.float32)
    .reshape(8, 128).T)


def run_blocks(key, block_names, wspecs, sspecs, x_shards, weights,
               extra_inputs=None):
    _NONCE[0] += 1
    nonce = _NONCE[0]
    base = _common_inputs(nonce)
    in_maps = []
    for c in range(8):
        m = {k: v for k, v in base[c].items() if k in wspecs or k == "cfg"}
        m["xin"] = x_shards[c]
        for k in wspecs:
            if k in ("rope_c", "rope_s", "p128"):
                m[k] = base[c][k]
            elif extra_inputs and k in extra_inputs:
                m[k] = extra_inputs[k] if not isinstance(extra_inputs[k], list) \
                    else extra_inputs[k][c]
            elif k in DERIVED_INPUTS:
                m[k] = DERIVED_INPUTS[k](weights)
            else:
                wv_ = np.ascontiguousarray(weights[k])
                if k in WEIGHT_TRANSFORMS:
                    wv_ = WEIGHT_TRANSFORMS[k](wv_)
                m[k] = wv_
        in_maps.append(m)
    if key not in _NC_CACHE:
        _NC_CACHE[key] = build_program(block_names, wspecs, sspecs)
    res = run_bass_kernel_spmd(_NC_CACHE[key], in_maps, core_ids=list(range(8)))
    return [res.results[c]["xout"] for c in range(8)]


def _shard_x(x):
    shards = []
    for c in range(8):
        b, p = c // 2, c % 2
        xt = np.ascontiguousarray(x[b].T)
        shards.append(np.ascontiguousarray(xt[:, own_positions(p)]))
    return shards


def _unshard_x(shards):
    out = np.zeros((B, S, D), np.float32)
    for c in range(8):
        b, p = c // 2, c % 2
        out[b, own_positions(p), :] = shards[c].T
    return out


ALL_BLOCKS = ["l0_attn", "l0_ffn", "l1_attn", "l1_ffn",
              "l2_attn", "l2_ffn", "l3_attn", "l3_ffn"]
ALL_WEIGHTS = {}
for _d in (L0A_WEIGHTS, L0F_WEIGHTS, L1A_WEIGHTS, L1F_WEIGHTS,
           L2A_WEIGHTS, L2F_WEIGHTS, L3A_WEIGHTS, L3F_WEIGHTS):
    for _k, _v in _d.items():
        assert _k not in ALL_WEIGHTS or ALL_WEIGHTS[_k] == _v, _k
        ALL_WEIGHTS[_k] = _v


def _all_extra():
    return {"l1_gq": [_l1_gq(c % 2) for c in range(8)],
            "l1_m01": [_l1_m01(c % 2) for c in range(8)],
            "l2_parf": [_l2_parf(c) for c in range(8)],
            "l3_m3": [_l3_m3(c % 2) for c in range(8)]}


def kernel(**inputs):
    x = np.asarray(inputs["L0_attn_x"], np.float32)
    shards = _shard_x(x)
    shards = run_blocks("fused", ALL_BLOCKS, ALL_WEIGHTS, [],
                        shards, inputs, extra_inputs=_all_extra())
    return _unshard_x(shards)


# version 07 (final)
# speedup vs baseline: 1.8430x; 1.0391x over previous
import numpy as np
import ml_dtypes
import concourse.bass as bass
import concourse.mybir as mybir
from concourse.bass_utils import run_bass_kernel_spmd

F32 = mybir.dt.float32
BF16 = mybir.dt.bfloat16
I32 = mybir.dt.int32
AF = mybir.ActivationFunctionType
ALU = mybir.AluOpType
AX = mybir.AxisListType
BF = ml_dtypes.bfloat16

D, S, B, T = 1024, 4096, 4, 2048
NCH, CH, DT = 4, 512, 8
LN_EPS = 1e-5
L0_SHIFT = 14.0
L3_SHIFT = 4.0
POOL_COLS = 42240

_NONCE = [int(np.random.default_rng().integers(1, 2 ** 24)) * 64]
L0_HEAD_PERM = [0, 4, 1, 5, 2, 6, 3, 7, 8, 12, 9, 13, 10, 14, 11, 15]


def own_positions(p):
    pos = []
    for k in range(NCH):
        base = 1024 * k + 512 * p
        pos.extend(range(base, base + CH))
    return np.array(pos, dtype=np.int64)



_DVE_OPS = {"tensor_tensor", "tensor_scalar", "tensor_copy", "tensor_add",
            "tensor_sub", "tensor_mul", "tensor_max", "tensor_relu",
            "scalar_tensor_tensor", "tensor_reduce", "reduce_max", "reduce_sum",
            "reciprocal", "memset", "memzero", "select", "tensor_tensor_scan",
            "stream_shuffle", "transpose", "copy_predicated", "bn_stats",
            "bn_aggr", "tensor_scalar_add", "tensor_scalar_mul",
            "tensor_scalar_max", "tensor_scalar_min", "tensor_scalar_sub"}
_ACT_OPS = {"activation", "copy", "add", "mul", "sqrt", "square", "sign",
            "memzero"}


class ChainedEngine:

    def __init__(self, cc, eng, semname, ops):
        self._cc, self._eng, self._sem, self._ops = cc, eng, semname, ops

    def __getattr__(self, name):
        eng = self._eng
        f = getattr(eng, name)
        if name not in self._ops:
            return f
        cc, semname = self._cc, self._sem

        def wrapped(*a, **k):
            cc.wait(eng, cc.tok(semname))
            inst = f(*a, **k)
            cc.inc(inst, semname)
            return inst
        return wrapped


class _ExitStack:
    def __init__(self):
        self._cms = []

    def push(self, cm):
        self._cms.append(cm)

    def close(self):
        for cm in reversed(self._cms):
            cm.__exit__(None, None, None)
        self._cms = []


class CC:
    def __init__(self, nc):
        self.nc = nc
        self._sems = {}
        self._cms = []
        self.flag_word = 0
        self.spin_id = 0

    def sem(self, name):
        if name not in self._sems:
            cm = self.nc.semaphore(name)
            h = cm.__enter__()
            self._cms.append(cm)
            self._sems[name] = [h, 0]
        return self._sems[name]

    def inc(self, inst, name, n=1):
        s = self.sem(name)
        inst.then_inc(s[0], n)
        s[1] += n
        return (name, s[1])

    def tok(self, name):
        return (name, self.sem(name)[1])

    def wait(self, eng, token):
        if token is None:
            return
        name, cnt = token
        if name == "multi":
            for t_ in cnt:
                self.wait(eng, t_)
            return
        if cnt > 0:
            eng.wait_ge(self._sems[name][0], cnt)


def _dma_on(cc, eng, sem, out, in_, slow=False):
    cc.wait(eng, cc.tok(sem))
    return cc.inc(eng.dma_start(out=out, in_=in_,
                                allow_slow_non_contiguous=slow), sem, 16)


def dma_w(cc, out, in_):
    cc._wrr = (getattr(cc, "_wrr", 0) + 1) % 2
    return _dma_on(cc, cc.nc.sync, f"w{cc._wrr}", out, in_)


def dma_g(cc, out, in_, slow=False):
    cc._grr = (getattr(cc, "_grr", 0) + 1) % 4
    return _dma_on(cc, cc.nc.gpsimd, f"g{cc._grr}", out, in_, slow=slow)


def dma_s(cc, out, in_):
    return _dma_on(cc, cc.nc.scalar, "s0", out, in_)


def gbar(cc, eng):
    for j in range(4):
        cc.wait(eng, cc.tok(f"g{j}"))
        cc.wait(eng, cc.tok(f"gx{j}"))


def gwait4(cc, eng):
    for j in range(4):
        cc.wait(eng, cc.tok(f"g{j}"))


def block_fence(cc):
    nc = cc.nc
    toks = ([cc.tok(nm) for nm in ("v", "a", "p")]
            + [cc.tok(f"g{j}") for j in range(4)]
            + [cc.tok(f"gx{j}") for j in range(4)]
            + [cc.tok("w0"), cc.tok("w1"), cc.tok("s0")])
    for eng in (nc.tensor, nc.vector, nc.scalar, nc.gpsimd, nc.sync):
        for tk in toks:
            cc.wait(eng, tk)


def percore(cc, fn):
    g = cc.nc.gpsimd
    for c in range(8):
        with g.If_eq(cc.me_reg, c):
            fn(c)


def ex_dma(cc, plan, slow=False):
    g = cc.nc.gpsimd
    s = cc.sem("gx")
    n = [None]

    sems = [cc.sem(f"gx{j}") for j in range(4)]
    n0 = [None]

    base = [cc.tok(f"gx{j}")[1] for j in range(4)]

    def body(c):
        pairs = plan(c)
        if n0[0] is None:
            n0[0] = len(pairs)
        assert len(pairs) == n0[0]
        uses = [0, 0, 0, 0]
        for j, (o, i) in enumerate(pairs):
            sj = j % 4
            if uses[sj] > 0:
                g.wait_ge(sems[sj][0], base[sj] + 16 * uses[sj])
            g.dma_start(out=o, in_=i,
                        allow_slow_non_contiguous=slow
                        ).then_inc(sems[sj][0], 16)
            uses[sj] += 1
    for j in range(min(4, 999)):
        cc.wait(g, cc.tok(f"gx{j}"))
    percore(cc, body)
    for j, (h, _) in enumerate(sems):
        pass
    cnt = [0, 0, 0, 0]
    for j in range(n0[0]):
        cnt[j % 4] += 16
    toks = []
    for j in range(4):
        if cnt[j]:
            sems[j][1] += cnt[j]
            toks.append((f"gx{j}", sems[j][1]))
    return ("multi", tuple(toks))


import os as _os
SIM_EXCHANGE = (_os.environ.get("BASS_SIM_FP64") == "1"
                or _os.environ.get("KERNEL_EXCHANGE") == "rdma")


def publish(cc, name, pairs, wait_tokens=()):
    g = cc.nc.gpsimd
    for t_ in wait_tokens:
        cc.wait(g, t_)
    srcs = [p if not isinstance(p, tuple) else p[0] for p in pairs]
    dt_ = srcs[0].dtype
    offs, o = [], 0
    for sp in srcs:
        offs.append(o)
        o += int(sp.shape[-1])
    if SIM_EXCHANGE:
        buf = cc.nc.dram_tensor(f"exb_{name}", [2, 128, o], dt_,
                                kind="Internal").ap()
        from concourse.libnrt import node_relative_to_routing_address
        from concourse.library_config import remote_dma as _RDMA_LIB
        from concourse.library_config import standard as _STD_LIB
        g.load_library(_RDMA_LIB)
        rs = cc.sem("rsem")
        ls = cc.sem("lsem")
        rdy = cc.sem("rdy")
        wtoks = ex_dma(cc, lambda c: [
            (buf[c % 2, :, offs[i]:offs[i] + int(srcs[i].shape[-1])], srcs[i])
            for i in range(len(srcs))])
        cc.wait(g, wtoks)

        def body0(c):
            peer = c ^ 1
            rid, pnc = node_relative_to_routing_address(peer, 8)
            g.remote_sem_update(rdy[0], ls[0], pid=pnc, routing_id=rid,
                                dma_engine_mask=0xFFFF)
        percore(cc, body0)
        g.trigger_dma()
        ls[1] += 16
        rdy[1] += 16
        cc.wait(g, ("rdy", rdy[1]))

        chunks = []
        for i, sp in enumerate(srcs):
            ncols = int(sp.shape[-1])
            for cstart in range(0, ncols, 2048):
                chunks.append((i, cstart, min(2048, ncols - cstart)))
        bounce = cc.ych
        for (i, cstart, n_) in chunks:

            def push(c, i=i, cstart=cstart, n_=n_):
                peer = c ^ 1
                rid, pnc = node_relative_to_routing_address(peer, 8)
                g.remote_dma(bounce[:, 0:n_], srcs[i][:, cstart:cstart + n_],
                             rs[0], ls[0], pid=pnc, routing_id=rid,
                             dma_engine_mask=0xFFFF)
            percore(cc, push)
            g.trigger_dma()
            rs[1] += 16
            ls[1] += 16
            cc.wait(g, ("rsem", rs[1]))
            ct = ex_dma(cc, lambda c, i=i, cstart=cstart, n_=n_: [
                (buf[(c % 2) ^ 1, :, offs[i] + cstart:offs[i] + cstart + n_],
                 bounce[:, 0:n_])])
            cc.wait(g, ct)

            def ack(c):
                peer = c ^ 1
                rid, pnc = node_relative_to_routing_address(peer, 8)
                g.remote_sem_update(rdy[0], ls[0], pid=pnc, routing_id=rid,
                                    dma_engine_mask=0xFFFF)
            percore(cc, ack)
            g.trigger_dma()
            ls[1] += 16
            rdy[1] += 16
            cc.wait(g, ("rdy", rdy[1]))
        g.load_library(_STD_LIB)
        return buf, offs, cc.tok("gx")
    buf = cc.nc.dram_tensor(f"exb_{name}", [2, 128, o], dt_,
                            kind="Internal", addr_space="Shared").ap()
    wtoks = ex_dma(cc, lambda c: [
        (buf[c % 2, :, offs[i]:offs[i] + int(srcs[i].shape[-1])], srcs[i])
        for i in range(len(srcs))])
    word = new_word(cc)
    post_flag(cc, word, [wtoks])
    spin_flag(cc, word)
    return buf, offs, cc.tok("gx")


def exchange(cc, name, pairs, wait_tokens=()):
    buf, offs, rtok = publish(cc, name, [p[1] for p in pairs], wait_tokens)
    g = cc.nc.gpsimd
    cc.wait(g, rtok)
    rdt = ex_dma(cc, lambda c: [
        (pairs[i][0],
         buf[(c % 2) ^ 1, :, offs[i]:offs[i] + int(pairs[i][1].shape[-1])])
        for i in range(len(pairs))])
    return rdt


def post_flag(cc, word, wait_tokens):
    g = cc.nc.gpsimd
    for t in wait_tokens:
        cc.wait(g, t)
    return ex_dma(cc, lambda c: [(cc.sh_flg[c, word:word + 1][None, :],
                                  cc.noncetile[0:1, 0:1])])


def spin_flag(cc, word):
    g = cc.nc.gpsimd
    sid = cc.spin_id
    cc.spin_id += 1
    assert sid < 32
    sp = cc.sem(f"sp{sid}")[0]
    ft = cc.flagtile[0:1, sid:sid + 1]

    def body(c):
        peer = c ^ 1
        with g.register(f"c{sid}_{c}") as creg, g.register(f"f{sid}_{c}") as fv:
            g.reg_mov(creg, 1)
            with g.While(creg):
                g.dma_start(out=ft, in_=cc.sh_flg[peer, word:word + 1][None, :]
                            ).then_inc(sp, 16)
                g.wait_ge(sp, 16)
                g.load(fv, ft)
                g.reg_sub(creg, fv, cc.nonce_reg)
    percore(cc, body)


def new_word(cc):
    w = cc.flag_word
    cc.flag_word += 1
    assert w < 63
    return w


def pv(cc, off, ncols):
    return cc.pool[:, off:off + ncols]


def chain(cc, eng, inst, name):
    tok = cc.inc(inst, name)
    cc.wait(eng, tok)
    return tok


def emit_ln_cols(cc, src_t, dst_t, mode, c0, src_c0=None, src_is_bf16=False,
                 wait_tokens=()):
    nc = cc.nc
    t, v, a = nc.tensor, cc.V, cc.A
    s0 = c0 if src_c0 is None else src_c0
    rows_guard = cc.tok("v")
    for tk in wait_tokens:
        cc.wait(v, tk)
        cc.wait(a, tk)
    mm_toks = {}
    for ti in range(DT):
        src = src_t(ti)[:, s0:s0 + 512]
        if src_is_bf16:
            xb = src
        else:
            xb = cc.xb2[:, (ti % 2) * 512:(ti % 2) * 512 + 512]
            if ti >= 2:
                cc.wait(v, mm_toks[ti - 2])
            v.tensor_copy(xb, src)
        sq = cc.sq2[:, (ti % 2) * 512:(ti % 2) * 512 + 512]
        if ti >= 2:
            cc.wait(a, mm_toks[ti - 2])
        a.activation(sq, src, AF.Square)
        ta = cc.tok("a")
        if not src_is_bf16:
            cc.wait(t, cc.tok("v"))
        cc.wait(t, ta)
        if ti == 0:
            cc.wait(t, rows_guard)
        t.matmul(cc.row0[0:1, :], cc.onesc[:, 0:1], xb,
                 start=(ti == 0), stop=(ti == DT - 1))
        i_ = t.matmul(cc.row1[0:1, :], cc.onesc[:, 0:1], sq,
                      start=(ti == 0), stop=(ti == DT - 1))
        mm_toks[ti] = cc.inc(i_, "p")
    tok_stats = mm_toks[DT - 1]
    mean = cc.rowf[0:1, 0:512]
    var = cc.rowf[0:1, 512:1024]
    rstd = cc.rowf[0:1, 1024:1536]
    cc.wait(v, tok_stats)
    v.tensor_scalar(mean, cc.row0[0:1, :], 1.0 / D, None, ALU.mult)
    v.tensor_tensor(var, mean, mean, ALU.mult)
    v.scalar_tensor_tensor(var, cc.row1[0:1, :], 1.0 / D, var,
                           ALU.mult, ALU.subtract)
    tv = cc.tok("v")
    cc.wait(a, tv)
    a.activation(rstd, var, AF.Sqrt, bias=cc.constt[0:1, 0:1])
    ta = cc.tok("a")
    cc.wait(v, ta)
    v.reciprocal(rstd, rstd)
    v.tensor_copy(cc.rowR[0:1, :], rstd)
    v.scalar_tensor_tensor(mean, mean, -1.0, rstd, ALU.mult, ALU.mult)
    v.tensor_copy(cc.rowM[0:1, :], mean)
    tv = cc.tok("v")
    cc.wait(t, tv)
    t.matmul(cc.big0[:, :], cc.ones1[0:1, :], cc.rowR[0:1, :], start=True, stop=True)
    i_ = t.matmul(cc.big1[:, :], cc.ones1[0:1, :], cc.rowM[0:1, :], start=True, stop=True)
    tok_bc = cc.inc(i_, "p")
    cc.wait(v, tok_bc)
    for tp_ in range(0, DT, 2):
        srcs = [src_t(tp_ + j)[:, s0:s0 + 512] for j in (0, 1)]
        dsts = [dst_t(tp_ + j)[:, c0:c0 + 512] for j in (0, 1)]
        tmps = [cc.lntmp[:, 0:512], cc.lntmp2[:, 0:512]]
        v.tensor_tensor(tmps[0], srcs[0], cc.big0[:, :], ALU.mult)
        v.tensor_tensor(tmps[1], srcs[1], cc.big0[:, :], ALU.mult)
        if mode == "set":
            v.tensor_tensor(dsts[0], tmps[0], cc.big1[:, :], ALU.add)
            v.tensor_tensor(dsts[1], tmps[1], cc.big1[:, :], ALU.add)
        else:
            v.tensor_tensor(tmps[0], tmps[0], cc.big1[:, :], ALU.add)
            v.tensor_tensor(tmps[1], tmps[1], cc.big1[:, :], ALU.add)
            v.tensor_tensor(dsts[0], dsts[0], tmps[0], ALU.add)
            v.tensor_tensor(dsts[1], dsts[1], tmps[1], ALU.add)
    return cc.tok("v")


def xres_t(cc, i):
    return cc.xres[:, i * T:(i + 1) * T]


def xn_t(cc, i):
    return cc.xn[:, i * T:(i + 1) * T]


def emit_ln_in(cc, wait_tokens=()):
    tk = None
    for c0 in range(0, T, 512):
        tk = emit_ln_cols(cc, lambda i: xres_t(cc, i), lambda i: xn_t(cc, i),
                          "set", c0, wait_tokens=wait_tokens)
    return tk


def emit_l0_attn(cc, W):
    import os
    PHASE = int(os.environ.get("L0A_PHASE", "9"))
    nc = cc.nc
    t, v, a, g = nc.tensor, cc.V, cc.A, nc.gpsimd

    tok_xn = emit_ln_in(cc)
    if PHASE <= 0:
        return tok_xn

    wq = pv(cc, 0, 8192)
    Ctab = pv(cc, 8192, T)
    Stab = pv(cc, 10240, T)
    P128 = pv(cc, 12288, 128)
    wk = pv(cc, 12416, 2048)
    wv = pv(cc, 14464, 2048)
    Ksb = pv(cc, 16512, 2 * T)
    Vsb = pv(cc, 20608, 16 * 260)
    Qf = pv(cc, 24768, 8 * T)

    def dma_wt(out, in_):
        return cc.inc(cc.nc.sync.dma_start(out=out, in_=in_), "wt", 16)
    dma_wt(Ctab, W["rope_c"][:, :])
    dma_wt(Stab, W["rope_s"][:, :])
    tok_tab3 = dma_wt(P128, W["p128"][:, :])
    tok_tab = [tok_tab3, tok_tab3, tok_tab3]
    tokw = []
    for dt_ in range(DT):
        tokw.append(dma_g(cc, wq[:, dt_ * 1024:(dt_ + 1) * 1024],
                          W["L0_attn_w_q"][dt_ * 128:(dt_ + 1) * 128, :]))
        tokw.append(dma_g(cc, wk[:, dt_ * 256:(dt_ + 1) * 256],
                          W["L0_attn_w_k"][dt_ * 128:(dt_ + 1) * 128, :]))
        tokw.append(dma_g(cc, wv[:, dt_ * 256:(dt_ + 1) * 256],
                          W["L0_attn_w_v"][dt_ * 128:(dt_ + 1) * 128, :]))
    tok_wlast = tokw[-1]

    v.memset(Vsb, 1.0)
    tok_vset = cc.tok("v")

    def rope_tile(dst, c0):
        cc.wait(a, cc.tok("p"))
        a.activation(cc.qswp[:, 0:512], cc.big2[:, :], AF.Copy)
        ta = cc.tok("a")
        cc.wait(t, ta)
        cc.wait(t, tok_tab[2])
        tp = cc.inc(t.matmul(cc.big3[:, :], P128, cc.qswp[:, 0:512],
                             start=True, stop=True), "p")
        cc.wait(v, tp)
        cc.wait(v, tok_tab[0])
        cc.wait(v, tok_tab[1])
        v.tensor_tensor(cc.lntmp[:, 0:512], cc.big2[:, :], Ctab[:, c0:c0 + 512],
                        ALU.mult)
        v.tensor_tensor(cc.lntmp2[:, 0:512], cc.big3[:, :],
                        Stab[:, c0:c0 + 512], ALU.mult)
        v.tensor_tensor(dst, cc.lntmp[:, 0:512], cc.lntmp2[:, 0:512], ALU.add)
        return cc.tok("v")

    tok_k = None
    for kt in range(2):
        for c0 in range(0, T, 512):
            for dt_ in range(DT):
                cc.wait(t, tok_wlast)
                cc.wait(t, tok_xn)
                if dt_ == 0:
                    cc.wait(t, tok_k)
                i_ = t.matmul(cc.big2[:, :],
                              wk[:, dt_ * 256 + kt * 128:dt_ * 256 + (kt + 1) * 128],
                              xn_t(cc, dt_)[:, c0:c0 + 512],
                              start=(dt_ == 0), stop=(dt_ == DT - 1))
            cc.inc(i_, "p")
            tok_k = rope_tile(Ksb[:, kt * T + c0:kt * T + c0 + 512], c0)

    tok_vv = None
    for tt in range(16):
        for dt_ in range(DT):
            cc.wait(t, tok_xn)
            cc.wait(t, tok_wlast)
            if dt_ == 0:
                cc.wait(t, tok_k)
                cc.wait(t, tok_vv)
            i_ = t.matmul(cc.big2[:, 0:256],
                          xn_t(cc, dt_)[:, tt * 128:(tt + 1) * 128],
                          wv[:, dt_ * 256:(dt_ + 1) * 256],
                          start=(dt_ == 0), stop=(dt_ == DT - 1))
        tp = cc.inc(i_, "p")
        cc.wait(a, tp)
        cc.wait(a, tok_vset)
        for kh in range(4):
            a.activation(Vsb[:, tt * 260 + kh * 65:tt * 260 + kh * 65 + 64],
                         cc.big2[:, kh * 64:(kh + 1) * 64], AF.Copy)
        tok_vv = cc.tok("a")

    tok_q = None
    for qt in range(DT):
        for c0 in range(0, T, 512):
            for dt_ in range(DT):
                cc.wait(t, tok_wlast)
                cc.wait(t, tok_xn)
                if dt_ == 0:
                    cc.wait(t, tok_q if tok_q is not None else tok_vv)
                i_ = t.matmul(cc.big2[:, :],
                              wq[:, dt_ * 1024 + qt * 128:dt_ * 1024 + (qt + 1) * 128],
                              xn_t(cc, dt_)[:, c0:c0 + 512],
                              start=(dt_ == 0), stop=(dt_ == DT - 1))
            cc.inc(i_, "p")
            tok_q = rope_tile(Qf[:, qt * T + c0:qt * T + c0 + 512], c0)

    if PHASE <= 1:
        return tok_q
    cc.wait(g, tok_q)
    Kpe = pv(cc, 0, 2 * T)
    Vpe = pv(cc, 4096, 16 * 260)
    woring = pv(cc, 8256, 2048)
    Pt0 = pv(cc, 10304, 512)
    Pt1 = pv(cc, 10816, 512)
    AVb = pv(cc, 11328, 8 * 512)
    ychunk = cc.ych
    tok_rd = exchange(cc, "l0kv", [
        (Kpe[:, 0:T], Ksb[:, 0:T]),
        (Kpe[:, T:2 * T], Ksb[:, T:2 * T]),
        (Vpe, Vsb),
    ], wait_tokens=(tok_k, tok_vv))

    if PHASE <= 2:
        return tok_rd
    ktile_src = [("own", k) for k in range(16)] + [("peer", k) for k in range(16)]
    prev_ln = None
    slab_used = [None, None]
    evac_toks = [None, None]
    for qc in range(NCH):
        c0 = qc * CH
        prev_evac = None
        for qs in range(16):
            qh = L0_HEAD_PERM[qs]
            kh = qh // 4
            qtile, qb = qs // 2, 64 * (qs % 2)
            ktt, kb = kh // 2, 64 * (kh % 2)
            assert qb == kb
            exp_toks, av_toks = {}, {}
            for ki, (srcid, kt) in enumerate(ktile_src):
                K_ = Ksb if srcid == "own" else Kpe
                V_ = Vsb if srcid == "own" else Vpe
                ps_s = cc.big0 if ki % 2 == 0 else cc.big1
                pt = Pt0 if ki % 2 == 0 else Pt1
                cc.wait(t, tok_q)
                if srcid == "peer":
                    cc.wait(t, tok_rd)
                if ki >= 2:
                    cc.wait(t, exp_toks[ki - 2])
                if ki == 0 and prev_evac is not None:
                    cc.wait(t, prev_evac)
                if ki == 0 and prev_ln is not None:
                    cc.wait(t, prev_ln)
                i_ = t.matmul(ps_s[:, :],
                              K_[kb:kb + 64, ktt * T + kt * 128:ktt * T + (kt + 1) * 128],
                              Qf[qb:qb + 64, qtile * T + c0:qtile * T + c0 + 512],
                              start=True, stop=True)
                ts_ = cc.inc(i_, "p")
                cc.wait(a, ts_)
                if ki >= 2:
                    cc.wait(a, av_toks[ki - 2])
                a.activation(pt, ps_s[:, :], AF.Exp, scale=0.125,
                             bias=cc.constt[:, 1:2])
                exp_toks[ki] = cc.tok("a")
                cc.wait(t, exp_toks[ki])
                i_ = t.matmul(cc.big2[0:65, :],
                              V_[:, kt * 260 + kh * 65:kt * 260 + kh * 65 + 65],
                              pt, start=(ki == 0), stop=(ki == 31))
                av_toks[ki] = cc.inc(i_, "p")
            cc.wait(v, av_toks[31])
            rec = cc.rowf[0:1, 0:512]
            v.reciprocal(rec, cc.big2[64:65, :])
            v.tensor_copy(cc.rowR[0:1, :], rec)
            tv = cc.tok("v")
            cc.wait(t, tv)
            tb = cc.inc(t.matmul(cc.big3[:, :], cc.ones1[0:1, :],
                                 cc.rowR[0:1, :], start=True, stop=True), "p")
            cc.wait(a, av_toks[31])
            cc.wait(a, prev_evac)
            a.activation(cc.qswp[0:64, 0:512], cc.big2[0:64, :], AF.Copy)
            ta2 = cc.tok("a")
            cc.wait(v, tb)
            cc.wait(v, ta2)
            v.tensor_tensor(AVb[qb:qb + 64, qtile * 512:(qtile + 1) * 512],
                            cc.qswp[0:64, 0:512], cc.big3[0:64, :], ALU.mult)
            prev_evac = cc.tok("v")
        tok_av = cc.tok("v")
        tok_y = None
        for ot in range(DT):
            slab = woring[:, (ot % 2) * 1024:(ot % 2) * 1024 + 1024]
            cc.wait(g, slab_used[ot % 2])
            rsem = "r0" if ot % 2 == 0 else "r1"
            tok_l = None
            for dt_ in range(DT):
                tok_l = cc.inc(g.dma_start(
                    out=slab[:, dt_ * 128:(dt_ + 1) * 128],
                    in_=W["L0_attn_w_o"][dt_ * 128:(dt_ + 1) * 128,
                                         ot * 128:(ot + 1) * 128]), rsem, 16)
            ps_o = cc.big4 if ot % 2 == 0 else cc.big5
            for dvt in range(DT):
                cc.wait(t, tok_av)
                cc.wait(t, tok_l)
                if dvt == 0:
                    cc.wait(t, evac_toks[ot % 2])
                if prev_ln is not None:
                    cc.wait(t, prev_ln)
                i_ = t.matmul(ps_o[:, :], slab[:, dvt * 128:(dvt + 1) * 128],
                              AVb[:, dvt * 512:(dvt + 1) * 512],
                              start=(dvt == 0), stop=(dvt == DT - 1))
            tp = cc.inc(i_, "p")
            slab_used[ot % 2] = tp
            cc.wait(a, tp)
            cc.wait(a, prev_ln)
            a.activation(ychunk[:, ot * 512:(ot + 1) * 512], ps_o[:, :], AF.Copy)
            tok_y = cc.tok("a")
            evac_toks[ot % 2] = tok_y
        prev_ln = emit_ln_cols(cc, lambda i: ychunk[:, i * 512:(i + 1) * 512],
                               lambda i: xres_t(cc, i), "add", c0, src_c0=0,
                               src_is_bf16=True, wait_tokens=(tok_y,))
    return prev_ln




def emit_l1_ffn(cc, W):
    nc = cc.nc
    t, v, a, g = nc.tensor, cc.V, cc.A, nc.gpsimd
    tok_xn = emit_ln_in(cc)
    HT = 24
    wgT = pv(cc, 0, 24576)
    wuring = pv(cc, 24576, 2048)
    wdring = pv(cc, 26624, 2048)
    h = pv(cc, 28672, HT * 512)
    tokw = None
    for dt_ in range(DT):
        tokw = dma_g(cc, wgT[:, dt_ * 3072:(dt_ + 1) * 3072],
                     W["l1_wgT"][dt_ * 128:(dt_ + 1) * 128, :])
    tok_wg = tokw

    prev_ln = None
    slabu = [None, None]
    slabd = [None, None]
    psev = [None, None]
    dvev = [None, None]
    yevac = [None, None]
    for qc in range(NCH):
        c0 = qc * CH
        for m in range(HT):
            par = m % 2
            psg = cc.big0 if par == 0 else cc.big1
            psu = cc.big2 if par == 0 else cc.big3
            tA = cc.lntmp[:, 0:512] if par == 0 else cc.xb2[:, 0:512]
            tB = cc.lntmp2[:, 0:512] if par == 0 else cc.sq2[:, 0:512]
            slab = wuring[:, par * 1024:par * 1024 + 1024]
            cc.wait(g, slabu[par])
            tok_u = dma_g(cc, slab.rearrange("p (k m) -> p k m", k=DT),
                          W["l1_wuT"][:, m * 128:(m + 1) * 128]
                          .rearrange("(k p) m -> p k m", p=128))
            i_ = None
            for dt_ in range(DT):
                cc.wait(t, tok_wg)
                cc.wait(t, tok_xn)
                if dt_ == 0:
                    cc.wait(t, psev[par])
                    if prev_ln is not None:
                        cc.wait(t, prev_ln)
                t.matmul(psg[:, :],
                         wgT[:, dt_ * 3072 + m * 128:dt_ * 3072 + (m + 1) * 128],
                         xn_t(cc, dt_)[:, c0:c0 + 512],
                         start=(dt_ == 0), stop=(dt_ == DT - 1))
                cc.wait(t, tok_u)
                i_ = t.matmul(psu[:, :], slab[:, dt_ * 128:(dt_ + 1) * 128],
                              xn_t(cc, dt_)[:, c0:c0 + 512],
                              start=(dt_ == 0), stop=(dt_ == DT - 1))
            tokp = cc.inc(i_, "p")
            slabu[par] = tokp
            cc.wait(a, tokp)
            cc.wait(a, dvev[par])
            a.activation(tA, psg[:, :], AF.Silu, scale=1.0 / 127.0)
            a.activation(tB, psu[:, :], AF.Copy, scale=1.0 / 127.0)
            psev[par] = cc.tok("a")
            cc.wait(v, psev[par])
            v.tensor_tensor(h[:, m * 512:(m + 1) * 512], tA, tB, ALU.mult)
            dvev[par] = cc.tok("v")
        tok_h = cc.tok("v")
        tok_y = None
        for ot in range(DT):
            ps_o = cc.big4 if ot % 2 == 0 else cc.big5
            slab = wdring[:, (ot % 2) * 1024:(ot % 2) * 1024 + 1024]
            i_ = None
            for sub in range(3):
                cc.wait(g, slabd[ot % 2])
                tok_d = dma_g(cc, slab.rearrange("p (k m) -> p k m", k=8),
                              W["l1_wdT"][sub * 1024:(sub + 1) * 1024,
                                          ot * 128:(ot + 1) * 128]
                              .rearrange("(k p) m -> p k m", p=128))
                for kk in range(8):
                    k = sub * 8 + kk
                    cc.wait(t, tok_d)
                    cc.wait(t, tok_h)
                    if k == 0:
                        cc.wait(t, yevac[ot % 2])
                    i_ = t.matmul(ps_o[:, :], slab[:, kk * 128:(kk + 1) * 128],
                                  h[:, k * 512:(k + 1) * 512],
                                  start=(k == 0), stop=(k == HT - 1))
                slabd[ot % 2] = cc.inc(i_, "p")
            tokp = slabd[ot % 2]
            cc.wait(a, tokp)
            if prev_ln is not None:
                cc.wait(a, prev_ln)
            a.activation(cc.ych[:, ot * 512:(ot + 1) * 512], ps_o[:, :],
                         AF.Copy, scale=1.0 / 127.0)
            tok_y = cc.tok("a")
            yevac[ot % 2] = tok_y
        prev_ln = emit_ln_cols(cc, lambda i: cc.ych[:, i * 512:(i + 1) * 512],
                               lambda i: xres_t(cc, i), "add", c0, src_c0=0,
                               src_is_bf16=True, wait_tokens=(tok_y,))
    return prev_ln



def emit_l3_ffn(cc, W):
    nc = cc.nc
    t, v, a, g = nc.tensor, cc.V, cc.A, nc.gpsimd
    tok_xn = emit_ln_in(cc)
    WcT = pv(cc, 0, 8192)
    stag = pv(cc, 8192, 2048)
    biascol = pv(cc, 10240, 16)
    accA = cc.lntmp[:, 0:512]
    accB = cc.lntmp2[:, 0:512]
    tok_b = dma_g(cc, cc.constt[:, 8:16], W["l3_bias"][:, :])
    for part in range(2):
        pre = "l3_t1r" if part == 0 else "l3_t1i"
        pre2 = "l3_t2r" if part == 0 else "l3_t2i"
        for rt in range(4):
            acc = accA
            for c4 in range(4):
                s1 = stag[:, 0:512]
                s2 = stag[:, 512:1024]
                tokl = None
                cc.wait(g, cc.tok("v"))
                dma_g(cc, s1, W[f"{pre}_{c4}"][rt * 128:(rt + 1) * 128, :])
                tokl = dma_g(cc, s2, W[f"{pre2}_{c4}"][rt * 128:(rt + 1) * 128, :])
                cc.wait(v, tokl)
                if c4 == 0:
                    v.tensor_tensor(acc, s1, s2, ALU.mult)
                else:
                    v.tensor_tensor(accB, s1, s2, ALU.mult)
                    v.tensor_tensor(acc, acc, accB, ALU.add)
            if part == 0:
                v.tensor_copy(WcT[:, rt * 1024:rt * 1024 + 512], acc)
                v.tensor_copy(WcT[:, (rt + 4) * 1024 + 512:(rt + 5) * 1024], acc)
            else:
                v.tensor_copy(WcT[:, rt * 1024 + 512:(rt + 1) * 1024], acc)
                v.tensor_scalar(WcT[:, (rt + 4) * 1024:(rt + 4) * 1024 + 512],
                                acc, -1.0, None, ALU.mult)
    tok_w = cc.tok("v")

    prev_ln = None
    yevac = [None, None]
    for qc in range(NCH):
        c0 = qc * CH
        tok_y = None
        for ot in range(DT):
            ps_o = cc.big4 if ot % 2 == 0 else cc.big5
            i_ = None
            for dt_ in range(DT):
                cc.wait(t, tok_w)
                cc.wait(t, tok_xn)
                if dt_ == 0:
                    cc.wait(t, yevac[ot % 2])
                    if prev_ln is not None:
                        cc.wait(t, prev_ln)
                i_ = t.matmul(ps_o[:, :],
                              WcT[:, dt_ * 1024 + ot * 128:dt_ * 1024 + (ot + 1) * 128],
                              xn_t(cc, dt_)[:, c0:c0 + 512],
                              start=(dt_ == 0), stop=(dt_ == DT - 1))
            tokp = cc.inc(i_, "p")
            cc.wait(v, tokp)
            cc.wait(v, tok_b)
            if prev_ln is not None:
                cc.wait(v, prev_ln)
            v.tensor_scalar(cc.ych[:, ot * 512:(ot + 1) * 512], ps_o[:, :],
                            cc.constt[:, 8 + ot:9 + ot], None, ALU.add)
            tok_y = cc.tok("v")
            yevac[ot % 2] = tok_y
        prev_ln = emit_ln_cols(cc, lambda i: cc.ych[:, i * 512:(i + 1) * 512],
                               lambda i: xres_t(cc, i), "add", c0, src_c0=0,
                               src_is_bf16=True, wait_tokens=(tok_y,))
    return prev_ln


def emit_l0_ffn(cc, W):
    nc = cc.nc
    t, v, a, g = nc.tensor, cc.V, cc.A, nc.gpsimd
    tok_xn = emit_ln_in(cc)
    HT = 32
    ha = pv(cc, 0, HT * 512)
    hb = pv(cc, 16384, HT * 512)
    stream = pv(cc, 32768, 8192)
    tok_s = dma_g(cc, cc.scl[:, :], W["l0_scales"][:, :])

    prev_ln = None
    psev = [None, None]
    dvev = [None, None]
    yevac = [None, None]
    for qc in range(NCH):
        c0 = qc * CH
        ring_tok = [cc.tok("p"), cc.tok("p")]
        for m in range(HT):
            par = m % 2
            psg = cc.big0 if par == 0 else cc.big1
            psu = cc.big2 if par == 0 else cc.big3
            tA = cc.lntmp[:, 0:512] if par == 0 else cc.xb2[:, 0:512]
            tB = cc.lntmp2[:, 0:512] if par == 0 else cc.sq2[:, 0:512]
            s1m = cc.scl[:, m:m + 1]
            s2m = cc.scl[:, 32 + m:33 + m]
            w1s = stream[:, par * 2048:par * 2048 + 1024]
            w2s = stream[:, par * 2048 + 1024:par * 2048 + 2048]
            cc.wait(g, ring_tok[par])
            dma_g(cc, w1s.rearrange("p (k m) -> p k m", k=DT),
                  W["l0_w1T"][:, m * 128:(m + 1) * 128]
                  .rearrange("(k p) m -> p k m", p=128))
            tok_l = dma_g(cc, w2s.rearrange("p (k m) -> p k m", k=DT),
                          W["l0_w2T"][:, m * 128:(m + 1) * 128]
                          .rearrange("(k p) m -> p k m", p=128))
            i_ = None
            for dt_ in range(DT):
                cc.wait(t, tok_l)
                cc.wait(t, tok_xn)
                if dt_ == 0:
                    cc.wait(t, psev[par])
                    if prev_ln is not None:
                        cc.wait(t, prev_ln)
                t.matmul(psg[:, :], w1s[:, dt_ * 128:(dt_ + 1) * 128],
                         xn_t(cc, dt_)[:, c0:c0 + 512],
                         start=(dt_ == 0), stop=(dt_ == DT - 1))
                i_ = t.matmul(psu[:, :], w2s[:, dt_ * 128:(dt_ + 1) * 128],
                              xn_t(cc, dt_)[:, c0:c0 + 512],
                              start=(dt_ == 0), stop=(dt_ == DT - 1))
            tokp = cc.inc(i_, "p")
            ring_tok[par] = tokp
            cc.wait(a, tokp)
            cc.wait(a, tok_s)
            cc.wait(a, dvev[par])
            a.activation(tA, psg[:, :], AF.Silu, scale=s1m)
            a.activation(tB, psu[:, :], AF.Copy, scale=s2m)
            psev[par] = cc.tok("a")
            cc.wait(v, psev[par])
            v.tensor_tensor(ha[:, m * 512:(m + 1) * 512], tA, tB, ALU.mult)
            dvev[par] = cc.tok("v")
        for stage, (wname, scoff, src, dst) in enumerate(
                [("l0_h1T", 64, ha, hb), ("l0_h2T", 96, hb, ha)]):
            tok_src = cc.tok("v") if stage == 0 else cc.tok("a")
            ring_tok = [cc.tok("p"), cc.tok("p")]
            for m in range(HT):
                par = m % 2
                ps = cc.big0 if par == 0 else cc.big1
                slab = stream[:, par * 4096:par * 4096 + 4096]
                cc.wait(g, ring_tok[par])
                tok_l = dma_g(cc, slab.rearrange("p (k m) -> p k m", k=HT),
                              W[wname][:, m * 128:(m + 1) * 128]
                              .rearrange("(k p) m -> p k m", p=128))
                i_ = None
                for k in range(HT):
                    cc.wait(t, tok_l)
                    cc.wait(t, tok_src)
                    if k == 0:
                        cc.wait(t, psev[par])
                    i_ = t.matmul(ps[:, :], slab[:, k * 128:(k + 1) * 128],
                                  src[:, k * 512:(k + 1) * 512],
                                  start=(k == 0), stop=(k == HT - 1))
                tokp = cc.inc(i_, "p")
                ring_tok[par] = tokp
                cc.wait(a, tokp)
                cc.wait(a, cc.tok("v"))
                a.activation(dst[:, m * 512:(m + 1) * 512], ps[:, :], AF.Gelu,
                             scale=cc.scl[:, scoff + m:scoff + m + 1])
                psev[par] = cc.tok("a")
        tok_hc = cc.tok("a")
        ring_tok = [cc.tok("p"), cc.tok("p")]
        tok_y = None
        for ot in range(DT):
            par = ot % 2
            ps_o = cc.big4 if par == 0 else cc.big5
            slab = stream[:, par * 4096:par * 4096 + 4096]
            cc.wait(g, ring_tok[par])
            tok_l = dma_g(cc, slab.rearrange("p (k m) -> p k m", k=HT),
                          W["l0_w3T"][:, ot * 128:(ot + 1) * 128]
                          .rearrange("(k p) m -> p k m", p=128))
            i_ = None
            for k in range(HT):
                cc.wait(t, tok_l)
                cc.wait(t, tok_hc)
                if k == 0:
                    cc.wait(t, yevac[par])
                i_ = t.matmul(ps_o[:, :], slab[:, k * 128:(k + 1) * 128],
                              ha[:, k * 512:(k + 1) * 512],
                              start=(k == 0), stop=(k == HT - 1))
            tokp = cc.inc(i_, "p")
            ring_tok[par] = tokp
            cc.wait(a, tokp)
            if prev_ln is not None:
                cc.wait(a, prev_ln)
            a.activation(cc.ych[:, ot * 512:(ot + 1) * 512], ps_o[:, :],
                         AF.Copy, scale=cc.scl[:, 128 + ot:129 + ot])
            tok_y = cc.tok("a")
            yevac[par] = tok_y
        prev_ln = emit_ln_cols(cc, lambda i: cc.ych[:, i * 512:(i + 1) * 512],
                               lambda i: xres_t(cc, i), "add", c0, src_c0=0,
                               src_is_bf16=True, wait_tokens=(tok_y,))
    return prev_ln


def emit_l3_attn(cc, W):
    nc = cc.nc
    t, v, a, g = nc.tensor, cc.V, cc.A, nc.gpsimd
    p_par = None
    tok_xn = emit_ln_in(cc)
    Ksb = pv(cc, 0, 8 * T)
    Vsb = pv(cc, 16384, 16 * 1040)
    wq2 = pv(cc, 40960, 64)
    wk2 = pv(cc, 41024, 64)
    wv2 = pv(cc, 41088, 64)
    m3c = pv(cc, 33024, 1920)
    tok_w = [dma_g(cc, wq2, W["l3_wq2"][:, :]),
             dma_g(cc, wk2, W["l3_wk2"][:, :]),
             dma_g(cc, wv2, W["l3_wv2"][:, :]),
             dma_w(cc, m3c, W["l3_m3"][:, :])]
    v.memset(Vsb, 1.0)
    tok_vset = cc.tok("v")

    tok_k = None
    for h in range(16):
        qb = 64 * (h % 2)
        ht = h // 2
        for c0 in range(0, T, 512):
            ps = cc.big0 if (c0 // 512) % 2 == 0 else cc.big1
            cc.wait(t, tok_w[1])
            cc.wait(t, tok_xn)
            cc.wait(t, cc.tok("a"))
            i_ = t.matmul(ps[0:64, :], wk2[qb:qb + 64, :],
                          xn_t(cc, ht)[qb:qb + 64, c0:c0 + 512],
                          start=True, stop=True)
            tp = cc.inc(i_, "p")
            cc.wait(a, tp)
            a.activation(Ksb[:, ht * T + c0:ht * T + c0 + 512][qb:qb + 64, :],
                         ps[0:64, :], AF.Copy)
            tok_k = cc.tok("a")
    tok_vv = None
    for h in range(16):
        qb = 64 * (h % 2)
        ht = h // 2
        for tt in range(16):
            ps = cc.big2 if tt % 2 == 0 else cc.big3
            cc.wait(t, tok_w[2])
            cc.wait(t, tok_xn)
            cc.wait(t, cc.tok("a"))
            i_ = t.matmul(ps[:, 0:64],
                          xn_t(cc, ht)[qb:qb + 64, tt * 128:(tt + 1) * 128],
                          wv2[qb:qb + 64, :], start=True, stop=True)
            tp = cc.inc(i_, "p")
            cc.wait(a, tp)
            cc.wait(a, tok_vset)
            a.activation(Vsb[:, tt * 1040 + h * 65:tt * 1040 + h * 65 + 64],
                         ps[:, 0:64], AF.Copy)
            tok_vv = cc.tok("a")

    srcs = [Ksb[:, i * T:(i + 1) * T] for i in range(8)] + \
           [Vsb[:, i * 1040:(i + 1) * 1040] for i in range(16)]
    buf, offs, rtok = publish(cc, "l3kv", srcs, wait_tokens=(tok_k, tok_vv))
    koff = offs[:8]
    voff = offs[8:]

    Qf = pv(cc, 0, 8 * T)
    cc.wait(a, rtok)
    tok_q = None
    for h in range(16):
        qb = 64 * (h % 2)
        ht = h // 2
        for c0 in range(0, T, 512):
            ps = cc.big0 if (c0 // 512) % 2 == 0 else cc.big1
            cc.wait(t, tok_w[0])
            cc.wait(t, cc.tok("a"))
            i_ = t.matmul(ps[0:64, :], wq2[qb:qb + 64, :],
                          xn_t(cc, ht)[qb:qb + 64, c0:c0 + 512],
                          start=True, stop=True)
            tp = cc.inc(i_, "p")
            cc.wait(a, tp)
            a.activation(Qf[:, ht * T + c0:ht * T + c0 + 512][qb:qb + 64, :],
                         ps[0:64, :], AF.Copy)
            tok_q = cc.tok("a")

    kring = pv(cc, 16384, 1024)
    vring = pv(cc, 17408, 520)
    AVb = pv(cc, 17928, 4096)
    woring = pv(cc, 22024, 2048)
    Pt0 = pv(cc, 24072, 512)
    Pt1 = pv(cc, 24584, 512)

    prev_ln = None
    slab_used = [None, None]
    yevac = [None, None]
    kused = [None, None]
    for qc in range(NCH):
        c0 = qc * CH
        prev_evac = None
        for qh in range(16):
            qb, qtile = 64 * (qh % 2), qh // 2
            nb = 2 * qc + 2
            exp_toks, av_toks = {}, {}
            ki = 0
            nkt = nb * 4
            for b in range(nb):
                kr = kring[:, (b % 2) * 512:(b % 2) * 512 + 512]
                vr = vring[:, (b % 2) * 260:(b % 2) * 260 + 260]
                cc.wait(g, kused[b % 2])
                def plan(c, b=b, qtile=qtile, qh=qh, kr=kr, vr=vr):
                    par = c % 2
                    w_ = b % 2
                    kk = b // 2
                    outs = [(kr, buf[w_, :, koff[qtile] + kk * 512:
                                    koff[qtile] + kk * 512 + 512])]
                    for j in range(4):
                        tt = kk * 4 + j
                        outs.append((vr[:, j * 65:(j + 1) * 65],
                                     buf[w_, :, voff[tt] + qh * 65:
                                         voff[tt] + qh * 65 + 65]))
                    return outs
                tok_kv = ex_dma(cc, plan)
                for j in range(4):
                    ps_s = cc.big0 if ki % 2 == 0 else cc.big1
                    pt = Pt0 if ki % 2 == 0 else Pt1
                    cc.wait(t, tok_q)
                    cc.wait(t, tok_kv)
                    if ki >= 2:
                        cc.wait(t, exp_toks[ki - 2])
                    if ki == 0:
                        if prev_evac is not None:
                            cc.wait(t, prev_evac)
                        if prev_ln is not None:
                            cc.wait(t, prev_ln)
                    i_ = t.matmul(ps_s[:, :], kr[qb:qb + 64, j * 128:(j + 1) * 128],
                                  Qf[qb:qb + 64, qtile * T + c0:qtile * T + c0 + 512],
                                  start=True, stop=True)
                    ts_ = cc.inc(i_, "p")
                    if b >= 2 * qc:
                        dd = (b - 2 * qc) * 512 + j * 128
                        cc.wait(v, ts_)
                        cc.wait(v, tok_w[3])
                        v.tensor_tensor(ps_s[:, :], ps_s[:, :],
                                        m3c[:, 1408 - dd:1408 - dd + 512],
                                        ALU.add)
                        cc.wait(a, cc.tok("v"))
                    cc.wait(a, ts_)
                    if ki >= 2:
                        cc.wait(a, av_toks[ki - 2])
                    a.activation(pt, ps_s[:, :], AF.Exp, scale=0.125,
                                 bias=cc.constt[:, 2:3])
                    exp_toks[ki] = cc.tok("a")
                    cc.wait(t, exp_toks[ki])
                    i_ = t.matmul(cc.big2[0:65, :], vr[:, j * 65:(j + 1) * 65],
                                  pt, start=(ki == 0), stop=(ki == nkt - 1))
                    av_toks[ki] = cc.inc(i_, "p")
                    ki += 1
                kused[b % 2] = av_toks[ki - 1]
            cc.wait(v, av_toks[nkt - 1])
            rec = cc.rowf[0:1, 0:512]
            v.reciprocal(rec, cc.big2[64:65, :])
            v.tensor_copy(cc.rowR[0:1, :], rec)
            tv = cc.tok("v")
            cc.wait(t, tv)
            tb = cc.inc(t.matmul(cc.big3[:, :], cc.ones1[0:1, :],
                                 cc.rowR[0:1, :], start=True, stop=True), "p")
            cc.wait(a, av_toks[nkt - 1])
            cc.wait(a, prev_evac)
            a.activation(cc.qswp[0:64, 0:512], cc.big2[0:64, :], AF.Copy)
            ta2 = cc.tok("a")
            cc.wait(v, tb)
            cc.wait(v, ta2)
            v.tensor_tensor(AVb[qb:qb + 64, qtile * 512:(qtile + 1) * 512],
                            cc.qswp[0:64, 0:512], cc.big3[0:64, :], ALU.mult)
            prev_evac = cc.tok("v")
        tok_av = cc.tok("v")
        tok_y = None
        for ot in range(DT):
            slab = woring[:, (ot % 2) * 1024:(ot % 2) * 1024 + 1024]
            cc.wait(g, slab_used[ot % 2])
            tok_l = None
            for dt_ in range(DT):
                tok_l = dma_g(cc, slab[:, dt_ * 128:(dt_ + 1) * 128],
                              W["l3_woT"][dt_ * 128:(dt_ + 1) * 128,
                                          ot * 128:(ot + 1) * 128])
            ps_o = cc.big4 if ot % 2 == 0 else cc.big5
            i_ = None
            for dvt in range(DT):
                cc.wait(t, tok_av)
                cc.wait(t, tok_l)
                if dvt == 0:
                    cc.wait(t, yevac[ot % 2])
                if prev_ln is not None:
                    cc.wait(t, prev_ln)
                i_ = t.matmul(ps_o[:, :], slab[:, dvt * 128:(dvt + 1) * 128],
                              AVb[:, dvt * 512:(dvt + 1) * 512],
                              start=(dvt == 0), stop=(dvt == DT - 1))
            tp = cc.inc(i_, "p")
            slab_used[ot % 2] = tp
            cc.wait(a, tp)
            cc.wait(a, prev_ln)
            a.activation(cc.ych[:, ot * 512:(ot + 1) * 512], ps_o[:, :], AF.Copy)
            tok_y = cc.tok("a")
            yevac[ot % 2] = tok_y
        prev_ln = emit_ln_cols(cc, lambda i: cc.ych[:, i * 512:(i + 1) * 512],
                               lambda i: xres_t(cc, i), "add", c0, src_c0=0,
                               src_is_bf16=True, wait_tokens=(tok_y,))
    return prev_ln


def emit_l1_attn(cc, W):
    nc = cc.nc
    t, v, a, g = nc.tensor, cc.V, cc.A, nc.gpsimd
    tok_xn = emit_ln_in(cc)
    GAM = [1.0 - 2.0 ** (-5 - (h // 2)) for h in range(16)]

    Ksb = pv(cc, 0, 8 * T)
    Vsb = pv(cc, 16384, 16 * 1024)
    ring = pv(cc, 32768, 2048)
    GQ = pv(cc, 34816, 4096)
    M01 = pv(cc, 38912, 1920)
    wring = pv(cc, 32768, 2048)
    tok_tab = [dma_w(cc, GQ, W["l1_gq"][:, :]),
               dma_w(cc, M01, W["l1_m01"][:, :]),
               dma_g(cc, cc.l1s[:, 0:16], W["l1_gk"][:, :]),
               dma_g(cc, cc.l1s[:, 16:24], W["l1_gnw"][:, :]),
               dma_g(cc, cc.l1s[:, 24:32], W["l1_gnb"][:, :]),
               dma_g(cc, cc.l1s[0:16, 32:40], W["l1_msk16"][:, :])]

    slab_tok = [None, None]
    tok_k = None
    for m in range(DT):
        par = m % 2
        slab = ring[:, par * 1024:par * 1024 + 1024]
        cc.wait(g, slab_tok[par])
        tok_l = dma_g(cc, slab.rearrange("p (k m) -> p k m", k=DT),
                      W["l1_wkcat"][:, m * 128:(m + 1) * 128]
                      .rearrange("(k p) m -> p k m", p=128))
        for c0 in range(0, T, 512):
            ps = cc.big0 if (c0 // 512) % 2 == 0 else cc.big1
            i_ = None
            for dt_ in range(DT):
                cc.wait(t, tok_l)
                cc.wait(t, tok_xn)
                if dt_ == 0:
                    cc.wait(t, cc.tok("a"))
                i_ = t.matmul(ps[:, :], slab[:, dt_ * 128:(dt_ + 1) * 128],
                              xn_t(cc, dt_)[:, c0:c0 + 512],
                              start=(dt_ == 0), stop=(dt_ == DT - 1))
            tp = cc.inc(i_, "p")
            slab_tok[par] = tp
            cc.wait(a, tp)
            a.activation(Ksb[:, m * T + c0:m * T + c0 + 512], ps[:, :], AF.Copy)
            tok_k = cc.tok("a")

    tok_vv = None
    vprev = None
    for q4 in range(4):
        cc.wait(g, slab_tok[0])
        cc.wait(g, slab_tok[1])
        cc.wait(g, vprev)
        tok_l = dma_g(cc, ring.rearrange("p (k m) -> p k m", k=DT),
                      W["l1_wvcat"][:, q4 * 256:(q4 + 1) * 256]
                      .rearrange("(k p) m -> p k m", p=128))
        for tt in range(16):
            ps = cc.big2 if tt % 2 == 0 else cc.big3
            i_ = None
            for dt_ in range(DT):
                cc.wait(t, tok_l)
                cc.wait(t, tok_xn)
                if dt_ == 0:
                    cc.wait(t, cc.tok("a"))
                i_ = t.matmul(ps[:, 0:256],
                              xn_t(cc, dt_)[:, tt * 128:(tt + 1) * 128],
                              ring[:, dt_ * 256:(dt_ + 1) * 256],
                              start=(dt_ == 0), stop=(dt_ == DT - 1))
            tp = cc.inc(i_, "p")
            vprev = tp
            cc.wait(a, tp)
            a.activation(Vsb[:, tt * 1024 + q4 * 256:tt * 1024 + (q4 + 1) * 256],
                         ps[:, 0:256], AF.Copy)
            tok_vv = cc.tok("a")
        slab_tok = [cc.tok("p"), cc.tok("p")]

    srcs = [Ksb[:, i * T:(i + 1) * T] for i in range(8)] + \
           [Vsb[:, i * 1024:(i + 1) * 1024] for i in range(16)]
    buf, offs, rtok = publish(cc, "l1kv", srcs, wait_tokens=(tok_k, tok_vv))
    koff = offs[:8]
    voff = offs[8:]

    avb = pv(cc, 0, 8 * T)
    Qc = pv(cc, 16384, 4096)
    kr = pv(cc, 20480, 256)
    vr = pv(cc, 20736, 128)
    qring = pv(cc, 22016, 2048)
    cc.wait(a, rtok)
    cc.wait(v, rtok)

    kused = [None, None]
    avev = None
    qslab_tok = [None, None]
    for qc in range(NCH):
        c0 = qc * CH
        tok_q = None
        for m in range(DT):
            par = m % 2
            slab = qring[:, par * 1024:par * 1024 + 1024]
            cc.wait(g, qslab_tok[par])
            tok_l = dma_g(cc, slab.rearrange("p (k m) -> p k m", k=DT),
                          W["l1_wqcat"][:, m * 128:(m + 1) * 128]
                          .rearrange("(k p) m -> p k m", p=128))
            ps = cc.big0 if par == 0 else cc.big1
            i_ = None
            for dt_ in range(DT):
                cc.wait(t, tok_l)
                cc.wait(t, tok_xn)
                if dt_ == 0:
                    cc.wait(t, cc.tok("v"))
                    cc.wait(t, avev)
                i_ = t.matmul(ps[:, :], slab[:, dt_ * 128:(dt_ + 1) * 128],
                              xn_t(cc, dt_)[:, c0:c0 + 512],
                              start=(dt_ == 0), stop=(dt_ == DT - 1))
            tp = cc.inc(i_, "p")
            qslab_tok[par] = tp
            cc.wait(v, tp)
            cc.wait(v, tok_tab[0])
            v.tensor_tensor(Qc[:, m * 512:(m + 1) * 512], ps[:, :],
                            GQ[:, m * 512:(m + 1) * 512], ALU.mult)
            tok_q = cc.tok("v")
        for h in range(16):
            qb, qtile = 64 * (h % 2), h // 2
            gam = GAM[h]
            kts = [kt for kt in range(8 * qc + 8)
                   if gam ** (1024 * qc - 128 * kt + 512) >= 1e-28]
            exp_toks, av_toks = {}, {}
            for ki, kt in enumerate(kts):
                par = ki % 2
                ps_s = cc.big0 if par == 0 else cc.big1
                bb = kt // 4
                w_ = bb % 2
                vtt = (bb // 2) * 4 + kt % 4
                kcol = koff[qtile] + (bb // 2) * 512 + (kt % 4) * 128
                cc.wait(g, kused[par])
                tok_kv = ex_dma(cc, lambda c, kcol=kcol, w_=w_, vtt=vtt, h=h, par=par: [
                    (kr[qb:qb + 64, par * 128:par * 128 + 128],
                     buf[w_, qb:qb + 64, kcol:kcol + 128]),
                    (vr[:, par * 64:par * 64 + 64],
                     buf[w_, :, voff[vtt] + h * 64:voff[vtt] + h * 64 + 64]),
                ])
                cc.wait(t, tok_q)
                cc.wait(t, tok_kv)
                if ki >= 2:
                    cc.wait(t, exp_toks[ki - 2])
                if ki == 0:
                    cc.wait(t, avev)
                i_ = t.matmul(ps_s[:, :],
                              kr[qb:qb + 64, par * 128:par * 128 + 128],
                              Qc[qb:qb + 64, qtile * 512:(qtile + 1) * 512],
                              start=True, stop=True)
                ts_ = cc.inc(i_, "p")
                tfac = gam ** (1024 * qc - 128 * kt)
                Ssl = cc.xb2[:, par * 512:par * 512 + 512]
                tmp = cc.sq2[:, par * 512:par * 512 + 512]
                cc.wait(v, ts_)
                if ki >= 2:
                    cc.wait(v, av_toks[ki - 2])
                dprime = 128 * kt - 1024 * qc
                cc.wait(v, tok_tab[2])
                if -128 < dprime < 1024:
                    cc.wait(v, tok_tab[1])
                    v.tensor_scalar(tmp, ps_s[:, :], cc.l1s[:, h:h + 1],
                                    float(tfac), ALU.mult, ALU.mult)
                    v.tensor_tensor(Ssl, tmp,
                                    M01[:, 1408 - dprime:1920 - dprime], ALU.mult)
                else:
                    v.tensor_scalar(Ssl, ps_s[:, :], cc.l1s[:, h:h + 1],
                                    float(tfac), ALU.mult, ALU.mult)
                exp_toks[ki] = cc.tok("v")
                cc.wait(t, exp_toks[ki])
                i_ = t.matmul(cc.big2[0:64, :], vr[:, par * 64:par * 64 + 64],
                              Ssl, start=(ki == 0), stop=(ki == len(kts) - 1))
                av_toks[ki] = cc.inc(i_, "p")
                kused[par] = av_toks[ki]
            cc.wait(a, av_toks[len(kts) - 1])
            a.activation(avb[:, qtile * T + c0:qtile * T + c0 + 512][qb:qb + 64, :],
                         cc.big2[0:64, :], AF.Copy)
            avev = cc.tok("a")
    return _l1_gn_out(cc, W, avb, wring, cc.tok("a"))


def _l1_gn_out(cc, W, avb, wring, tok_av):
    nc = cc.nc
    t, v, a, g = nc.tensor, cc.V, cc.A, nc.gpsimd
    P1 = cc.l1s[:, 64:72]
    P2 = cc.l1s[:, 72:80]
    for ti in range(DT):
        cc.wait(v, tok_av)
        v.tensor_reduce(P1[:, ti:ti + 1], avb[:, ti * T:(ti + 1) * T],
                        axis=AX.X, op=ALU.add)
        for c0 in range(0, T, 512):
            sq = cc.sq2[:, 0:512]
            cc.wait(a, tok_av)
            cc.wait(a, cc.tok("v"))
            a.activation(sq, avb[:, ti * T + c0:ti * T + c0 + 512], AF.Square)
            cc.wait(v, cc.tok("a"))
            if c0 == 0:
                v.tensor_reduce(P2[:, ti:ti + 1], sq, axis=AX.X, op=ALU.add)
            else:
                v.tensor_reduce(cc.l1s[:, 80:81], sq, axis=AX.X, op=ALU.add)
                v.tensor_tensor(P2[:, ti:ti + 1], P2[:, ti:ti + 1],
                                cc.l1s[:, 80:81], ALU.add)
    tok_p = cc.tok("v")
    Ppe = cc.l1s[:, 81:97]
    rdt = exchange(cc, "l1st", [(Ppe, cc.l1s[:, 64:80])], wait_tokens=(tok_p,))
    cc.wait(v, rdt)
    v.tensor_tensor(P1, P1, Ppe[:, 0:8], ALU.add)
    v.tensor_tensor(P2, P2, Ppe[:, 8:16], ALU.add)
    tok_c = cc.tok("v")
    for st_, src in ((0, P1), (16, P2)):
        for hh in range(16):
            qb = 64 * (hh % 2)
            ti = hh // 2
            cc.wait(t, tok_c)
            i_ = t.matmul(cc.row0[0:1, st_ + hh:st_ + hh + 1],
                          cc.onescf[qb:qb + 64, 0:1], src[qb:qb + 64, ti:ti + 1],
                          start=True, stop=True)
    tok_s = cc.inc(i_, "p")
    cnt = 64.0 * 4096.0
    mean = cc.rowf[0:1, 0:16]
    var = cc.rowf[0:1, 16:32]
    rstd = cc.rowf[0:1, 32:48]
    cc.wait(v, tok_s)
    v.tensor_scalar(mean, cc.row0[0:1, 0:16], 1.0 / cnt, None, ALU.mult)
    v.tensor_tensor(var, mean, mean, ALU.mult)
    v.scalar_tensor_tensor(var, cc.row0[0:1, 16:32], 1.0 / cnt, var,
                           ALU.mult, ALU.subtract)
    tv = cc.tok("v")
    cc.wait(a, tv)
    a.activation(rstd, var, AF.Sqrt, bias=cc.constt[0:1, 0:1])
    cc.wait(v, cc.tok("a"))
    v.reciprocal(rstd, rstd)
    v.tensor_copy(cc.rowf[0:1, 48:64], mean)
    v.tensor_copy(cc.rowf[0:1, 64:80], rstd)
    tvb = cc.tok("v")
    bnc = cc.nc.dram_tensor("l1_bounce", [2, 16], F32, kind="Internal").ap()
    cc.wait(g, tvb)
    t1 = dma_g(cc, bnc.rearrange("a b -> (a b)")[None, :], cc.rowf[0:1, 48:80])
    cc.wait(g, t1)
    t2 = dma_g(cc, cc.l1s[0:16, 104:106], bnc.rearrange("a b -> b a"), slow=True)
    cc.wait(v, t2)
    v.tensor_scalar(cc.e2t[0:16, 128:136], cc.l1s[0:16, 32:40],
                    cc.l1s[0:16, 105:106], None, ALU.mult)
    v.tensor_scalar(cc.e2t[0:16, 136:144], cc.l1s[0:16, 32:40],
                    cc.l1s[0:16, 104:105], None, ALU.mult)
    tv2 = cc.tok("v")
    e2 = cc.l1s[0:16, 122:129] if False else None
    tok_e2 = dma_g(cc, cc.e2t[0:16, 0:128], W["l1_e2"][:, :])
    cc.wait(t, tv2)
    cc.wait(t, tok_e2)
    t.matmul(cc.big0[:, 0:8], cc.e2t[0:16, 0:128], cc.e2t[0:16, 128:136],
             start=True, stop=True)
    i_ = t.matmul(cc.big1[:, 0:8], cc.e2t[0:16, 0:128], cc.e2t[0:16, 136:144],
                  start=True, stop=True)
    tok_bc = cc.inc(i_, "p")
    cc.wait(v, tok_bc)
    A_ = cc.l1s[:, 40:48]
    B_ = cc.l1s[:, 48:56]
    v.tensor_tensor(A_, cc.l1s[:, 16:24], cc.big0[:, 0:8], ALU.mult)
    v.tensor_tensor(B_, cc.big1[:, 0:8], A_, ALU.mult)
    v.scalar_tensor_tensor(B_, B_, -1.0, cc.l1s[:, 24:32], ALU.mult, ALU.add)
    tok_ab = cc.tok("v")

    prev_ln = None
    slab_used = [None, None]
    yevac = [None, None]
    cn = pv(cc, 16384, 4096)
    for qc in range(NCH):
        c0 = qc * CH
        for ti in range(DT):
            cc.wait(v, tok_ab)
            if prev_ln is not None:
                cc.wait(v, cc.tok("p"))
            v.tensor_scalar(cn[:, ti * 512:(ti + 1) * 512],
                            avb[:, ti * T + c0:ti * T + c0 + 512],
                            A_[:, ti:ti + 1], B_[:, ti:ti + 1],
                            ALU.mult, ALU.add)
        tok_cn = cc.tok("v")
        tok_y = None
        for ot in range(DT):
            slab = wring[:, (ot % 2) * 1024:(ot % 2) * 1024 + 1024]
            cc.wait(g, slab_used[ot % 2])
            tok_l = None
            for dt_ in range(DT):
                tok_l = dma_g(cc, slab[:, dt_ * 128:(dt_ + 1) * 128],
                              W["L1_attn_Wout"][dt_ * 128:(dt_ + 1) * 128,
                                                ot * 128:(ot + 1) * 128])
            ps_o = cc.big4 if ot % 2 == 0 else cc.big5
            i_ = None
            for dt_ in range(DT):
                cc.wait(t, tok_cn)
                cc.wait(t, tok_l)
                if dt_ == 0:
                    cc.wait(t, yevac[ot % 2])
                if prev_ln is not None:
                    cc.wait(t, prev_ln)
                i_ = t.matmul(ps_o[:, :], slab[:, dt_ * 128:(dt_ + 1) * 128],
                              cn[:, dt_ * 512:(dt_ + 1) * 512],
                              start=(dt_ == 0), stop=(dt_ == DT - 1))
            tp = cc.inc(i_, "p")
            slab_used[ot % 2] = tp
            cc.wait(a, tp)
            cc.wait(a, prev_ln)
            a.activation(cc.ych[:, ot * 512:(ot + 1) * 512], ps_o[:, :], AF.Copy)
            tok_y = cc.tok("a")
            yevac[ot % 2] = tok_y
        prev_ln = emit_ln_cols(cc, lambda i: cc.ych[:, i * 512:(i + 1) * 512],
                               lambda i: xres_t(cc, i), "add", c0, src_c0=0,
                               src_is_bf16=True, wait_tokens=(tok_y,))
    return prev_ln


def _quant_weights_to_hbm(cc, W, wname, rows, cols, qbuf, recip_dst):
    nc = cc.nc
    t, v, a, g = nc.tensor, cc.V, cc.A, nc.gpsimd
    RT = rows // 128
    cc.wait(nc.sync, cc.tok("v"))
    cc.wait(nc.sync, cc.tok("a"))
    acc = cc.l1s[:, 0:1]
    first = [True]
    toks = []
    for rt in range(RT):
        for c0 in range(0, cols, 512):
            sl = cc.rowf[:, (len(toks) % 2) * 512:(len(toks) % 2) * 512 + 512]
            if len(toks) >= 2:
                cc.wait(nc.sync, cc.tok("a"))
            tok_l = dma_w(cc, sl, W[wname][rt * 128:(rt + 1) * 128, c0:c0 + 512])
            toks.append(tok_l)
            cc.wait(a, tok_l)
            cc.wait(a, cc.tok("v"))
            a.activation(cc.qswp[:, 0:512], sl, AF.Abs,
                         accum_out=cc.l1s[:, 1:2])
            cc.wait(v, cc.tok("a"))
            if first[0]:
                v.tensor_copy(acc, cc.l1s[:, 1:2])
                first[0] = False
            else:
                v.tensor_tensor(acc, acc, cc.l1s[:, 1:2], ALU.add)
    tok_acc = cc.tok("v")
    cc.wait(t, tok_acc)
    i_ = t.matmul(cc.row0[0:1, 0:1], cc.onescf[:, 0:1], acc, start=True, stop=True)
    tp = cc.inc(i_, "p")
    cc.wait(v, tp)
    v.tensor_scalar(cc.rowf[0:1, 1024:1025], cc.row0[0:1, 0:1],
                    1.0 / float(rows * cols), None, ALU.mult)
    v.tensor_scalar(cc.rowf[0:1, 1024:1025], cc.rowf[0:1, 1024:1025], 1e-5,
                    None, ALU.max)
    v.tensor_copy(recip_dst, cc.rowf[0:1, 1024:1025])
    v.reciprocal(cc.rowf[0:1, 1025:1026], cc.rowf[0:1, 1024:1025])
    v.tensor_copy(cc.rowR[0:1, 0:1], cc.rowf[0:1, 1025:1026])
    tv = cc.tok("v")
    cc.wait(t, tv)
    i_ = t.matmul(cc.big3[:, 0:1], cc.ones1[0:1, :], cc.rowR[0:1, 0:1],
                  start=True, stop=True)
    tp = cc.inc(i_, "p")
    cc.wait(v, tp)
    v.tensor_copy(cc.l1s[:, 2:3], cc.big3[:, 0:1])
    tok_s = cc.tok("v")
    cc.wait(nc.sync, cc.tok("a"))
    qi = cc.lntmp[:, 0:512]
    tok_qb = getattr(cc, "last_qbuf", None)
    for rt in range(RT):
        for c0 in range(0, cols, 512):
            sl = cc.rowf[:, ((rt * (cols // 512) + c0 // 512) % 2) * 512:
                         ((rt * (cols // 512) + c0 // 512) % 2) * 512 + 512]
            cc.wait(nc.sync, cc.tok("v"))
            tok_l = dma_w(cc, sl, W[wname][rt * 128:(rt + 1) * 128, c0:c0 + 512])
            cc.wait(v, tok_l)
            cc.wait(v, tok_s)
            cc.wait(v, tok_qb)
            v.tensor_scalar(qi, sl, cc.l1s[:, 2:3], None, ALU.mult)
            v.tensor_copy(cc.qint2[:, 0:512], qi)
            v.tensor_scalar(cc.qint2[:, 0:512], cc.qint2[:, 0:512], 1,
                            None, ALU.min)
            v.tensor_scalar(cc.qint2[:, 0:512], cc.qint2[:, 0:512], -1,
                            None, ALU.max)
            v.tensor_copy(cc.xb2[:, 0:512], cc.qint2[:, 0:512])
            tv = cc.tok("v")
            cc.wait(g, tv)
            tok_qb = dma_g(cc, qbuf[rt * 128:(rt + 1) * 128, c0:c0 + 512],
                           cc.xb2[:, 0:512])
    cc.last_qbuf = tok_qb
    return tok_qb


def emit_l2_ffn(cc, W):
    nc = cc.nc
    t, v, a, g = nc.tensor, cc.V, cc.A, nc.gpsimd
    HT = 32
    w1q = nc.dram_tensor("l2f_w1q", [1024, 4096], BF16, kind="Internal").ap()
    w2q = nc.dram_tensor("l2f_w2q", [4096, 1024], BF16, kind="Internal").ap()
    hsp = nc.dram_tensor("l2f_hsp", [NCH, HT, 128, 512], F32, kind="Internal").ap()
    ssp = nc.dram_tensor("l2f_ssp", [HT, 128, 512], F32, kind="Internal").ap()
    _quant_weights_to_hbm(cc, W, "l2_w1T", 1024, 4096, w1q, cc.l1s[0:1, 8:9])
    tok_q2 = _quant_weights_to_hbm(cc, W, "l2_w2T", 4096, 1024, w2q,
                                   cc.l1s[0:1, 9:10])
    tok_cw = dma_g(cc, cc.l1s[:, 16:112], W["l2_convw"][:, :])
    tok_cb = dma_g(cc, cc.scl[:, 0:32], W["l2_convb"][:, :])
    cc.wait(nc.vector, cc.last_qbuf)

    xq2 = pv(cc, 0, HT * 512)
    ring = pv(cc, 33800, 8192)
    edges = pv(cc, 16384, 257)
    v.memset(edges[:, 256:257], 0.0)

    def colacc(src_fp32, first):
        if first:
            v.tensor_copy(cc.rowf[:, 512:1024], src_fp32)
            v.tensor_copy(cc.rowf[:, 1024:1536], src_fp32)
        else:
            v.tensor_tensor(cc.rowf[:, 512:1024], cc.rowf[:, 512:1024],
                            src_fp32, ALU.max)
            v.tensor_tensor(cc.rowf[:, 1024:1536], cc.rowf[:, 1024:1536],
                            src_fp32, ALU.min)

    def colreduce(spill_wait=None):
        maxa = cc.rowf[:, 512:1024]
        tmp = cc.lntmp[:, 0:512]
        cc.wait(v, cc.tok("a"))
        cc.wait(v, spill_wait)
        gwait4(cc, v)
        v.tensor_scalar(tmp, cc.rowf[:, 1024:1536], -1.0, None, ALU.mult)
        v.tensor_tensor(maxa, maxa, tmp, ALU.max)
        for sft in (16, 8, 4, 2, 1):
            mask = [min(i + sft, 31) for i in range(32)]
            v.stream_shuffle(tmp, maxa, mask)
            v.tensor_tensor(maxa, maxa, tmp, ALU.max)
        quad = cc.lntmp2[0:32, 0:512]
        v.memset(quad, 0.0)
        v.tensor_copy(cc.lntmp2[0:1, 0:512], maxa[0:1, :])
        tv_ = cc.tok("v")
        cc.wait(nc.scalar, tv_)
        cc.wait(nc.scalar, spill_wait)
        tks = [dma_s(cc, cc.lntmp2[1:2, 0:512], maxa[32:33, :]),
               dma_s(cc, cc.lntmp2[2:3, 0:512], maxa[64:65, :]),
               dma_s(cc, cc.lntmp2[3:4, 0:512], maxa[96:97, :])]
        for tk in tks:
            cc.wait(v, tk)
        for sft in (2, 1):
            mask = [min(i + sft, 31) for i in range(32)]
            v.stream_shuffle(cc.lntmp[0:32, 0:512], quad, mask)
            v.tensor_tensor(quad, quad, cc.lntmp[0:32, 0:512], ALU.max)
        v.tensor_copy(cc.rowf[0:1, 0:512], cc.lntmp2[0:1, 0:512])

    def finish_scale(recip_cell):
        v.tensor_scalar(cc.rowf[0:1, 0:512], cc.rowf[0:1, 0:512], 1e-5,
                        None, ALU.max)
        v.tensor_scalar(cc.rowf[0:1, 1024:1536], cc.rowf[0:1, 0:512],
                        recip_cell, 1.0 / 127.0, ALU.mult, ALU.mult)
        v.reciprocal(cc.rowf[0:1, 0:512], cc.rowf[0:1, 0:512])
        v.tensor_scalar(cc.rowR[0:1, 0:512], cc.rowf[0:1, 0:512], 127.0,
                        None, ALU.mult)
        tv_ = cc.tok("v")
        cc.wait(t, tv_)
        i_ = t.matmul(cc.big3[:, :], cc.ones1[0:1, :], cc.rowR[0:1, :],
                      start=True, stop=True)
        return cc.inc(i_, "p")

    def quantize_tile(src_fp32, dst_bf16, tok_bc):
        cc.wait(v, tok_bc)
        v.tensor_tensor(cc.lntmp[:, 0:512], src_fp32, cc.big3[:, :], ALU.mult)
        v.tensor_copy(cc.qint2[:, 0:512], cc.lntmp[:, 0:512])
        v.tensor_scalar(cc.qint2[:, 0:512], cc.qint2[:, 0:512], 127,
                        None, ALU.min)
        v.tensor_scalar(cc.qint2[:, 0:512], cc.qint2[:, 0:512], -128,
                        None, ALU.max)
        v.tensor_copy(dst_bf16, cc.qint2[:, 0:512])

    prev_ln = None
    slab_tok = [None, None]
    yevac = [None, None]
    for qc in range(NCH):
        c0 = qc * CH
        wt = [] if prev_ln is None else [prev_ln]
        wt.append(getattr(cc, "_l2f_hsp_tok", None))
        tok_ln = emit_ln_cols(cc, lambda i: xres_t(cc, i),
                              lambda i: xn_t(cc, i), "set", c0,
                              wait_tokens=tuple(wt))
        for ti in range(DT):
            fl = cc.lntmp2[:, 0:512]
            cc.wait(v, tok_ln)
            v.tensor_tensor(fl, xres_t(cc, ti)[:, c0:c0 + 512], cc.big0[:, :],
                            ALU.mult)
            v.tensor_tensor(fl, fl, cc.big1[:, :], ALU.add)
            colacc(fl, ti == 0)
        colreduce()
        tok_bc = finish_scale(cc.l1s[0:1, 8:9])
        for ti in range(DT):
            fl = cc.lntmp2[:, 0:512]
            cc.wait(v, cc.tok("gp"))
            v.tensor_tensor(fl, xres_t(cc, ti)[:, c0:c0 + 512], cc.big0[:, :],
                            ALU.mult)
            v.tensor_tensor(fl, fl, cc.big1[:, :], ALU.add)
            quantize_tile(fl, xn_t(cc, ti)[:, c0:c0 + 512], tok_bc)
        tok_xq = cc.tok("v")
        cc.wait(v, cc.tok("p"))
        v.tensor_copy(cc.rowR[0:1, :], cc.rowf[0:1, 1024:1536])
        tv = cc.tok("v")
        cc.wait(t, tv)
        i_ = t.matmul(cc.big2[:, :], cc.ones1[0:1, :], cc.rowR[0:1, :],
                      start=True, stop=True)
        tok_u1 = cc.inc(i_, "p")
        for m in range(HT):
            par = m % 2
            ps = cc.big4 if par == 0 else cc.big5
            slab = ring[:, par * 1024:par * 1024 + 1024]
            cc.wait(g, slab_tok[par])
            tok_l = dma_g(cc, slab.rearrange("p (k m) -> p k m", k=DT),
                          w1q[:, m * 128:(m + 1) * 128]
                          .rearrange("(k p) m -> p k m", p=128))
            i_ = None
            for dt_ in range(DT):
                cc.wait(t, tok_l)
                cc.wait(t, tok_xq)
                if dt_ == 0:
                    cc.wait(t, cc.tok("a"))
                i_ = t.matmul(ps[:, :], slab[:, dt_ * 128:(dt_ + 1) * 128],
                              xn_t(cc, dt_)[:, c0:c0 + 512],
                              start=(dt_ == 0), stop=(dt_ == DT - 1))
            tp = cc.inc(i_, "p")
            slab_tok[par] = tp
            cc.wait(a, tp)
            a.activation(cc.lntmp[:, 0:512], ps[:, :], AF.Copy)
            ta = cc.tok("a")
            cc.wait(v, ta)
            cc.wait(v, tok_u1)
            cc.wait(v, getattr(cc, "_l2f_hsp_tok", None))
            v.tensor_tensor(cc.lntmp2[:, 0:512], cc.lntmp[:, 0:512],
                            cc.big2[:, :], ALU.mult)
            v.tensor_copy(edges[:, qc * 32 + m:qc * 32 + m + 1],
                          cc.lntmp2[:, 0:1])
            v.tensor_copy(edges[:, 128 + qc * 32 + m:129 + qc * 32 + m],
                          cc.lntmp2[:, 511:512])
            th = cc.tok("v")
            cc.wait(g, th)
            cc._l2f_hsp_tok = dma_g(cc, hsp[qc, m], cc.lntmp2[:, 0:512])
    tok_sp = cc.tok("g")
    tok_eg = cc.tok("v")
    ebuf, eoffs, ertok = publish(cc, "l2fe", [edges],
                                 wait_tokens=(tok_sp, tok_eg))
    hal = pv(cc, 17000, 256)
    cc.wait(nc.vector, ertok)
    v.memset(hal[:, :], 0.0)
    cc.wait(g, cc.tok("v"))

    def halplan(c):
        pr = (c ^ 1) % 2
        if c % 2 == 0:
            return [(hal[:, 32:128], ebuf[pr, :, 128:224]),
                    (hal[:, 128:256], ebuf[pr, :, 0:128])]
        return [(hal[:, 0:128], ebuf[pr, :, 128:256]),
                (hal[:, 128:224], ebuf[pr, :, 32:128])]
    tok_hal = ex_dma(cc, halplan)
    gbar(cc, g)
    for qc in range(NCH):
        c0 = qc * CH
        Bslot = [cc.lntmp2[:, 0:512], cc.rowf[:, 0:512]]
        cc.wait(g, cc.tok("v"))
        cc.wait(g, cc.tok("a"))
        cc.wait(g, getattr(cc, "_l2f_ssp_tok", None))
        tok_B = [None, None]
        tok_B[0] = dma_g(cc, Bslot[0], hsp[qc, 0])
        for m in range(HT):
            B = Bslot[m % 2]
            acc = cc.lntmp[:, 0:512]
            if m + 1 < HT:
                cc.wait(g, cc.tok("v"))
                cc.wait(g, cc.tok("a"))
                cc.wait(g, getattr(cc, "_l2f_ssp_tok", None))
                tok_B[(m + 1) % 2] = dma_g(cc, Bslot[(m + 1) % 2],
                                           hsp[qc, m + 1])
            tok_hr = tok_B[m % 2]
            hl = hal[:, 32 * qc + m:32 * qc + m + 1]
            hr = hal[:, 128 + 32 * qc + m:129 + 32 * qc + m]
            w0 = cc.l1s[:, 16 + m * 3:17 + m * 3]
            w1_ = cc.l1s[:, 17 + m * 3:18 + m * 3]
            w2_ = cc.l1s[:, 18 + m * 3:19 + m * 3]
            cc.wait(v, tok_hr)
            cc.wait(v, tok_hal)
            cc.wait(v, tok_cw)
            v.tensor_scalar(cc.lntmp[:, 1:512],
                            B[:, 0:511], w0, None, ALU.mult)
            v.tensor_scalar(cc.lntmp[:, 0:1], hl, w0,
                            None, ALU.mult)
            v.scalar_tensor_tensor(acc, B, w1_, acc, ALU.mult, ALU.add)
            v.scalar_tensor_tensor(cc.lntmp[:, 0:511], B[:, 1:512], w2_,
                                   cc.lntmp[:, 0:511], ALU.mult, ALU.add)
            v.scalar_tensor_tensor(cc.lntmp[:, 511:512], hr,
                                   w2_, cc.lntmp[:, 511:512], ALU.mult, ALU.add)
            tv = cc.tok("v")
            cc.wait(a, tv)
            cc.wait(a, tok_cb)
            cc.wait(a, getattr(cc, "_l2f_ssp_tok", None))
            a.activation(B, acc, AF.Silu, bias=cc.scl[:, m:m + 1])
            ta = cc.tok("a")
            cc.wait(v, ta)
            colacc(B, m == 0)
            tv = cc.tok("v")
            cc.wait(g, tv)
            cc._l2f_ssp_tok = dma_g(cc, ssp[m], B)
        colreduce(spill_wait=cc._l2f_ssp_tok)
        tok_bc2 = finish_scale(cc.l1s[0:1, 9:10])
        gwait4(cc, g)
        Fslot = [cc.lntmp2[:, 0:512], cc.rowf[:, 0:512]]
        cc.wait(g, cc.tok("v"))
        tok_F = [None, None]
        tok_F[0] = dma_g(cc, Fslot[0], ssp[0])
        for m in range(HT):
            fl = Fslot[m % 2]
            if m + 1 < HT:
                cc.wait(g, cc.tok("v"))
                tok_F[(m + 1) % 2] = dma_g(cc, Fslot[(m + 1) % 2], ssp[m + 1])
            cc.wait(v, tok_F[m % 2])
            quantize_tile(fl, xq2[:, m * 512:(m + 1) * 512], tok_bc2)
        tok_xq2 = cc.tok("v")
        cc.wait(v, cc.tok("p"))
        v.tensor_copy(cc.rowR[0:1, :], cc.rowf[0:1, 1024:1536])
        tv = cc.tok("v")
        cc.wait(t, tv)
        i_ = t.matmul(cc.big2[:, :], cc.ones1[0:1, :], cc.rowR[0:1, :],
                      start=True, stop=True)
        tok_u2 = cc.inc(i_, "p")
        tok_y = None
        for ot in range(DT):
            par = ot % 2
            ps_o = cc.big4 if par == 0 else cc.big5
            slab = ring[:, par * 4096:par * 4096 + 4096]
            cc.wait(g, slab_tok[par])
            tok_l = dma_g(cc, slab.rearrange("p (k m) -> p k m", k=HT),
                          w2q[:, ot * 128:(ot + 1) * 128]
                          .rearrange("(k p) m -> p k m", p=128))
            i_ = None
            for k in range(HT):
                cc.wait(t, tok_l)
                cc.wait(t, tok_xq2)
                cc.wait(t, tok_q2)
                if k == 0:
                    cc.wait(t, yevac[par])
                i_ = t.matmul(ps_o[:, :], slab[:, k * 128:(k + 1) * 128],
                              xq2[:, k * 512:(k + 1) * 512],
                              start=(k == 0), stop=(k == HT - 1))
            tp = cc.inc(i_, "p")
            slab_tok[par] = tp
            cc.wait(a, tp)
            cc.wait(a, prev_ln)
            cc.wait(a, tok_y)
            a.activation(cc.xb2[:, 0:512], ps_o[:, :], AF.Copy)
            ta = cc.tok("a")
            cc.wait(v, ta)
            cc.wait(v, tok_u2)
            v.tensor_tensor(cc.ych[:, ot * 512:(ot + 1) * 512],
                            cc.xb2[:, 0:512], cc.big2[:, :], ALU.mult)
            tok_y = cc.tok("v")
            yevac[par] = tok_y
        prev_ln = emit_ln_cols(cc, lambda i: cc.ych[:, i * 512:(i + 1) * 512],
                               lambda i: xres_t(cc, i), "add", c0, src_c0=0,
                               src_is_bf16=True, wait_tokens=(tok_y,))
    return prev_ln


def emit_l2_attn(cc, W):
    nc = cc.nc
    t, v, a, g = nc.tensor, cc.V, cc.A, nc.gpsimd
    HT2 = 12
    ubuf = nc.dram_tensor("l2a_ubuf", [NCH, HT2, 128, 512], F32,
                          kind="Internal").ap()
    h0buf = nc.dram_tensor("l2a_h0buf", [NCH, HT2, 128, 512], F32,
                           kind="Internal").ap()
    pbuf = nc.dram_tensor("l2a_pbuf", [NCH, HT2, 128, 512], F32,
                          kind="Internal").ap()
    ybf = pv(cc, 0, HT2 * 512)
    zeros = pv(cc, 6144, 512)
    estage = pv(cc, 6656, 147)
    halo = pv(cc, 6816, 144)
    hg = pv(cc, 7168, HT2 * 512)
    ring = pv(cc, 33800, 8192)
    zero0 = cc.constt[:, 3:4]
    one0 = cc.constt[:, 4:5]

    tok_xn = emit_ln_in(cc)
    v.memset(zero0, 0.0)
    v.memset(one0, 1.0)
    v.memset(zeros[:, :], 0.0)
    v.memset(estage[:, 144:147], 0.0)
    v.memset(halo[:, 0:36], 0.0)
    tok_sc = [dma_w(cc, cc.scl[:, 0:48], W["l2a_cw"][:, :]),
              dma_w(cc, cc.scl[:, 48:60], W["l2a_cb"][:, :]),
              dma_w(cc, cc.scl[:, 60:72], W["l2a_lam"][:, :]),
              dma_w(cc, cc.scl[:, 84:96], W["l2a_bi"][:, :]),
              dma_w(cc, cc.scl[:, 96:108], W["l2a_bg"][:, :]),
              dma_w(cc, cc.scl[:, 108:109], W["l2_parf"][:, :])]
    lam = cc.scl[:, 60:72]
    spl = cc.scl[:, 72:84]
    tmpc = cc.scl[:, 112:124]
    for tk in tok_sc:
        cc.wait(a, tk)
        cc.wait(v, tk)
    cc.wait(a, cc.tok("v"))
    a.activation(tmpc, lam, AF.Abs, bias=zero0)
    cc.wait(v, cc.tok("a"))
    v.tensor_scalar(tmpc, tmpc, -1.0, None, ALU.mult)
    cc.wait(a, cc.tok("v"))
    a.activation(tmpc, tmpc, AF.Exp, bias=zero0)
    cc.wait(v, cc.tok("a"))
    v.tensor_scalar(tmpc, tmpc, 1.0, None, ALU.add)
    cc.wait(a, cc.tok("v"))
    a.activation(tmpc, tmpc, AF.Ln, bias=zero0)
    v.tensor_scalar(spl, lam, 0.0, None, ALU.max)
    cc.wait(v, cc.tok("a"))
    v.tensor_tensor(spl, spl, tmpc, ALU.add)
    v.tensor_scalar(spl, spl, -8.0, None, ALU.mult)

    rtok = {}
    spill_tok = [None, None]
    for k in range(NCH):
        ck = k * CH
        for m in range(HT2):
            par = m % 2
            ps = cc.big4 if par == 0 else cc.big5
            roff = par * 1024
            slab = ring[:, roff:roff + 1024]
            cc.wait(g, rtok.get(roff))
            tok_l = dma_g(cc, slab.rearrange("p (kt m) -> p kt m", kt=DT),
                          W["l2_wiT"][:, m * 128:(m + 1) * 128]
                          .rearrange("(kt p) m -> p kt m", p=128))
            i_ = None
            for kt in range(DT):
                cc.wait(t, tok_l)
                cc.wait(t, tok_xn)
                if kt == 0:
                    cc.wait(t, cc.tok("a"))
                i_ = t.matmul(ps[:, :], slab[:, kt * 128:(kt + 1) * 128],
                              xn_t(cc, kt)[:, ck:ck + 512],
                              start=(kt == 0), stop=(kt == DT - 1))
            tp = cc.inc(i_, "p")
            rtok[roff] = tp
            cc.wait(a, tp)
            cc.wait(a, spill_tok[par])
            cc.wait(a, cc.tok("v"))
            dst = cc.lntmp if par == 0 else cc.lntmp2
            a.activation(dst[:, 0:512], ps[:, :], AF.Copy)
            ta = cc.tok("a")
            cc.wait(v, ta)
            eb = 36 * k + 3 * m
            v.tensor_copy(estage[:, eb:eb + 3], dst[:, 509:512])
            tv = cc.tok("v")
            cc.wait(g, tv)
            spill_tok[par] = dma_g(cc, ubuf[k, m], dst[:, 0:512])
    ertok = None
    ebuf, _, ertok = publish(cc, "l2ae", [estage],
                             wait_tokens=(cc.tok("v"),))
    cc.wait(g, ertok)
    cc.wait(g, cc.tok("v"))
    tok_halo = ex_dma(cc, lambda c: [
        ((halo[:, 36:144], ebuf[(c % 2) ^ 1, :, 0:108]) if c % 2 == 0
         else (halo[:, 0:144], ebuf[(c % 2) ^ 1, :, 0:144]))])

    rspill = [None, None]
    for k in range(NCH):
        for m in range(HT2):
            B = cc.lntmp2[:, 0:512]
            acc = cc.lntmp[:, 0:512]
            cc.wait(g, cc.tok("v"))
            cc.wait(g, cc.tok("a"))
            gbar(cc, g)
            tok_u = dma_g(cc, B, ubuf[k, m])
            hb = 36 * k + 3 * m
            w0 = cc.scl[:, 4 * m:4 * m + 1]
            w1 = cc.scl[:, 4 * m + 1:4 * m + 2]
            w2 = cc.scl[:, 4 * m + 2:4 * m + 3]
            w3 = cc.scl[:, 4 * m + 3:4 * m + 4]
            cb = cc.scl[:, 48 + m:49 + m]
            cc.wait(v, tok_u)
            cc.wait(v, tok_halo)
            v.tensor_scalar(acc, B, w3, None, ALU.mult)
            v.scalar_tensor_tensor(acc[:, 1:512], B[:, 0:511], w2,
                                   acc[:, 1:512], ALU.mult, ALU.add)
            v.scalar_tensor_tensor(acc[:, 0:1], halo[:, hb + 2:hb + 3], w2,
                                   acc[:, 0:1], ALU.mult, ALU.add)
            v.scalar_tensor_tensor(acc[:, 2:512], B[:, 0:510], w1,
                                   acc[:, 2:512], ALU.mult, ALU.add)
            v.scalar_tensor_tensor(acc[:, 0:2], halo[:, hb + 1:hb + 3], w1,
                                   acc[:, 0:2], ALU.mult, ALU.add)
            v.scalar_tensor_tensor(acc[:, 3:512], B[:, 0:509], w0,
                                   acc[:, 3:512], ALU.mult, ALU.add)
            v.scalar_tensor_tensor(acc[:, 0:3], halo[:, hb:hb + 3], w0,
                                   acc[:, 0:3], ALU.mult, ALU.add)
            v.tensor_scalar(ybf[:, m * 512:(m + 1) * 512], acc, cb,
                            None, ALU.add)
        tok_y = cc.tok("v")
        for m in range(HT2):
            par = m % 2
            rgf = cc.lntmp[:, 0:512]
            alf = cc.lntmp2[:, 0:512]
            igf = cc.rowf[:, 0:512]
            btf = cc.rowf[:, 512:1024]
            xbf = cc.rowf[:, 1024:1536]
            ps = cc.big4
            roff = 2048 + par * 1536
            slab = ring[:, roff:roff + 1536]
            cc.wait(g, rtok.get(roff))
            tok_l = dma_g(cc, slab.rearrange("p (kt m) -> p kt m", kt=HT2),
                          W["l2_lwgT"][:, m * 128:(m + 1) * 128]
                          .rearrange("(kt p) m -> p kt m", p=128))
            i_ = None
            for kt in range(HT2):
                cc.wait(t, tok_l)
                cc.wait(t, tok_y)
                if kt == 0:
                    cc.wait(t, cc.tok("a"))
                i_ = t.matmul(ps[:, :], slab[:, kt * 128:(kt + 1) * 128],
                              ybf[:, kt * 512:(kt + 1) * 512],
                              start=(kt == 0), stop=(kt == HT2 - 1))
            tp = cc.inc(i_, "p")
            rtok[roff] = tp
            cc.wait(a, tp)
            cc.wait(a, rspill[0])
            cc.wait(a, cc.tok("v"))
            a.activation(rgf, ps[:, :], AF.Sigmoid,
                         bias=cc.scl[:, 96 + m:97 + m])
            cc.wait(v, cc.tok("a"))
            v.tensor_scalar(alf, rgf, cc.scl[:, 72 + m:73 + m], None, ALU.mult)
            cc.wait(a, cc.tok("v"))
            a.activation(alf, alf, AF.Exp, bias=zero0)
            ps2 = cc.big5
            roff2 = 5120 + par * 1536
            slab2 = ring[:, roff2:roff2 + 1536]
            cc.wait(g, rtok.get(roff2))
            tok_l2 = dma_g(cc, slab2.rearrange("p (kt m) -> p kt m", kt=HT2),
                           W["l2_lwiT"][:, m * 128:(m + 1) * 128]
                           .rearrange("(kt p) m -> p kt m", p=128))
            i_ = None
            for kt in range(HT2):
                cc.wait(t, tok_l2)
                if kt == 0:
                    cc.wait(t, cc.tok("a"))
                i_ = t.matmul(ps2[:, :], slab2[:, kt * 128:(kt + 1) * 128],
                              ybf[:, kt * 512:(kt + 1) * 512],
                              start=(kt == 0), stop=(kt == HT2 - 1))
            tp2 = cc.inc(i_, "p")
            rtok[roff2] = tp2
            cc.wait(a, tp2)
            cc.wait(a, cc.tok("v"))
            a.activation(igf, ps2[:, :], AF.Sigmoid,
                         bias=cc.scl[:, 84 + m:85 + m])
            cc.wait(v, cc.tok("a"))
            cc.wait(v, rspill[1])
            v.tensor_tensor(btf, alf, alf, ALU.mult)
            v.tensor_scalar(btf, btf, -1.0, 1.0 + 1e-6, ALU.mult, ALU.add)
            cc.wait(a, cc.tok("v"))
            a.activation(btf, btf, AF.Sqrt, bias=zero0)
            cc.wait(v, cc.tok("a"))
            v.tensor_tensor(xbf, btf, igf, ALU.mult)
            v.tensor_tensor(xbf, xbf, ybf[:, m * 512:(m + 1) * 512], ALU.mult)
            v.tensor_tensor_scan(rgf, alf, xbf, zero0[:, 0:1],
                                 ALU.mult, ALU.add)
            v.tensor_copy(cc.l1s[:, 48 + 12 * k + m:49 + 12 * k + m],
                          rgf[:, 511:512])
            tv = cc.tok("v")
            cc.wait(g, tv)
            rspill[0] = dma_g(cc, h0buf[k, m], rgf)
            v.tensor_tensor_scan(btf, alf, zeros[:, 0:512], one0[:, 0:1],
                                 ALU.mult, ALU.add)
            v.tensor_copy(cc.l1s[:, 12 * k + m:12 * k + m + 1],
                          btf[:, 511:512])
            tv = cc.tok("v")
            cc.wait(g, tv)
            rspill[1] = dma_g(cc, pbuf[k, m], btf)

    sbuf, soffs, srtok = publish(cc, "l2as", [cc.l1s[:, 0:96]],
                                 wait_tokens=(cc.tok("v"),))
    cc.wait(g, srtok)
    cc.wait(g, cc.tok("v"))
    gbar(cc, g)
    tok_rA = dma_g(cc, cc.rowf[:, 0:96], sbuf[0, :, 0:96])
    tok_rB = dma_g(cc, cc.rowf[:, 96:192], sbuf[1, :, 0:96])
    cc.wait(v, tok_rA)
    cc.wait(v, tok_rB)
    Hcur = cc.rowf[:, 192:204]
    Hodd = cc.rowf[:, 204:216]
    tmp = cc.rowf[:, 216:228]
    tmp2 = cc.rowf[:, 228:240]
    parf = cc.scl[:, 108:109]
    v.memset(Hcur, 0.0)
    for k in range(NCH):
        rE = cc.rowf[:, 0:96]
        A_e = rE[:, 12 * k:12 * k + 12]
        h_e = rE[:, 48 + 12 * k:48 + 12 * k + 12]
        rO = cc.rowf[:, 96:192]
        A_o = rO[:, 12 * k:12 * k + 12]
        h_o = rO[:, 48 + 12 * k:48 + 12 * k + 12]
        v.tensor_tensor(tmp, A_e, Hcur, ALU.mult)
        v.tensor_tensor(Hodd, tmp, h_e, ALU.add)
        v.tensor_tensor(tmp2, Hodd, Hcur, ALU.subtract)
        v.tensor_scalar(tmp2, tmp2, parf, None, ALU.mult)
        v.tensor_tensor(cc.l1s[:, 12 * k:12 * k + 12], Hcur, tmp2, ALU.add)
        if k < NCH - 1:
            v.tensor_tensor(tmp, A_o, Hodd, ALU.mult)
            v.tensor_tensor(Hcur, tmp, h_o, ALU.add)

    prev_ln = None
    yevac = [None, None]
    for k in range(NCH):
        ck = k * CH
        for m in range(HT2):
            par = m % 2
            hf = cc.rowf[:, 0:512]
            ggf = cc.rowf[:, 512:1024]
            cc.wait(g, cc.tok("v"))
            gbar(cc, g)
            tok_s0 = dma_g(cc, cc.lntmp[:, 0:512], h0buf[k, m])
            tok_pp = dma_g(cc, cc.lntmp2[:, 0:512], pbuf[k, m])
            ps = cc.big4 if par == 0 else cc.big5
            roff = par * 1024
            slab = ring[:, roff:roff + 1024]
            cc.wait(g, rtok.get(roff))
            tok_l = dma_g(cc, slab.rearrange("p (kt m) -> p kt m", kt=DT),
                          W["l2_wgT"][:, m * 128:(m + 1) * 128]
                          .rearrange("(kt p) m -> p kt m", p=128))
            i_ = None
            for kt in range(DT):
                cc.wait(t, tok_l)
                if kt == 0:
                    cc.wait(t, cc.tok("a"))
                i_ = t.matmul(ps[:, :], slab[:, kt * 128:(kt + 1) * 128],
                              xn_t(cc, kt)[:, ck:ck + 512],
                              start=(kt == 0), stop=(kt == DT - 1))
            tp = cc.inc(i_, "p")
            rtok[roff] = tp
            cc.wait(a, tp)
            cc.wait(a, cc.tok("v"))
            a.activation(ggf, ps[:, :], AF.Gelu, bias=zero0)
            cc.wait(v, tok_s0)
            cc.wait(v, tok_pp)
            v.scalar_tensor_tensor(hf, cc.lntmp2[:, 0:512],
                                   cc.l1s[:, 12 * k + m:12 * k + m + 1],
                                   cc.lntmp[:, 0:512], ALU.mult, ALU.add)
            cc.wait(v, cc.tok("a"))
            v.tensor_tensor(hg[:, m * 512:(m + 1) * 512], ggf, hf, ALU.mult)
        tok_hg = cc.tok("v")
        for o in range(DT):
            par = o % 2
            ps = cc.big4 if par == 0 else cc.big5
            roff = 2048 + par * 1536
            slab = ring[:, roff:roff + 1536]
            cc.wait(g, rtok.get(roff))
            tok_l = dma_g(cc, slab.rearrange("p (kt m) -> p kt m", kt=HT2),
                          W["l2_woT"][:, o * 128:(o + 1) * 128]
                          .rearrange("(kt p) m -> p kt m", p=128))
            i_ = None
            for kt in range(HT2):
                cc.wait(t, tok_l)
                cc.wait(t, tok_hg)
                if kt == 0:
                    cc.wait(t, yevac[par])
                    cc.wait(t, cc.tok("a"))
                i_ = t.matmul(ps[:, :], slab[:, kt * 128:(kt + 1) * 128],
                              hg[:, kt * 512:(kt + 1) * 512],
                              start=(kt == 0), stop=(kt == HT2 - 1))
            tp = cc.inc(i_, "p")
            rtok[roff] = tp
            cc.wait(a, tp)
            cc.wait(a, prev_ln)
            a.activation(cc.ych[:, o * 512:(o + 1) * 512], ps[:, :], AF.Copy)
            yevac[par] = cc.tok("a")
        prev_ln = emit_ln_cols(cc, lambda i: cc.ych[:, i * 512:(i + 1) * 512],
                               lambda i: xres_t(cc, i), "add", ck, src_c0=0,
                               src_is_bf16=True, wait_tokens=(cc.tok("a"),))
    return prev_ln


BLOCK_EMITTERS = {"l0_attn": emit_l0_attn, "l1_ffn": emit_l1_ffn,
                  "l3_ffn": emit_l3_ffn, "l0_ffn": emit_l0_ffn,
                  "l3_attn": emit_l3_attn, "l1_attn": emit_l1_attn,
                  "l2_ffn": emit_l2_ffn, "l2_attn": emit_l2_attn}


def build_program(block_names, weight_specs, shared_specs):
    nc = bass.Bass("TRN2", target_bir_lowering=False)
    cc = CC(nc)
    xin = nc.dram_tensor("xin", [D, T], F32, kind="ExternalInput").ap()
    xout = nc.dram_tensor("xout", [D, T], F32, kind="ExternalOutput").ap()
    cfg = nc.dram_tensor("cfg", [1, 8], I32, kind="ExternalInput").ap()
    W = {}
    for name, (shape, dtname) in weight_specs.items():
        W[name] = nc.dram_tensor(
            name, list(shape),
            {"float32": F32, "int32": I32, "bfloat16": BF16}[dtname],
            kind="ExternalInput").ap()
    for name, shape, dt_ in shared_specs:
        setattr(cc, name, nc.dram_tensor(name, list(shape), dt_, kind="Internal",
                                         addr_space="Shared").ap())
    cc.sh_flg = nc.dram_tensor("sh_flg", [8, 64], I32, kind="Internal",
                               addr_space="Shared").ap()

    st = _ExitStack()

    def sb(name, cols, dtype, parts=128):
        cm = nc.sbuf_tensor(name, [parts, cols], dtype)
        h = cm.__enter__()
        st.push(cm)
        return h[:, :]

    try:
        cc.xres = sb("xres", 8 * T, F32)
        cc.xn = sb("xn", 8 * T, BF16)
        cc.pool = sb("pool", POOL_COLS, BF16)
        cc.ych = sb("ych", 4096, BF16)
        cc.xb2 = sb("xb2", 1024, BF16)
        cc.sq2 = sb("sq2", 1024, BF16)
        cc.lntmp = sb("lntmp", 512, F32)
        cc.lntmp2 = sb("lntmp2", 512, F32)
        cc.qswp = sb("qswp", 512, BF16)
        cc.rowf = sb("rowf", 1536, F32)
        cc.rowR = sb("rowR", 512, BF16)
        cc.rowM = sb("rowM", 512, BF16)
        cc.ones1 = sb("ones1", 128, BF16, parts=1)
        cc.onesc = sb("onesc", 1, BF16)
        cc.onescf = sb("onescf", 1, F32)
        cc.constt = sb("constt", 16, F32)
        cc.scl = sb("scl", 136, F32)
        cc.l1s = sb("l1s", 128, F32)
        cc.qint2 = sb("qint2", 512, I32)
        cc.e2t = sb("e2t", 160, BF16)
        cc.cfgt = sb("cfgt", 8, I32, parts=1)
        cc.flagtile = sb("flagtile", 32, I32, parts=1)
        cc.noncetile = sb("noncetile", 1, I32, parts=1)

        with (
            nc.psum_tensor("big0", [128, 512], F32) as b0,
            nc.psum_tensor("big1", [128, 512], F32) as b1,
            nc.psum_tensor("big2", [128, 512], F32) as b2,
            nc.psum_tensor("big3", [128, 512], F32) as b3,
            nc.psum_tensor("big4", [128, 512], F32) as b4,
            nc.psum_tensor("big5", [128, 512], F32) as b5,
            nc.psum_tensor("row0", [1, 512], F32) as r0,
            nc.psum_tensor("row1", [1, 512], F32) as r1,
            nc.Block(),
        ):
            cc.big0, cc.big1, cc.big2 = b0[:, :], b1[:, :], b2[:, :]
            cc.big3, cc.big4, cc.big5 = b3[:, :], b4[:, :], b5[:, :]
            cc.row0, cc.row1 = r0[:, :], r1[:, :]

            cc.V = ChainedEngine(cc, nc.vector, "v", _DVE_OPS)
            cc.A = ChainedEngine(cc, nc.scalar, "a", _ACT_OPS)
            g, v = nc.gpsimd, cc.V
            t_cfg = cc.inc(nc.gpsimd.dma_start(out=cc.cfgt[0:1, :],
                                               in_=cfg[0:1, :]), "cfg", 16)
            t_x = dma_w(cc, cc.xres.rearrange("p (n m) -> p n m", n=DT),
                        xin.rearrange("(n p) m -> p n m", p=128))
            cc.wait(g, t_cfg)
            me_cm = g.register("me")
            cc.me_reg = me_cm.__enter__()
            nonce_cm = g.register("nonce")
            cc.nonce_reg = nonce_cm.__enter__()
            g.load(cc.me_reg, cc.cfgt[0:1, 0:1])
            g.load(cc.nonce_reg, cc.cfgt[0:1, 1:2])
            g.tensor_copy(cc.noncetile[0:1, 0:1], cc.cfgt[0:1, 1:2])
            g.memset(cc.flagtile[0:1, :], 0)
            v.memset(cc.ones1[0:1, :], 1.0)
            v.memset(cc.constt[:, 0:1], float(LN_EPS))
            v.memset(cc.constt[:, 1:2], -L0_SHIFT)
            v.memset(cc.constt[:, 2:3], -L3_SHIFT)
            v.memset(cc.onesc[:, 0:1], 1.0)
            v.memset(cc.onescf[:, 0:1], 1.0)
            tv = cc.tok("v")
            cc.wait(nc.tensor, tv)
            for eng in (nc.tensor, v, nc.scalar):
                cc.wait(eng, t_x)

            for bi, bn in enumerate(block_names):
                if bi > 0:
                    block_fence(cc)
                BLOCK_EMITTERS[bn](cc, W)

            sc = nc.scalar
            for nm in ("v", "a", "p"):
                cc.wait(sc, cc.tok(nm))
            t_out = dma_s(cc, xout.rearrange("(n p) m -> p n m", p=128),
                          cc.xres.rearrange("p (n m) -> p n m", n=DT))
            cc.wait(sc, t_out)
            me_cm.__exit__(None, None, None)
            nonce_cm.__exit__(None, None, None)
    finally:
        st.close()
    return nc


def _rope_tables(p):
    half = 32
    inv_freq = 1.0 / (10000.0 ** (np.arange(half, dtype=np.float64) * 2.0 / 64.0))
    pos = own_positions(p).astype(np.float64)
    fr = pos[None, :] * inv_freq[:, None]
    cos, sin = np.cos(fr), np.sin(fr)
    Ctab = np.zeros((128, T), np.float32)
    Stab = np.zeros((128, T), np.float32)
    for blk in range(2):
        r = blk * 64
        Ctab[r:r + 32] = cos
        Ctab[r + 32:r + 64] = cos
        Stab[r:r + 32] = -sin
        Stab[r + 32:r + 64] = sin
    return Ctab.astype(BF), Stab.astype(BF)


def _p128swap():
    P = np.zeros((128, 128), np.float32)
    for i in range(128):
        blk, off = divmod(i, 32)
        j = (blk ^ 1) * 32 + off
        P[j, i] = 1.0
    return P.astype(BF)


L0A_WEIGHTS = {
    "L0_attn_w_q": ((1024, 1024), "float32"),
    "L0_attn_w_k": ((1024, 256), "float32"),
    "L0_attn_w_v": ((1024, 256), "float32"),
    "L0_attn_w_o": ((1024, 1024), "float32"),
    "rope_c": ((128, T), "bfloat16"),
    "rope_s": ((128, T), "bfloat16"),
    "p128": ((128, 128), "bfloat16"),
}

L0A_SHARED = []

_NC_CACHE = {}


def _common_inputs(nonce):
    outs = []
    p128 = _p128swap()
    for c in range(8):
        p = c % 2
        ct, stt = _rope_tables(p)
        outs.append({
            "cfg": np.array([[c, nonce, c // 2, p, 0, 0, 0, 0]], np.int32),
            "rope_c": ct, "rope_s": stt, "p128": p128,
        })
    return outs


def _l0_head_col_perm():
    idx = []
    for s_ in range(16):
        h = L0_HEAD_PERM[s_]
        idx.extend(range(h * 64, h * 64 + 64))
    return np.array(idx)


WEIGHT_TRANSFORMS = {
    "L0_attn_w_q": lambda w: np.ascontiguousarray(w[:, _l0_head_col_perm()]),
    "L0_attn_w_o": lambda w: np.ascontiguousarray(w[_l0_head_col_perm(), :]),
}

DERIVED_INPUTS = {
    "l1_wgT": lambda ins: np.ascontiguousarray(ins["L1_ffn_w_gate"].T),
    "l1_wuT": lambda ins: np.ascontiguousarray(ins["L1_ffn_w_up"].T),
    "l1_wdT": lambda ins: np.ascontiguousarray(ins["L1_ffn_w_down"].T),
}

L1F_WEIGHTS = {
    "l1_wgT": ((1024, 3072), "int32"),
    "l1_wuT": ((1024, 3072), "int32"),
    "l1_wdT": ((3072, 1024), "int32"),
}


def _tile_t1(G1, c):
    t = np.asarray(G1)[0][:, c, :]
    return np.ascontiguousarray(
        np.broadcast_to(t.T[:, None, :, None], (32, 16, 32, 16))
        .reshape(512, 512)).astype(np.float32)


def _tile_t2(G2, c):
    u = np.asarray(G2)[c, :, 0, :]
    return np.ascontiguousarray(
        np.broadcast_to(u.T[None, :, None, :], (32, 16, 32, 16))
        .reshape(512, 512)).astype(np.float32)


L0F_WEIGHTS = {
    "l0_w1T": ((1024, 4096), "float32"),
    "l0_w2T": ((1024, 4096), "float32"),
    "l0_h1T": ((4096, 4096), "float32"),
    "l0_h2T": ((4096, 4096), "float32"),
    "l0_w3T": ((4096, 1024), "float32"),
    "l0_scales": ((128, 136), "float32"),
}
DERIVED_INPUTS["l0_w1T"] = lambda ins: np.ascontiguousarray(np.asarray(ins["L0_ffn_w1_w"]).T)
DERIVED_INPUTS["l0_w2T"] = lambda ins: np.ascontiguousarray(np.asarray(ins["L0_ffn_w2_w"]).T)
DERIVED_INPUTS["l0_h1T"] = lambda ins: np.ascontiguousarray(np.asarray(ins["L0_ffn_h1_w"]).T)
DERIVED_INPUTS["l0_h2T"] = lambda ins: np.ascontiguousarray(np.asarray(ins["L0_ffn_h2_w"]).T)
DERIVED_INPUTS["l0_w3T"] = lambda ins: np.ascontiguousarray(np.asarray(ins["L0_ffn_w3_w"]).T)


def _l0_scales(ins):
    out = np.zeros((128, 136), np.float32)
    out[:, 0:32] = np.asarray(ins["L0_ffn_w1_s"]).reshape(32, 128).T
    out[:, 32:64] = np.asarray(ins["L0_ffn_w2_s"]).reshape(32, 128).T
    out[:, 64:96] = np.asarray(ins["L0_ffn_h1_s"]).reshape(32, 128).T
    out[:, 96:128] = np.asarray(ins["L0_ffn_h2_s"]).reshape(32, 128).T
    out[:, 128:136] = np.asarray(ins["L0_ffn_w3_s"]).reshape(8, 128).T
    return out


DERIVED_INPUTS["l0_scales"] = _l0_scales

L3A_WEIGHTS = {
    "l3_wq2": ((128, 64), "float32"),
    "l3_wk2": ((128, 64), "float32"),
    "l3_wv2": ((128, 64), "float32"),
    "l3_m3": ((128, 1920), "bfloat16"),
    "l3_woT": ((1024, 1024), "float32"),
}
DERIVED_INPUTS["l3_wq2"] = lambda ins: np.ascontiguousarray(
    np.concatenate([np.asarray(ins["L3_attn_Wq"]).T] * 2, axis=0).astype(np.float32))
DERIVED_INPUTS["l3_wk2"] = lambda ins: np.ascontiguousarray(
    np.concatenate([np.asarray(ins["L3_attn_Wk"]).T] * 2, axis=0).astype(np.float32))
DERIVED_INPUTS["l3_wv2"] = lambda ins: np.ascontiguousarray(
    np.concatenate([np.asarray(ins["L3_attn_Wv"]).T] * 2, axis=0).astype(np.float32))
DERIVED_INPUTS["l3_woT"] = lambda ins: np.ascontiguousarray(
    np.asarray(ins["L3_attn_Wo"]).T.astype(np.float32))


def _l3_m3(p):
    kk = np.arange(128)[:, None]
    u = np.arange(1920)[None, :]
    return np.where(u >= kk + 1408 - 512 * p, 0.0, -240.0).astype(BF)


L1A_WEIGHTS = {
    "l1_wqcat": ((1024, 1024), "float32"),
    "l1_wkcat": ((1024, 1024), "float32"),
    "l1_wvcat": ((1024, 1024), "float32"),
    "L1_attn_Wout": ((1024, 1024), "float32"),
    "l1_gq": ((128, 4096), "bfloat16"),
    "l1_m01": ((128, 1920), "bfloat16"),
    "l1_gk": ((128, 16), "float32"),
    "l1_gnw": ((128, 8), "float32"),
    "l1_gnb": ((128, 8), "float32"),
    "l1_msk16": ((16, 8), "float32"),
    "l1_e2": ((16, 128), "bfloat16"),
}


def _l1_cat(w):
    w = np.asarray(w)
    return np.ascontiguousarray(w.transpose(1, 0, 2).reshape(1024, 1024)
                                .astype(np.float32))


DERIVED_INPUTS["l1_wqcat"] = lambda ins: _l1_cat(ins["L1_attn_Wq"])
DERIVED_INPUTS["l1_wkcat"] = lambda ins: _l1_cat(ins["L1_attn_Wk"])
DERIVED_INPUTS["l1_wvcat"] = lambda ins: _l1_cat(ins["L1_attn_Wv"])
DERIVED_INPUTS["l1_gnw"] = lambda ins: np.ascontiguousarray(
    np.asarray(ins["L1_attn_gn_weight"]).astype(np.float32).reshape(8, 128).T)
DERIVED_INPUTS["l1_gnb"] = lambda ins: np.ascontiguousarray(
    np.asarray(ins["L1_attn_gn_bias"]).astype(np.float32).reshape(8, 128).T)

L1_GAMMAS = [1.0 - 2.0 ** (-5 - (h // 2)) for h in range(16)]


def _l1_gq(p):
    out = np.zeros((128, 4096), np.float64)
    qq = np.arange(512, dtype=np.float64)
    for t_ in range(8):
        for par in range(2):
            gam = L1_GAMMAS[2 * t_ + par]
            out[par * 64:(par + 1) * 64, t_ * 512:(t_ + 1) * 512] = \
                gam ** (qq + 512.0 * p) * 0.125
    return out.astype(BF)


def _l1_gk():
    out = np.zeros((128, 16), np.float64)
    kk = np.arange(128, dtype=np.float64)
    for h in range(16):
        out[:, h] = L1_GAMMAS[h] ** (-kk)
    return out.astype(np.float32)


def _l1_m01(p):
    kk = np.arange(128)[:, None]
    u = np.arange(1920)[None, :]
    return np.where(u >= kk + 1408 - 512 * p, 1.0, 0.0).astype(BF)


def _l1_msk16():
    out = np.zeros((16, 8), np.float32)
    for h in range(16):
        out[h, h // 2] = 1.0
    return out


def _l1_e2():
    out = np.zeros((16, 128), np.float32)
    for h in range(16):
        out[h, 64 * (h % 2):64 * (h % 2) + 64] = 1.0
    return out.astype(BF)


DERIVED_INPUTS["l1_gk"] = lambda ins: _l1_gk()
DERIVED_INPUTS["l1_msk16"] = lambda ins: _l1_msk16()
DERIVED_INPUTS["l1_e2"] = lambda ins: _l1_e2()

L2F_WEIGHTS = {
    "l2_w1T": ((1024, 4096), "float32"),
    "l2_w2T": ((4096, 1024), "float32"),
    "l2_convw": ((128, 96), "float32"),
    "l2_convb": ((128, 32), "float32"),
}
DERIVED_INPUTS["l2_w1T"] = lambda ins: np.ascontiguousarray(
    np.asarray(ins["L2_ffn_w1"]).T.astype(np.float32))
DERIVED_INPUTS["l2_w2T"] = lambda ins: np.ascontiguousarray(
    np.asarray(ins["L2_ffn_w2"]).T.astype(np.float32))


def _l2_convw(ins):
    w = np.asarray(ins["L2_ffn_conv_w"]).astype(np.float32)
    out = np.zeros((128, 96), np.float32)
    for m in range(32):
        for j in range(3):
            out[:, m * 3 + j] = w[m * 128:(m + 1) * 128, 0, j]
    return out


DERIVED_INPUTS["l2_convw"] = _l2_convw
DERIVED_INPUTS["l2_convb"] = lambda ins: np.ascontiguousarray(
    np.asarray(ins["L2_ffn_conv_b"]).astype(np.float32).reshape(32, 128).T)

L2A_WEIGHTS = {
    "l2_wgT": ((1024, 1536), "float32"),
    "l2_wiT": ((1024, 1536), "float32"),
    "l2_lwgT": ((1536, 1536), "float32"),
    "l2_lwiT": ((1536, 1536), "float32"),
    "l2_woT": ((1536, 1024), "float32"),
    "l2a_cw": ((128, 48), "float32"),
    "l2a_cb": ((128, 12), "float32"),
    "l2a_lam": ((128, 12), "float32"),
    "l2a_bi": ((128, 12), "float32"),
    "l2a_bg": ((128, 12), "float32"),
    "l2_parf": ((128, 1), "float32"),
}
DERIVED_INPUTS["l2_wgT"] = lambda ins: np.ascontiguousarray(
    np.asarray(ins["L2_attn_Wg"]).T.astype(np.float32))
DERIVED_INPUTS["l2_wiT"] = lambda ins: np.ascontiguousarray(
    np.asarray(ins["L2_attn_Wi"]).T.astype(np.float32))
DERIVED_INPUTS["l2_lwgT"] = lambda ins: np.ascontiguousarray(
    np.asarray(ins["L2_attn_lru_Wg"]).T.astype(np.float32))
DERIVED_INPUTS["l2_lwiT"] = lambda ins: np.ascontiguousarray(
    np.asarray(ins["L2_attn_lru_Wi"]).T.astype(np.float32))
DERIVED_INPUTS["l2_woT"] = lambda ins: np.ascontiguousarray(
    np.asarray(ins["L2_attn_Wo"]).T.astype(np.float32))


def _l2a_cw(ins):
    w = np.asarray(ins["L2_attn_conv_w"]).astype(np.float32)
    out = np.zeros((128, 48), np.float32)
    for m in range(12):
        for j in range(4):
            out[:, 4 * m + j] = w[m * 128:(m + 1) * 128, 0, j]
    return out


DERIVED_INPUTS["l2a_cw"] = _l2a_cw
DERIVED_INPUTS["l2a_cb"] = lambda ins: np.ascontiguousarray(
    np.asarray(ins["L2_attn_conv_b"]).astype(np.float32).reshape(12, 128).T)
DERIVED_INPUTS["l2a_lam"] = lambda ins: np.ascontiguousarray(
    np.asarray(ins["L2_attn_lam"]).astype(np.float32).reshape(12, 128).T)
DERIVED_INPUTS["l2a_bi"] = lambda ins: np.ascontiguousarray(
    np.asarray(ins["L2_attn_lru_bi"]).astype(np.float32).reshape(12, 128).T)
DERIVED_INPUTS["l2a_bg"] = lambda ins: np.ascontiguousarray(
    np.asarray(ins["L2_attn_lru_bg"]).astype(np.float32).reshape(12, 128).T)


def _l2_parf(c):
    return np.full((128, 1), float(c % 2), np.float32)


L3F_WEIGHTS = {"l3_bias": ((128, 8), "float32")}
for _c in range(4):
    for _nm in ("t1r", "t2r", "t1i", "t2i"):
        L3F_WEIGHTS[f"l3_{_nm}_{_c}"] = ((512, 512), "float32")
for _c in range(4):
    DERIVED_INPUTS[f"l3_t1r_{_c}"] = (lambda ins, c=_c: _tile_t1(ins["L3_ffn_G1_R"], c))
    DERIVED_INPUTS[f"l3_t2r_{_c}"] = (lambda ins, c=_c: _tile_t2(ins["L3_ffn_G2_R"], c))
    DERIVED_INPUTS[f"l3_t1i_{_c}"] = (lambda ins, c=_c: _tile_t1(ins["L3_ffn_G1_I"], c))
    DERIVED_INPUTS[f"l3_t2i_{_c}"] = (lambda ins, c=_c: _tile_t2(ins["L3_ffn_G2_I"], c))
DERIVED_INPUTS["l3_bias"] = lambda ins: np.ascontiguousarray(
    np.concatenate([np.asarray(ins["L3_ffn_bias_real"]),
                    np.asarray(ins["L3_ffn_bias_imag"])]).astype(np.float32)
    .reshape(8, 128).T)


def run_blocks(key, block_names, wspecs, sspecs, x_shards, weights,
               extra_inputs=None):
    _NONCE[0] += 1
    nonce = _NONCE[0]
    base = _common_inputs(nonce)
    in_maps = []
    for c in range(8):
        m = {k: v for k, v in base[c].items() if k in wspecs or k == "cfg"}
        m["xin"] = x_shards[c]
        for k in wspecs:
            if k in ("rope_c", "rope_s", "p128"):
                m[k] = base[c][k]
            elif extra_inputs and k in extra_inputs:
                m[k] = extra_inputs[k] if not isinstance(extra_inputs[k], list) \
                    else extra_inputs[k][c]
            elif k in DERIVED_INPUTS:
                m[k] = DERIVED_INPUTS[k](weights)
            else:
                wv_ = np.ascontiguousarray(weights[k])
                if k in WEIGHT_TRANSFORMS:
                    wv_ = WEIGHT_TRANSFORMS[k](wv_)
                m[k] = wv_
        in_maps.append(m)
    if key not in _NC_CACHE:
        _NC_CACHE[key] = build_program(block_names, wspecs, sspecs)
    res = run_bass_kernel_spmd(_NC_CACHE[key], in_maps, core_ids=list(range(8)))
    return [res.results[c]["xout"] for c in range(8)]


def _shard_x(x):
    shards = []
    for c in range(8):
        b, p = c // 2, c % 2
        xt = np.ascontiguousarray(x[b].T)
        shards.append(np.ascontiguousarray(xt[:, own_positions(p)]))
    return shards


def _unshard_x(shards):
    out = np.zeros((B, S, D), np.float32)
    for c in range(8):
        b, p = c // 2, c % 2
        out[b, own_positions(p), :] = shards[c].T
    return out


ALL_BLOCKS = ["l0_attn", "l0_ffn", "l1_attn", "l1_ffn",
              "l2_attn", "l2_ffn", "l3_attn", "l3_ffn"]
ALL_WEIGHTS = {}
for _d in (L0A_WEIGHTS, L0F_WEIGHTS, L1A_WEIGHTS, L1F_WEIGHTS,
           L2A_WEIGHTS, L2F_WEIGHTS, L3A_WEIGHTS, L3F_WEIGHTS):
    for _k, _v in _d.items():
        assert _k not in ALL_WEIGHTS or ALL_WEIGHTS[_k] == _v, _k
        ALL_WEIGHTS[_k] = _v


def _all_extra():
    return {"l1_gq": [_l1_gq(c % 2) for c in range(8)],
            "l1_m01": [_l1_m01(c % 2) for c in range(8)],
            "l2_parf": [_l2_parf(c) for c in range(8)],
            "l3_m3": [_l3_m3(c % 2) for c in range(8)]}


def kernel(**inputs):
    x = np.asarray(inputs["L0_attn_x"], np.float32)
    shards = _shard_x(x)
    shards = run_blocks("fused", ALL_BLOCKS, ALL_WEIGHTS, [],
                        shards, inputs, extra_inputs=_all_extra())
    return _unshard_x(shards)
